# version 07
# baseline (speedup 1.0000x reference)
import numpy as np
import concourse.bass as bass
import concourse.mybir as mybir
from concourse.bass_utils import run_bass_kernel_spmd

F32 = mybir.dt.float32
BF16 = mybir.dt.bfloat16
AF = mybir.ActivationFunctionType
ALU = mybir.AluOpType
AX = mybir.AxisListType.X

D = 1024
H = 16
DH = 64
FC = 8
LNEPS = 1e-5
L2EPS = 1e-12
EPS1 = 1e-10


class Ev:

    def __init__(self, nc, name, engine, dma=False):
        self.h = nc.alloc_semaphore(name)
        self.n = 0
        self.engine = engine
        self.dma = dma


class Ph:
    def __init__(self, nc, name=""):
        self.nc = nc
        self.items = []
        self.name = name

    ENG_EV = {}

    def rec(self, engine, fn, waits=(), sig=None, n=1, ss=False):
        wl = []
        for ev in waits:
            if ev is None:
                continue
            if isinstance(ev, tuple):
                if ev[1] > 0:
                    wl.append(ev if len(ev) == 3 else (ev[0], ev[1], False))
            elif ev.n > 0:
                wl.append((ev, ev.n, False))
        if ss and sig is None:
            sig = Ph.ENG_EV[engine]
        if sig is not None:
            assert sig.engine == engine, f"event {sig.h.name} owned by {sig.engine}, signaled from {engine}"
            sig.n += n
        self.items.append((engine, fn, wl, sig, n, ss))

    def emit(self):
        engines = []
        for it in self.items:
            if it[0] not in engines:
                engines.append(it[0])
        sigtgt = {}
        run = {}
        for it in self.items:
            sig = it[3]
            if sig is not None:
                run[sig] = run.get(sig, sig.n - sum(j[4] for j in self.items if j[3] is sig))
        for idx, it in enumerate(self.items):
            sig = it[3]
            if sig is not None:
                run[sig] += it[4]
                sigtgt[idx] = run[sig]
        with self.nc.Block() as blk:
            for ename in engines:
                section = getattr(blk, ename)

                def body(E, ename=ename):
                    waited = {}
                    for idx, (eng, fn, wl, sig, n, ss) in enumerate(self.items):
                        if eng != ename:
                            continue
                        for ev, v, force in wl:
                            if ev.engine == ename and not ev.dma and not force:
                                continue
                            if waited.get(ev, -1) >= v:
                                continue
                            E.wait_ge(ev.h, v)
                            waited[ev] = v
                        ins = fn(E)
                        if sig is not None:
                            ins.then_inc(sig.h, n)
                            if ss:
                                E.wait_ge(sig.h, sigtgt[idx])
                                waited[sig] = max(waited.get(sig, -1), sigtgt[idx])
                section(body)


class Build:
    def __init__(self, S=2048, NSEQ=2, NE=8, FF=4096, dbg=False):
        ik = "ExternalOutput" if dbg else "Internal"
        self.S = S
        self.NSEQ = NSEQ
        self.NE = NE
        self.FF = FF
        self.FFC = FF // 128
        self.TT = S // 128
        self.QC = S // 512
        self.TOK = S * NSEQ
        nc = bass.Bass("TRN2", target_bir_lowering=False)
        self.nc = nc
        S_ = S

        dt = nc.dram_tensor
        self.x = dt("x", [NSEQ, S, D], F32, kind="ExternalInput").ap()
        self.wk1 = dt("wk1", [H, D, DH], F32, kind="ExternalInput").ap()
        self.wq1 = dt("wq1", [H, D, DH], F32, kind="ExternalInput").ap()
        self.wv1 = dt("wv1", [H, D, DH], F32, kind="ExternalInput").ap()
        self.wo1 = dt("wo1", [H, DH, D], F32, kind="ExternalInput").ap()
        self.bk1 = dt("bk1", [H, DH], F32, kind="ExternalInput").ap()
        self.bq1 = dt("bq1", [H, DH], F32, kind="ExternalInput").ap()
        self.bv1 = dt("bv1", [H, DH], F32, kind="ExternalInput").ap()
        self.bo1 = dt("bo1", [D], F32, kind="ExternalInput").ap()
        self.lng = dt("lng", [D], F32, kind="ExternalInput").ap()
        self.lnb = dt("lnb", [D], F32, kind="ExternalInput").ap()
        self.wq2 = dt("wq2", [D, D], F32, kind="ExternalInput").ap()
        self.wk2 = dt("wk2", [D, D], F32, kind="ExternalInput").ap()
        self.wv2 = dt("wv2", [D, D], F32, kind="ExternalInput").ap()
        self.wo2 = dt("wo2", [D, D], F32, kind="ExternalInput").ap()
        self.qs2 = dt("qs2", [H, 1, DH], F32, kind="ExternalInput").ap()
        self.ks2 = dt("ks2", [H, 1, DH], F32, kind="ExternalInput").ap()
        self.gw = dt("gw", [D, 8], F32, kind="ExternalInput").ap()
        self.gb = dt("gb", [8], F32, kind="ExternalInput").ap()
        self.w1 = dt("w1", [NE, D, FF], F32, kind="ExternalInput").ap()
        self.b1 = dt("b1", [NE, FF], F32, kind="ExternalInput").ap()
        self.w2 = dt("w2", [NE, FF, D], F32, kind="ExternalInput").ap()
        self.b2 = dt("b2", [NE, D], F32, kind="ExternalInput").ap()
        self.y = dt("y", [NSEQ, S, D], F32, kind="ExternalOutput").ap()
        self.h2d = dt("h2d", [NSEQ, D, S], BF16, kind=ik).ap()
        self.hidd = dt("hidd", [FF, self.TOK], BF16, kind=ik).ap()
        self.moed = dt("moed", [self.TOK, D], F32, kind=ik).ap()
        self.wfd = dt("wfd", [FC, 128, S], BF16, kind=ik).ap()
        self.gtd = dt("gtd", [8, self.TOK], BF16, kind=ik).ap()
        self.dbg = dbg
        if dbg:
            self.dbgd = dt("dbgd", [8, 128, S], BF16, kind="ExternalOutput").ap()
            self.dbgf = dt("dbgf", [4, 128, 512], F32, kind="ExternalOutput").ap()

        def at(name, shape, dtype):
            return nc.alloc_sbuf_tensor(name, shape, dtype)

        wa_cols = max(FC * self.FF, FC * S_ + self.TT * H * 65)
        self.WA = at("WA", [128, wa_cols], BF16)
        self.xnh = self.WA[:, 0:FC * S_]
        self.Vt = self.WA[:, FC * S_: FC * S_ + self.TT * H * 65]
        wb_cols = max(self.FFC * D, 3 * FC * D + 4 * S_)
        self.WB = at("WB", [128, wb_cols], BF16)
        self.w0 = self.WB[:, 0:FC * D]
        self.w1s = self.WB[:, FC * D:2 * FC * D]
        self.w2s = self.WB[:, 2 * FC * D:3 * FC * D]
        kq0 = 3 * FC * D
        self.kb = [self.WB[0:65, kq0 + i * S_: kq0 + (i + 1) * S_] for i in range(2)]
        self.qb = [self.WB[0:65, kq0 + (2 + i) * S_: kq0 + (3 + i) * S_] for i in range(2)]
        self.Creg = at("Creg", [128, 8192], BF16)
        self.h2cur = [self.Creg[:, i * 4096:(i + 1) * 4096] for i in range(2)]
        self.hid2 = self.Creg
        self.wfb = [self.Creg[:, i * S_:(i + 1) * S_] for i in range(2)]
        self.opj = [self.Creg[:, 2 * S_ + i * 1024: 2 * S_ + (i + 1) * 1024] for i in range(2)]
        self.h2ft = self.Creg[:, 2 * S_ + 2048: 2 * S_ + 3072]
        self.t32 = [at(f"t32_{i}", [128, D], F32) for i in range(2)]
        self.sqscr = at("sqscr", [128, D], F32)
        self.n32 = at("n32", [128, D], F32)
        self.g_bc = at("g_bc", [128, D], BF16)
        self.b_bc = at("b_bc", [128, D], BF16)
        self.bo_bc = at("bo_bc", [128, D], BF16)
        self.bv_bc = at("bv_bc", [128, D], BF16)
        self.gb_bc = at("gb_bc", [128, 8], F32)
        self.TR = at("TR", [65, S_], BF16)
        self.sacc = at("sacc", [65, 65], F32)
        self.sufB = at("sufB", [65, self.TT * 65], BF16)
        self.lrow = at("lrow", [65, 512], F32)
        self.mk = at("mk", [65, 4], F32)
        self.r0a = at("r0a", [1, 512], F32)
        self.r0b = at("r0b", [1, 512], BF16)
        self.rbc_sb = at("rbc_sb", [64, 512], BF16)
        self.rb32 = at("rb32", [64, 512], F32)
        self.recrow = at("recrow", [65, 512], F32)
        self.sq_sc = at("sq_sc", [64, 512], BF16)
        self.wnorm = at("wnorm", [64, 512], BF16)
        self.dsc = at("dsc", [128, 128], BF16)
        self.U01 = at("U01", [128, 128], BF16)
        self.L01 = at("L01", [128, 128], BF16)
        self.idn = at("idn", [128, 128], BF16)
        self.idnf = at("idnf", [128, 128], F32)
        self.ones_c = at("ones_c", [128, 1], BF16)
        self.onef_c = at("onef_c", [128, 1], F32)
        self.ones_r0 = at("ones_r0", [1, 128], BF16)
        self.onef_r64 = at("onef_r64", [65, 64], F32)
        self.onef_r = at("onef_r", [65, 128], F32)
        self.bkc = at("bkc", [64, H], F32)
        self.bqc = at("bqc", [64, H], F32)
        self.qsc = at("qsc", [64, H], F32)
        self.ksc = at("ksc", [64, H], F32)
        self.gcol = at("gcol", [128, FC], F32)
        self.bcol = at("bcol", [128, FC], F32)
        self.b1c = [at(f"b1c_{i}", [128, 32], F32) for i in range(2)]
        self.b2bf = at("b2bf", [8, D], BF16)
        self.gwc = at("gwc", [128, FC * 8], BF16)
        self.stat = at("stat", [128, 8], F32)
        self.gsc = at("gsc", [128, 8], F32)
        self.gsc2 = at("gsc2", [128, 8], F32)
        self.gb16 = at("gb16", [128, 8], BF16)
        self.grow = [at(f"grow_{i}", [8, 128], BF16) for i in range(2)]
        self.gate_all = at("gate_all", [128, NSEQ * self.TT * 8], F32)
        self.hsb = [at(f"hsb_{i}", [128, 512], BF16) for i in range(3)]
        self.xb16 = at("xb16", [128, D], BF16)
        self.rowtmp = at("rowtmp", [128, 256], F32)
        self.wst = [at(f"wst_{i}", [128, 512], F32) for i in range(3)]
        self.wstF = at("wstF", [128, 512], F32)
        self.cst = at("cst", [128, 4], F32)
        for i, vv in enumerate([LNEPS, L2EPS, 1e-30, EPS1 / 8]):
            nc.const_aps.register(F32, vv, self.cst[:, i:i + 1])

        ap_ = nc.alloc_psum_tensor
        self.psP = [ap_(f"psP{i}", [128, 512], F32) for i in range(2)]
        self.psS = [ap_(f"psS{i}", [128, 512], F32) for i in range(2)]
        self.psV = ap_("psV", [128, 512], F32)
        self.psR = ap_("psR", [128, 512], F32)
        self.psO = ap_("psO", [128, 1024], F32)
        self.psOb = self.psO[:].bitcast(BF16)

        self.pe = Ev(nc, "pe", "tensor")
        self.pe2 = Ev(nc, "pe2", "tensor")
        self.act = Ev(nc, "act", "scalar")
        self.act2 = Ev(nc, "act2", "scalar")
        self.dve = Ev(nc, "dve", "vector")
        self.dve2 = Ev(nc, "dve2", "vector")
        self.pool = Ev(nc, "pool", "gpsimd")
        self.syn = Ev(nc, "syn", "sync", dma=True)
        self.scl = Ev(nc, "scl", "scalar", dma=True)
        self.gps = Ev(nc, "gps", "gpsimd", dma=True)
        self.gsm = Ev(nc, "gsm", "gpsimd", dma=True)
        self.gsh = Ev(nc, "gsh", "gpsimd", dma=True)
        self.all_dma = [self.syn, self.scl, self.gps, self.gsm, self.gsh]
        Ph.ENG_EV = {"vector": self.dve, "scalar": self.act,
                     "tensor": self.pe, "gpsimd": self.pool}

    def stage_w(self, R, dst, src, cols, part=128):
        if not hasattr(self, "wst_last"):
            self.wst_last = {0: 0, 1: 0, 2: 0}
            self._convq = []
        multi = len(src.shape) > 2
        inner = int(np.prod(src.shape[2:])) if multi else 1
        if multi:
            assert 512 % inner == 0 and cols == src.shape[1] * inner
        pieces = list(range(0, cols, 512))
        for a in pieces:
            c = min(512, cols - a)
            b = getattr(self, "wst_rr", 0)
            self.wst_rr = (b + 1) % 3
            if any(j[1] == b for j in self._convq):
                self.stage_fence(R)
            def dmafn(sy, b=b, a=a, c=c, src=src, part=part, multi=multi, inner=inner):
                if multi:
                    return sy.dma_start(
                        self.wst[b][0:part, 0:c].rearrange(
                            "p (x y) -> p x y", y=inner),
                        src[0:part, a // inner:(a + c) // inner])
                return sy.dma_start(self.wst[b][0:part, 0:c], src[0:part, a:a + c])
            R("sync", dmafn, waits=[(self.dve, self.wst_last[b])], sig=self.syn, n=16)
            tok = self.syn.n
            if part == 128 and c == 512:
                self._flush_convs(R, tok)
            self._convq.append((tok, b, dst, a, c, part))
        return None

    def _flush_convs(self, R, tok_f):
        rem = []
        for (tok_p, b, dst, a, c, part) in self._convq:
            if tok_p < tok_f:
                R("vector", lambda v, b=b, a=a, c=c, dst=dst, part=part:
                  v.tensor_copy(dst[0:part, a:a + c], self.wst[b][0:part, 0:c]),
                  waits=[(self.syn, tok_f)], sig=self.dve, ss=True)
                self.wst_last[b] = self.dve.n
            else:
                rem.append((tok_p, b, dst, a, c, part))
        self._convq = rem

    def stage_fence(self, R):
        if not getattr(self, "_convq", None):
            return self.dve.n
        R("sync", lambda sy: sy.dma_start(self.wstF[:], self.x[0, 0:128, 0:512]),
          sig=self.syn, n=16)
        self._flush_convs(R, self.syn.n)
        R("vector", lambda v: v.engine_nop(), sig=self.dve)
        return self.dve.n

    def vsl(self, t, h, n=65):
        return self.Vt[:, t * (H * 65) + 65 * h: t * (H * 65) + 65 * h + n]

    def build(self, upto="end", a2mode="full"):
        self.a2mode = a2mode
        self.setup()
        for s in range(self.NSEQ):
            self.stage_a(s)
            if upto == "A":
                continue
            self.attn(s, 1)
            if upto == "attn1":
                continue
            self.attn(s, 2)
        if upto in ("A", "attn1", "attn2"):
            self.finish()
            return self.nc
        for e in range(self.NE):
            self.moe_phase1(e)
            self.moe_phase2(e)
        self.stage_e()
        self.finish()
        return self.nc

    def setup(self):
        S = self.S
        ph = Ph(self.nc, "setup")
        R = ph.rec
        def mk_msets(g):
            g.memset(self.idn[:], 0.0)
            g.memset(self.idnf[:], 0.0)
            g.memset(self.U01[:], 0.0)
            return g.memset(self.L01[:], 1.0)
        R("gpsimd", mk_msets, sig=self.pool, ss=True)
        def mk_asels(g):
            g.affine_select(out=self.idn[:], in_=self.idn[:], compare_op=ALU.not_equal,
                            fill=1.0, base=0, pattern=[[-1, 128]], channel_multiplier=1)
            g.affine_select(out=self.idnf[:], in_=self.idnf[:], compare_op=ALU.not_equal,
                            fill=1.0, base=0, pattern=[[-1, 128]], channel_multiplier=1)
            g.affine_select(out=self.U01[:], in_=self.U01[:], compare_op=ALU.is_gt,
                            fill=1.0, base=0, pattern=[[-1, 128]], channel_multiplier=1)
            return g.affine_select(out=self.L01[:], in_=self.L01[:], compare_op=ALU.is_gt,
                                   fill=0.0, base=0, pattern=[[-1, 128]], channel_multiplier=1)
        R("gpsimd", mk_asels, sig=self.pool, ss=True)
        def mk_masks(g):
            g.memset(self.ones_c[:], 1.0)
            g.memset(self.onef_c[:], 1.0)
            g.memset(self.ones_r0[:], 1.0)
            g.memset(self.onef_r64[64:65, :], 1.0)
            g.memset(self.onef_r[:], 1.0)
            for i, vv in enumerate([LNEPS, L2EPS, 1e-30, EPS1 / 8]):
                g.memset(self.cst[:, i:i + 1], vv)
            g.memset(self.kb[0][64:65, :], 1.0)
            g.memset(self.kb[1][64:65, :], 1.0)
            ins = g.memset(
                self.Vt.rearrange("p (t h c) -> p t h c", h=H, c=65)[:, :, :, 64:65], 1.0)
            return ins
        R("gpsimd", mk_masks, sig=self.pool)

        for fc in range(FC):
            self.stage_w(R, self.w0[:, fc * D:(fc + 1) * D],
                         self.wk1.rearrange("n d h -> d (n h)")[fc * 128:(fc + 1) * 128]
                         if False else
                         self.wk1.rearrange("n d h -> d n h")[fc * 128:(fc + 1) * 128], D)
            self.stage_w(R, self.w1s[:, fc * D:(fc + 1) * D],
                         self.wq1.rearrange("n d h -> d n h")[fc * 128:(fc + 1) * 128], D)
            self.stage_w(R, self.w2s[:, fc * D:(fc + 1) * D],
                         self.wv1.rearrange("n d h -> d n h")[fc * 128:(fc + 1) * 128], D)

        self.stage_w(R, self.b2bf, self.b2, D, part=self.NE)
        for fc in range(FC):
            self.stage_w(R, self.gwc[:, fc * 8:(fc + 1) * 8],
                         self.gw[fc * 128:(fc + 1) * 128, :], 8)

        def rows(sy):
            sy.dma_start(self.t32[0][0:H, 0:DH], self.bk1).then_inc(self.syn.h, 16)
            sy.dma_start(self.t32[0][0:H, 64:64 + DH], self.bq1).then_inc(self.syn.h, 16)
            sy.dma_start(self.t32[0][0:H, 128:128 + DH],
                         self.qs2.rearrange("h o d -> h (o d)")).then_inc(self.syn.h, 16)
            sy.dma_start(self.t32[0][0:H, 192:192 + DH],
                         self.ks2.rearrange("h o d -> h (o d)")).then_inc(self.syn.h, 16)
            sy.dma_start(self.t32[0][32:33, 0:D], self.lng[None, :]).then_inc(self.syn.h, 16)
            sy.dma_start(self.t32[1][0:1, 0:D], self.lnb[None, :]).then_inc(self.syn.h, 16)
            sy.dma_start(self.sqscr[0:1, 0:D], self.bo1[None, :]).then_inc(self.syn.h, 16)
            sy.dma_start(self.sqscr[32:33, 0:D],
                         self.bv1.rearrange("h d -> (h d)")[None, :]).then_inc(self.syn.h, 16)
            ins = sy.dma_start(self.sqscr[64:65, 0:8], self.gb[None, :])
            return ins
        self.syn.n += 8 * 16
        R("sync", rows, sig=self.syn, n=16)

        self.stage_w(R, self.xb16[:, 0:128],
                     self.lng.rearrange("(a b) -> a b", b=128), 128, part=FC)
        self.stage_w(R, self.xb16[:, 128:256],
                     self.lnb.rearrange("(a b) -> a b", b=128), 128, part=FC)
        self.lnchunk_tok = self.stage_fence(R)
        ph.emit()

        ph = Ph(self.nc, "setup2")
        R = ph.rec
        R("tensor", lambda t: t.transpose(self.psR[0:DH, 0:128], self.t32[0][:, 0:DH], self.idnf[:]),
          waits=[self.syn], sig=self.pe)
        R("tensor", lambda t: t.transpose(self.psR[0:DH, 128:256],
                                          self.t32[0][:, 64:64 + DH], self.idnf[:]), sig=self.pe)
        R("tensor", lambda t: t.transpose(self.psR[0:DH, 256:384],
                                          self.t32[0][:, 128:128 + DH], self.idnf[:]), sig=self.pe)
        R("tensor", lambda t: t.transpose(self.psR[0:DH, 384:512],
                                          self.t32[0][:, 192:192 + DH], self.idnf[:]), sig=self.pe)
        R("tensor", lambda t: t.transpose(self.psV[:].bitcast(BF16)[:, 0:128],
                                          self.xb16[:, 0:128], self.idn[:]),
          waits=[self.dve], sig=self.pe)
        R("tensor", lambda t: t.transpose(self.psV[:].bitcast(BF16)[:, 128:256],
                                          self.xb16[:, 128:256], self.idn[:]), sig=self.pe)
        def bc(t, dst, src_row, base):
            return t.matmul(dst, self.onef_r[base:base + 1, :], src_row,
                            start=True, stop=True, skip_group_check=True)
        R("tensor", lambda t: bc(t, self.psO[:, 0:512], self.t32[0][32:33, 0:512], 32), sig=self.pe)
        R("tensor", lambda t: bc(t, self.psO[:, 512:1024], self.t32[0][32:33, 512:1024], 32), sig=self.pe)
        R("tensor", lambda t: bc(t, self.psP[0][:, 0:512], self.t32[1][0:1, 0:512], 0), sig=self.pe)
        R("tensor", lambda t: bc(t, self.psP[1][:, 0:512], self.t32[1][0:1, 512:1024], 0), sig=self.pe)
        R("tensor", lambda t: bc(t, self.psS[0][:, 0:512], self.sqscr[0:1, 0:512], 0), sig=self.pe)
        R("tensor", lambda t: bc(t, self.psS[1][:, 0:512], self.sqscr[0:1, 512:1024], 0), sig=self.pe)
        R("tensor", lambda t: bc(t, self.psV[:, 200:208], self.sqscr[64:65, 0:8], 64), sig=self.pe)

        def evacs(v):
            v.tensor_copy(self.bkc[:, :], self.psR[0:DH, 0:H])
            v.tensor_copy(self.bqc[:, :], self.psR[0:DH, 128:128 + H])
            v.tensor_copy(self.qsc[:, :], self.psR[0:DH, 256:256 + H])
            v.tensor_copy(self.ksc[:, :], self.psR[0:DH, 384:384 + H])
            v.tensor_copy(self.gcol[:, :], self.psV[:].bitcast(BF16)[:, 0:FC])
            v.tensor_copy(self.bcol[:, :], self.psV[:].bitcast(BF16)[:, 128:128 + FC])
            v.tensor_copy(self.g_bc[:, :], self.psO[:, :])
            v.tensor_copy(self.b_bc[:, 0:512], self.psP[0][:, :])
            v.tensor_copy(self.b_bc[:, 512:1024], self.psP[1][:, :])
            v.tensor_copy(self.bo_bc[:, 0:512], self.psS[0][:, :])
            v.tensor_copy(self.bo_bc[:, 512:1024], self.psS[1][:, :])
            return v.tensor_copy(self.gb_bc[:, :], self.psV[:, 200:208])
        R("vector", evacs, waits=[self.pe], sig=self.dve)
        R("tensor", lambda t: bc(t, self.psR[:, 0:512], self.sqscr[32:33, 0:512], 32),
          waits=[self.dve], sig=self.pe)
        R("vector", lambda v: v.tensor_copy(self.bv_bc[:, 0:512], self.psR[:, 0:512]),
          waits=[self.pe], sig=self.dve)
        R("tensor", lambda t: bc(t, self.psR[:, 0:512], self.sqscr[32:33, 512:1024], 32),
          waits=[self.dve], sig=self.pe)
        R("vector", lambda v: v.tensor_copy(self.bv_bc[:, 512:1024], self.psR[:, 0:512]),
          waits=[self.pe], sig=self.dve)
        ph.emit()

    def stage_a(self, s):
        S, TT = self.S, self.TT
        ph = Ph(self.nc, f"A{s}")
        R = ph.rec
        xtok = []
        evtok = []
        for tt in range(TT):
            b = tt % 2
            w = [(self.dve2, evtok[tt - 2][0])] if tt >= 2 else [(self.scl, self.scl.n)]
            R("sync", lambda sy, s=s, tt=tt, b=b:
              sy.dma_start(self.t32[b][:], self.x[s, tt * 128:(tt + 1) * 128, :]),
              waits=w, sig=self.syn, n=16)
            xtok.append(self.syn.n)
            R("scalar", lambda sc, b=b: sc.activation(self.sqscr[:], self.t32[b][:], AF.Square),
              waits=[(self.syn, xtok[tt]), (self.dve, self.dve.n)], sig=self.act)
            R("vector", lambda v, b=b: v.reduce_sum(self.stat[:, 0:1], self.t32[b][:], axis=AX),
              waits=[(self.syn, xtok[tt])], sig=self.dve, ss=True)
            R("vector", lambda v: v.reduce_sum(self.stat[:, 1:2], self.sqscr[:], axis=AX),
              waits=[self.act], sig=self.dve, ss=True)
            c = 1.0 / D
            R("vector", lambda v, c=c: v.tensor_scalar(self.stat[:, 2:3], self.stat[:, 0:1], c, None, ALU.mult),
              ss=True)
            R("vector", lambda v, c=c: v.tensor_scalar(self.stat[:, 3:4], self.stat[:, 1:2], c, None, ALU.mult),
              ss=True)
            R("vector", lambda v: v.tensor_tensor(self.stat[:, 4:5], self.stat[:, 2:3], self.stat[:, 2:3], ALU.mult),
              ss=True)
            R("vector", lambda v: v.tensor_tensor(self.stat[:, 5:6], self.stat[:, 3:4], self.stat[:, 4:5], ALU.subtract),
              sig=self.dve, ss=True)
            R("scalar", lambda sc: sc.activation(self.stat[:, 6:7], self.stat[:, 5:6], AF.Ln, bias=LNEPS),
              waits=[self.dve], ss=True)
            R("scalar", lambda sc: sc.activation(self.stat[:, 6:7], self.stat[:, 6:7], AF.Exp, scale=-0.5),
              sig=self.act, ss=True)
            R("vector", lambda v: v.tensor_tensor(self.stat[:, 7:8], self.stat[:, 2:3], self.stat[:, 6:7], ALU.mult),
              waits=[self.act], ss=True)
            R("vector", lambda v: v.tensor_scalar(self.stat[:, 7:8], self.stat[:, 7:8], -1.0, None, ALU.mult),
              sig=self.dve, ss=True)
            if getattr(self, "a_noln", False):
                R("vector", lambda v, b=b: v.tensor_copy(self.xb16[:], self.t32[b][:]),
                  waits=[self.pe, self.act], sig=self.dve2)
            else:
                R("vector", lambda v, b=b: v.tensor_scalar(
                    self.xb16[:], self.t32[b][:], self.stat[:, 6:7], self.stat[:, 7:8],
                    ALU.mult, ALU.add),
                  waits=[self.pe, self.act], sig=self.dve2)
            evtok.append((self.dve2.n, self.dve.n))
            def trs(t, tt=tt):
                for fc in range(FC):
                    ins = t.transpose(self.psOb[:, fc * 128:(fc + 1) * 128],
                                      self.xb16[:, fc * 128:(fc + 1) * 128], self.idn[:])
                return ins
            R("tensor", trs, waits=[self.dve2], sig=self.pe)
            R("vector", lambda v, tt=tt: v.tensor_copy(
                self.xnh.rearrange("p (fc s) -> p fc s", fc=FC)[:, :, tt * 128:(tt + 1) * 128],
                self.psOb[:, 0:FC * 128].rearrange("p (fc j) -> p fc j", fc=FC)),
              waits=[self.pe], sig=self.dve2)
        if getattr(self, "dbg", False) and s == 0:
            R("gpsimd", lambda g: g.dma_start(self.dbgd[7], self.xnh[:, 0:self.S]),
              waits=[self.dve2], sig=self.gsh, n=16)
        ph.emit()

    def attn(self, s, kind):
        a1 = kind == 1
        S, TT, QC = self.S, self.TT, self.QC
        nc = self.nc

        if not a1:
            ph = Ph(nc, f"hconv{s}")
            def hconv(v):
                for fc in range(FC):
                    ins = v.tensor_scalar(
                        self.xnh[:, fc * S:(fc + 1) * S], self.xnh[:, fc * S:(fc + 1) * S],
                        self.gcol[:, fc:fc + 1], self.bcol[:, fc:fc + 1], ALU.mult, ALU.add)
                return ins
            ph.rec("vector", hconv, sig=self.dve)
            for fc in range(FC):
                self.stage_w(ph.rec, self.w0[:, fc * D:(fc + 1) * D],
                             self.wk2[fc * 128:(fc + 1) * 128, :], D)
                self.stage_w(ph.rec, self.w1s[:, fc * D:(fc + 1) * D],
                             self.wq2[fc * 128:(fc + 1) * 128, :], D)
                self.stage_w(ph.rec, self.w2s[:, fc * D:(fc + 1) * D],
                             self.wv2[fc * 128:(fc + 1) * 128, :], D)
            self.stage_fence(ph.rec)
            ph.emit()
            if getattr(self, "dbg_stall", False):
                with nc.Block() as blk:
                    @blk.gpsimd
                    def _(g):
                        for _i in range(40):
                            g.memset(self.t32[0][:], 0.0)
        elif s == 1:
            ph = Ph(nc, "w1reload")
            for fc in range(FC):
                self.stage_w(ph.rec, self.w0[:, fc * D:(fc + 1) * D],
                             self.wk1.rearrange("n d h -> d n h")[fc * 128:(fc + 1) * 128], D)
                self.stage_w(ph.rec, self.w1s[:, fc * D:(fc + 1) * D],
                             self.wq1.rearrange("n d h -> d n h")[fc * 128:(fc + 1) * 128], D)
                self.stage_w(ph.rec, self.w2s[:, fc * D:(fc + 1) * D],
                             self.wv1.rearrange("n d h -> d n h")[fc * 128:(fc + 1) * 128], D)
            self.stage_fence(ph.rec)
            ph.emit()
        wready = self.dve.n

        ph = Ph(nc, f"attn{kind}_{s}")
        R = ph.rec

        vev = []
        for tt in range(TT):
            pp = [self.psP[tt % 2], self.psS[tt % 2]]
            w = [(self.dve, wready)]
            if tt >= 2:
                w.append((self.dve, vev[tt - 2]))
            def vproj(t, tt=tt, pp=pp):
                for half in range(2):
                    for fc in range(FC):
                        ins = t.matmul(
                            pp[half][:],
                            self.xnh[:, fc * S + tt * 128: fc * S + (tt + 1) * 128],
                            self.w2s[:, fc * D + half * 512: fc * D + (half + 1) * 512],
                            start=(fc == 0), stop=(fc == FC - 1))
                return ins
            R("tensor", vproj, waits=w, sig=self.pe)
            def vevac(v, tt=tt, pp=pp, a1=a1):
                for h in range(H):
                    src = pp[h // 8][:, (h % 8) * 64:(h % 8) * 64 + 64]
                    dst = self.vsl(tt, h, 64)
                    if a1:
                        ins = v.tensor_tensor(dst, src, self.bv_bc[:, h * 64:(h + 1) * 64], ALU.add)
                    else:
                        ins = v.tensor_copy(dst, src)
                return ins
            R("vector", vevac, waits=[self.pe], sig=self.dve)
            vev.append(self.dve.n)
        vdone_dve = self.dve.n

        first = True
        for fc in range(FC):
            if a1:
                src = self.wo1.rearrange("h d e -> (h d) e")[fc * 128:(fc + 1) * 128, :]
            else:
                src = self.wo2[fc * 128:(fc + 1) * 128, :]
            if first:
                R("vector", lambda v: v.engine_nop(), waits=[(self.dve, vdone_dve, True)],
                  sig=self.dve)
                for bb in range(3):
                    self.wst_last[bb] = self.dve.n
                first = False
            self.stage_w(R, self.w2s[:, fc * D:(fc + 1) * D], src, D)
        self.stage_fence(R)
        wo_ready = self.dve.n

        self.vdone_dve = vdone_dve
        self.wf_guard = self.scl.n
        wf_scl = []
        nheads = 0 if (not a1 and getattr(self, "a2mode", "full") == "nohead") else H
        if not a1 and getattr(self, "dbg_nheads", None):
            nheads = self.dbg_nheads
        for h in range(nheads):
            self.rec_head(R, s, kind, h, wf_scl)

        ph.emit()

        if a1 or getattr(self, "a2mode", "full") not in ("noout", "nohead"):
            self.attn_out(s, kind, wo_ready, wf_scl)
        self.y_attn_tok = self.scl.n

    def rec_head(self, R, s, kind, h, wf_scl):
        a1 = kind == 1
        S, TT, QC = self.S, self.TT, self.QC
        i = h % 2
        kb, qb = self.kb[i], self.qb[i]
        guard = [(self.pe, self.headuse[i]) if hasattr(self, "headuse") and i in getattr(self, "headuse", {}) else None]
        if not hasattr(self, "headuse"):
            self.headuse = {}
        guard = []
        if i in self.headuse:
            guard = [(self.pe, self.headuse[i])]

        if getattr(self, "dbg", False) and kind == 2 and h == 0 and getattr(self, "dbg_wdump", False):
            R("vector", lambda v: v.tensor_copy(
                self.Creg[:, 4096:4096 + 8 * S], self.xnh[:, 0:8 * S]),
              waits=[(self.dve, self.vdone_dve)], sig=self.dve, ss=True)

        if getattr(self, "dbg_projonly", False) and kind == 2 and h == 0:
            for side, (wslot, psp) in enumerate([(self.w0, self.psP[0]), (self.w1s, self.psP[1])]):
                def projX(t, wslot=wslot, psp=psp, h=h):
                    for fc in range(FC):
                        ins = t.matmul(
                            psp[0:64, :],
                            wslot[:, fc * D + h * 64: fc * D + h * 64 + 64],
                            self.xnh[:, fc * S: fc * S + 512],
                            start=(fc == 0), stop=(fc == FC - 1))
                    return ins
                wX = ([self.dve2, (self.dve, self.dve.n)]
                      if getattr(self, "dbg_wowait", False)
                      else [self.dve2, (self.dve, self.vdone_dve)])
                R("tensor", projX, waits=wX, sig=self.pe)
            R("vector", lambda v: v.tensor_copy(self.n32[0:64, 0:512], self.psP[0][0:64, :]),
              waits=[self.pe], sig=self.dve, ss=True)
            R("vector", lambda v: v.tensor_copy(self.n32[0:64, 512:1024], self.psP[1][0:64, :]),
              sig=self.dve, ss=True)
            R("sync", lambda sy: sy.dma_start(self.dbgf[0, 0:64, :], self.n32[0:64, 0:512]),
              waits=[self.dve], sig=self.syn, n=16)
            R("sync", lambda sy: sy.dma_start(self.dbgf[1, 0:64, :], self.n32[0:64, 512:1024]),
              waits=[self.dve], sig=self.syn, n=16)
            if getattr(self, "dbg_slivers", True):
                for fc in range(FC):
                    R("sync", lambda sy, fc=fc: sy.dma_start(
                        self.dbgd[0, :, fc * 64:(fc + 1) * 64],
                        self.w0[:, fc * D + h * 64: fc * D + h * 64 + 64]),
                      waits=[self.dve], sig=self.syn, n=16)
                    R("sync", lambda sy, fc=fc: sy.dma_start(
                        self.dbgd[1, :, fc * 64:(fc + 1) * 64],
                        self.w1s[:, fc * D + h * 64: fc * D + h * 64 + 64]),
                      waits=[self.dve], sig=self.syn, n=16)
            return

        for side, (wslot, psp, bias_c, sc_c) in enumerate(
                [(self.w0, self.psP[0], self.bkc, self.ksc),
                 (self.w1s, self.psP[1], self.bqc, self.qsc)]):
            dstb = kb if side == 0 else qb
            for qc in range(QC):
                def proj(t, wslot=wslot, psp=psp, qc=qc, h=h):
                    for fc in range(FC):
                        ins = t.matmul(
                            psp[0:64, :],
                            wslot[:, fc * D + h * 64: fc * D + h * 64 + 64],
                            self.xnh[:, fc * S + qc * 512: fc * S + (qc + 1) * 512],
                            start=(fc == 0), stop=(fc == FC - 1))
                    return ins
                R("tensor", proj,
                  waits=guard + [self.dve2, (self.dve, self.vdone_dve)], sig=self.pe)
                if a1:
                    R("vector", lambda v, psp=psp, dstb=dstb, qc=qc, bias_c=bias_c, h=h:
                      v.tensor_scalar(dstb[0:64, qc * 512:(qc + 1) * 512], psp[0:64, :],
                                      bias_c[:, h:h + 1], None, ALU.add),
                      waits=[self.pe], sig=self.dve2)
                else:
                    R("scalar", lambda sc, psp=psp: sc.activation(
                        self.sq_sc[0:64, :], psp[0:64, :], AF.Square),
                      waits=[self.pe, self.pe2], sig=self.act)
                    R("tensor", lambda t: t.matmul(
                        self.psR[0:1, :], self.ones_c[0:64, 0:1], self.sq_sc[0:64, :],
                        start=True, stop=True, skip_group_check=True),
                      waits=[self.act, self.dve2], sig=self.pe2)
                    R("scalar", lambda sc: sc.activation(self.r0a[:], self.psR[0:1, :],
                                                         AF.Ln, bias=L2EPS),
                      waits=[self.pe2], sig=self.act, ss=True)
                    R("scalar", lambda sc: sc.activation(self.r0b[:], self.r0a[:],
                                                         AF.Exp, scale=-0.5), sig=self.act, ss=True)
                    R("tensor", lambda t: t.matmul(
                        self.psR[0:64, :], self.ones_r0[0:1, 0:64], self.r0b[:],
                        start=True, stop=True, skip_group_check=True),
                      waits=[self.act], sig=self.pe2)
                    R("vector", lambda v: v.tensor_copy(self.rbc_sb[:], self.psR[0:64, :]),
                      waits=[self.pe2], sig=self.dve, ss=True)
                    R("vector", lambda v, psp=psp, dstb=dstb, qc=qc, sc_c=sc_c, h=h:
                      v.scalar_tensor_tensor(
                          dstb[0:64, qc * 512:(qc + 1) * 512], psp[0:64, :],
                          sc_c[:, h:h + 1], self.rbc_sb[:], ALU.mult, ALU.mult),
                      sig=self.dve2)

        ksrc = kb
        for qc in range(QC):
            R("scalar", lambda sc, qc=qc: sc.activation(
                self.sq_sc[0:64, :], ksrc[0:64, qc * 512:(qc + 1) * 512], AF.Square),
              waits=[self.dve2, self.pe2], sig=self.act)
            R("tensor", lambda t: t.matmul(
                self.psR[64:65, :], self.ones_c[0:64, 0:1], self.sq_sc[0:64, :],
                start=True, stop=True, skip_group_check=True),
              waits=[self.act, self.dve], sig=self.pe2)
            if qc == 0:
                R("vector", lambda v: v.reduce_max(self.mk[64:65, 0:1], self.psR[64:65, :], axis=AX),
                  waits=[self.pe2], sig=self.dve, ss=True)
            else:
                R("vector", lambda v: v.reduce_max(self.mk[64:65, 2:3], self.psR[64:65, :], axis=AX),
                  waits=[self.pe2], sig=self.dve, ss=True)
                R("vector", lambda v: v.tensor_tensor(self.mk[64:65, 0:1], self.mk[64:65, 0:1],
                                                      self.mk[64:65, 2:3], ALU.max),
                  sig=self.dve, ss=True)
        R("scalar", lambda sc: sc.activation(self.mk[64:65, 1:2], self.mk[64:65, 0:1],
                                             AF.Ln, bias=1e-30), waits=[self.dve], ss=True)
        R("scalar", lambda sc: sc.mul(self.mk[64:65, 1:2], self.mk[64:65, 1:2], 0.5),
          sig=self.act, ss=True)
        for qc in range(QC):
            R("scalar", lambda sc, qc=qc: sc.activation(
                self.sq_sc[0:64, :], qb[0:64, qc * 512:(qc + 1) * 512], AF.Square),
              waits=[self.pe2], sig=self.act)
            R("tensor", lambda t: t.matmul(
                self.psR[64:65, :], self.ones_c[0:64, 0:1], self.sq_sc[0:64, :],
                start=True, stop=True, skip_group_check=True),
              waits=[self.act], sig=self.pe2)
            R("scalar", lambda sc: sc.activation(self.lrow[64:65, :], self.psR[64:65, :],
                                                 AF.Ln, bias=1e-30),
              waits=[self.pe2], sig=self.act, ss=True)
            R("scalar", lambda sc: sc.activation(
                self.recrow[64:65, :], self.lrow[64:65, :], AF.Exp,
                scale=0.5, bias=self.mk[64:65, 1:2]), sig=self.act, ss=True)
            R("scalar", lambda sc, qc=qc: sc.mul(
                qb[64:65, qc * 512:(qc + 1) * 512], self.recrow[64:65, :], -1.0),
              sig=self.act2, ss=True)
            if a1:
                R("scalar", lambda sc, qc=qc: sc.activation(
                    self.TR[64:65, qc * 512:(qc + 1) * 512],
                    qb[64:65, qc * 512:(qc + 1) * 512],
                    AF.Exp, scale=0.125, bias=EPS1 / 8), sig=self.act2, ss=True)
        brow_tok = (self.act2.n, self.pe2.n)

        if a1:
            R("vector", lambda v: v.memset(self.sacc[64:65, :], 0.0),
              waits=[self.pe], sig=self.dve, ss=True)
            for t_ in range(TT - 1, -1, -1):
                R("vector", lambda v, t_=t_: v.tensor_copy(
                    self.sufB[64:65, t_ * 65:(t_ + 1) * 65], self.sacc[64:65, :]),
                  sig=self.dve, ss=True)
                if t_ > 0:
                    R("tensor", lambda t, t_=t_: t.matmul(
                        self.psR[64:65, 0:65], self.ones_c[:, 0:1], self.vsl(t_, h),
                        start=True, stop=True, skip_group_check=True),
                      waits=[self.act, self.dve], sig=self.pe2)
                    R("vector", lambda v: v.tensor_tensor(
                        self.sacc[64:65, :], self.sacc[64:65, :], self.psR[64:65, 0:65], ALU.add),
                      waits=[self.pe2], sig=self.dve, ss=True)

        scale = 0.125 if a1 else 10.0
        np_ = 0
        ptok = [0, 0, 0]
        for qc in range(QC):
            for kt in range(4 * qc + 4):
                j = kt - 4 * qc
                col0 = max(0, j * 128)
                ncol = 512 - col0
                b = (qc * 97 + kt) % 2
                r = np_ % 3
                np_ += 1
                w = [(self.act, self.actS.get((s, kind, b), 0) if hasattr(self, 'actS') else 0)]
                if not hasattr(self, "psS_last"):
                    self.psS_last = {}
                w = []
                if b in self.psS_last:
                    w.append((self.act2, self.psS_last[b]))
                if r in getattr(self, "pt_last", {}):
                    w.append((self.pe, self.pt_last[r]))
                w.append((self.act2, brow_tok[0]))
                def smm(t, kb=kb, qb=qb, b=b, qc=qc, kt=kt, col0=col0, ncol=ncol):
                    return t.matmul(
                        self.psS[b][:, col0:col0 + ncol],
                        kb[0:65, kt * 128:(kt + 1) * 128],
                        qb[0:65, qc * 512 + col0: qc * 512 + col0 + ncol],
                        start=True, stop=True)
                R("tensor", smm, waits=w, sig=self.pe)
                R("scalar", lambda sc, b=b, r=r, col0=col0, ncol=ncol, scale=scale:
                  sc.activation(self.hsb[r][:, 0:ncol], self.psS[b][:, col0:col0 + ncol],
                                AF.Exp, scale=scale),
                  waits=[self.pe], sig=self.act2)
                self.psS_last[b] = self.act2.n
                dv_fix = None
                if j >= 0:
                    if a1:
                        R("tensor", lambda t, qc=qc, j=j: t.matmul(
                            self.psR[0:128, 0:128], kb[64:65, 0:128],
                            self.TR[64:65, qc * 512 + j * 128: qc * 512 + j * 128 + 128],
                            start=True, stop=True, skip_group_check=True),
                          waits=[self.act, self.dve], sig=self.pe2)
                        R("vector", lambda v, r=r: v.tensor_tensor(
                            self.hsb[r][:, 0:128], self.hsb[r][:, 0:128], self.U01[:], ALU.mult),
                          waits=[self.act2], sig=self.dve)
                        R("vector", lambda v: v.tensor_tensor(
                            self.dsc[:], self.psR[0:128, 0:128], self.L01[:], ALU.mult),
                          waits=[self.pe2], sig=self.dve)
                        R("vector", lambda v, r=r: v.tensor_tensor(
                            self.hsb[r][:, 0:128], self.hsb[r][:, 0:128], self.dsc[:], ALU.add),
                          sig=self.dve)
                    else:
                        R("vector", lambda v, r=r: v.tensor_tensor(
                            self.hsb[r][:, 0:128], self.hsb[r][:, 0:128], self.U01[:], ALU.mult),
                          waits=[self.act2], sig=self.dve)
                    dv_fix = self.dve.n
                wpv = [self.act2] if dv_fix is None else [(self.dve, dv_fix)]
                if kt == 0:
                    wpv.append(self.dve2)
                def pv(t, kt=kt, h=h, r=r, col0=col0, ncol=ncol, qc=qc):
                    return t.matmul(self.psV[0:65, col0:col0 + ncol],
                                    self.vsl(kt, h), self.hsb[r][:, 0:ncol],
                                    start=(kt == 0), stop=False, skip_group_check=True)
                R("tensor", pv, waits=wpv, sig=self.pe)
                if not hasattr(self, "pt_last"):
                    self.pt_last = {}
                self.pt_last[r] = self.pe.n
            if a1:
                for j2 in range(4):
                    R("tensor", lambda t, qc=qc, j2=j2: t.matmul(
                        self.psV[0:65, j2 * 128:(j2 + 1) * 128],
                        self.sufB[64:65, (4 * qc + j2) * 65:(4 * qc + j2 + 1) * 65],
                        self.TR[64:65, qc * 512 + j2 * 128: qc * 512 + (j2 + 1) * 128],
                        start=False, stop=(j2 == 3), skip_group_check=True),
                      waits=[self.dve], sig=self.pe)
            else:
                pass
            pv_done = self.pe.n
            R("vector", lambda v: v.reciprocal(self.recrow[64:65, :], self.psV[64:65, :]),
              waits=[(self.pe, pv_done)], sig=self.dve, ss=True)
            R("tensor", lambda t: t.matmul(
                self.psR[0:64, :], self.onef_r64[64:65, 0:64], self.recrow[64:65, :],
                start=True, stop=True, skip_group_check=True),
              waits=[self.dve], sig=self.pe2)
            R("vector", lambda v: v.tensor_copy(self.rb32[:], self.psR[0:64, :]),
              waits=[self.pe2], sig=self.dve, ss=True)
            R("vector", lambda v: v.tensor_tensor(
                self.wnorm[:], self.psV[0:64, :], self.rb32[:], ALU.mult), sig=self.dve, ss=True)
            cbuf = (h // 2) % 2
            wfw = [(self.scl, wf_scl[-2])] if len(wf_scl) >= 2 else                   [(self.scl, self.wf_guard)] if hasattr(self, "wf_guard") else []
            if h % 2 == 0:
                R("vector", lambda v, cbuf=cbuf, qc=qc: v.tensor_copy(
                    self.wfb[cbuf][0:64, qc * 512:(qc + 1) * 512], self.wnorm[:]),
                  waits=wfw, sig=self.dve2)
            else:
                R("gpsimd", lambda g, cbuf=cbuf, qc=qc: g.dma_start(
                    self.wfb[cbuf][64:128, qc * 512:(qc + 1) * 512], self.wnorm[:]),
                  waits=wfw + [self.dve], sig=self.gsh, n=16)
                R("vector", lambda v: v.engine_nop(), waits=[self.gsh], sig=self.dve2)
        if getattr(self, "dbg", False) and kind == 2 and h == 0 and getattr(self, "dbg_wdump", False):
            if getattr(self, "dbg_wdump", 0) == 2:
                for sl in range(8):
                    R("sync", lambda sy, sl=sl: sy.dma_start(
                        self.dbgd[sl, :, 0:512], self.Creg[:, 4096 + sl * 512:4096 + (sl + 1) * 512]),
                      waits=[self.dve2, self.dve], sig=self.syn, n=16)
            else:
                for sl in range(6):
                    R("sync", lambda sy, sl=sl: sy.dma_start(
                        self.dbgd[2 + sl, :, 0:512], self.w0[:, sl * 512:(sl + 1) * 512]),
                      waits=[self.dve2], sig=self.syn, n=16)
            R("vector", lambda v: v.tensor_copy(self.n32[0:64, 0:512], self.psP[0][0:64, :]),
              waits=[self.dve2, self.pe, self.pe2], sig=self.dve, ss=True)
            R("vector", lambda v: v.tensor_copy(self.n32[0:64, 512:1024], self.psP[1][0:64, :]),
              sig=self.dve, ss=True)
            R("sync", lambda sy: sy.dma_start(self.dbgf[0, 0:64, :], self.n32[0:64, 0:512]),
              waits=[self.dve], sig=self.syn, n=16)
            R("sync", lambda sy: sy.dma_start(self.dbgf[1, 0:64, :], self.n32[0:64, 512:1024]),
              waits=[self.dve], sig=self.syn, n=16)
        if getattr(self, "dbg", False) and h == 0 and kind == getattr(self, "dbgkind", 1):
            R("sync", lambda sy: sy.dma_start(self.dbgd[0, 0:65, :], kb[0:65, :]),
              waits=[self.dve2, self.act2], sig=self.syn, n=16)
            R("sync", lambda sy: sy.dma_start(self.dbgd[1, 0:65, :], qb[0:65, :]),
              waits=[self.dve2, self.act2], sig=self.syn, n=16)
            R("sync", lambda sy: sy.dma_start(self.dbgd[2, 64:65, :], self.TR[64:65, :]),
              waits=[self.act2], sig=self.syn, n=16)
            R("sync", lambda sy: sy.dma_start(self.dbgd[3, 64:65, 0:self.TT * 65], self.sufB[64:65, :]),
              waits=[self.dve], sig=self.syn, n=16)
            R("sync", lambda sy: sy.dma_start(self.dbgd[4, 64:65, 0:512], self.wnorm[0:1, :]),
              waits=[self.dve], sig=self.syn, n=16)
            R("sync", lambda sy: sy.dma_start(self.dbgd[6, 0:64, 0:512], self.wnorm[:]),
              waits=[self.dve], sig=self.syn, n=16)
            R("sync", lambda sy: sy.dma_start(
                self.dbgd[7], self.xnh[:, 0:self.S]),
              waits=[self.dve2], sig=self.syn, n=16)
        self.headuse[i] = self.pe.n
        if h % 2 == 1:
            cbuf = (h // 2) % 2
            c = h // 2
            R("scalar", lambda sc, c=c, cbuf=cbuf: sc.dma_start(
                self.wfd[c], self.wfb[cbuf]),
              waits=[self.dve2, self.gsh], sig=self.scl, n=16)
            wf_scl.append(self.scl.n)

    def attn_out(self, s, kind, wo_ready, wf_scl):
        a1 = kind == 1
        S, TT = self.S, self.TT
        ph = Ph(self.nc, f"aout{kind}_{s}")
        R = ph.rec
        optok = []
        petok = []
        a1tok = []
        for tt in range(TT):
            b = tt % 2
            w = [(self.scl, wf_scl[-1])]
            if tt >= 2:
                w.append((self.pe, petok[tt - 2]))
            R("sync", lambda sy, tt=tt, b=b: sy.dma_start(
                self.opj[b][:].rearrange("p (c j) -> p c j", c=FC),
                self.wfd.rearrange("c p s -> p c s")[:, :, tt * 128:(tt + 1) * 128]),
              waits=w, sig=self.syn, n=16)
            optok.append(self.syn.n)
            if a1:
                xw = [(self.dve, a1tok[tt - 2]), (self.scl, self.scl.n)] if tt >= 2 else []
                R("sync", lambda sy, tt=tt, b=b: sy.dma_start(
                    self.t32[b][:], self.x[s, tt * 128:(tt + 1) * 128, :]),
                  waits=xw, sig=self.syn, n=16)
            wmm = [(self.syn, optok[tt] + (16 if a1 else 0)), (self.dve, wo_ready)]
            if tt >= 1:
                wmm.append(self.dve)
                wmm.append(self.dve2)
            def omm(t, b=b):
                for half in range(2):
                    for c in range(FC):
                        ins = t.matmul(self.psO[:, half * 512:(half + 1) * 512],
                                       self.opj[b][:, c * 128:(c + 1) * 128],
                                       self.w2s[:, c * D + half * 512: c * D + (half + 1) * 512],
                                       start=(c == 0), stop=(c == FC - 1))
                return ins
            R("tensor", omm, waits=wmm, sig=self.pe)
            petok.append(self.pe.n)
            if not a1 and getattr(self, "a2mode", "full") == "plainout":
                R("vector", lambda v: v.tensor_tensor(self.n32[:], self.psO[:], self.bo_bc[:], ALU.add),
                  waits=[self.pe], sig=self.dve)
                continue
            if a1:
                R("vector", lambda v: v.tensor_tensor(self.n32[:], self.psO[:], self.bo_bc[:], ALU.add),
                  waits=[self.pe], sig=self.dve)
                R("vector", lambda v, b=b: v.tensor_tensor(
                    self.t32[b][:], self.n32[:], self.t32[b][:], ALU.add),
                  waits=[(self.scl, self.scl.n), (self.syn, optok[tt] + 16)], sig=self.dve)
                a1tok.append(self.dve.n)
                R("scalar", lambda sc, s=s, tt=tt, b=b: sc.dma_start(
                    self.y[s, tt * 128:(tt + 1) * 128, :], self.t32[b][:]),
                  waits=[self.dve], sig=self.scl, n=16)
            else:
                self.rec_h2_gate(R, s, tt, b)
        ph.emit()

    def rec_h2_gate(self, R, s, tt, b):
        S = self.S
        R("scalar", lambda sc: sc.activation(self.sqscr[:], self.psO[:], AF.Square),
          waits=[self.pe], sig=self.act)
        R("vector", lambda v: v.reduce_sum(self.stat[:, 0:1], self.psO[:], axis=AX),
          waits=[self.pe, self.act], sig=self.dve, ss=True)
        R("vector", lambda v: v.reduce_sum(self.stat[:, 1:2], self.sqscr[:], axis=AX),
          sig=self.dve, ss=True)
        c = 1.0 / D
        R("vector", lambda v, c=c: v.tensor_scalar(self.stat[:, 2:3], self.stat[:, 0:1], c, None, ALU.mult),
          ss=True)
        R("vector", lambda v, c=c: v.tensor_scalar(self.stat[:, 3:4], self.stat[:, 1:2], c, None, ALU.mult),
          ss=True)
        R("vector", lambda v: v.tensor_tensor(self.stat[:, 4:5], self.stat[:, 2:3], self.stat[:, 2:3], ALU.mult),
          ss=True)
        R("vector", lambda v: v.tensor_tensor(self.stat[:, 5:6], self.stat[:, 3:4], self.stat[:, 4:5], ALU.subtract),
          sig=self.dve, ss=True)
        R("scalar", lambda sc: sc.activation(self.stat[:, 6:7], self.stat[:, 5:6], AF.Ln, bias=LNEPS),
          waits=[self.dve], ss=True)
        R("scalar", lambda sc: sc.activation(self.stat[:, 6:7], self.stat[:, 6:7], AF.Exp, scale=-0.5),
          sig=self.act, ss=True)
        R("vector", lambda v: v.tensor_tensor(self.stat[:, 7:8], self.stat[:, 2:3], self.stat[:, 6:7], ALU.mult),
          waits=[self.act], ss=True)
        R("vector", lambda v: v.tensor_scalar(self.stat[:, 7:8], self.stat[:, 7:8], -1.0, None, ALU.mult),
          sig=self.dve, ss=True)
        R("vector", lambda v: v.tensor_scalar(self.n32[:], self.psO[:],
                                              self.stat[:, 6:7], self.stat[:, 7:8],
                                              ALU.mult, ALU.add),
          sig=self.dve)
        R("vector", lambda v: v.tensor_tensor(self.sqscr[:], self.n32[:], self.g_bc[:], ALU.mult))
        R("vector", lambda v: v.tensor_tensor(self.xb16[:], self.sqscr[:], self.b_bc[:], ALU.add),
          waits=[self.pe], sig=self.dve)
        if getattr(self, "a2mode", "full") == "h2a":
            return
        def trs(t):
            for fc in range(FC):
                ins = t.transpose(self.psOb[:, fc * 128:(fc + 1) * 128],
                                  self.xb16[:, fc * 128:(fc + 1) * 128], self.idn[:])
            return ins
        R("tensor", trs, waits=[self.dve, self.dve2], sig=self.pe)
        R("vector", lambda v: v.tensor_copy(self.h2ft[:], self.psOb[:, 0:FC * 128]),
          waits=[self.pe], sig=self.dve2)
        if getattr(self, "a2mode", "full") == "h2b":
            return
        R("scalar", lambda sc, s=s, tt=tt: sc.dma_start(
            self.h2d.rearrange("n (c p) s -> n p c s", p=128)[s, :, :, tt * 128:(tt + 1) * 128],
            self.h2ft[:].rearrange("p (c j) -> p c j", c=FC)),
          waits=[self.dve2], sig=self.scl, n=16)
        if getattr(self, "a2mode", "full") == "h2nogate":
            R("vector", lambda v, s=s, tt=tt: v.memset(
                self.gate_all[:, (s * self.TT + tt) * 8:(s * self.TT + tt + 1) * 8], 0.125),
              sig=self.dve)
            R("vector", lambda v, b=b: v.memset(self.grow[b][:], 0.125), sig=self.dve)
            R("scalar", lambda sc, s=s, tt=tt, b=b: sc.dma_start(
                self.gtd[:, s * self.S + tt * 128: s * self.S + (tt + 1) * 128], self.grow[b][:]),
              waits=[self.dve], sig=self.scl, n=16)
            return
        def gmm(t):
            for fc in range(FC):
                ins = t.matmul(self.psR[:, 0:8], self.h2ft[:, fc * 128:(fc + 1) * 128],
                               self.gwc[:, fc * 8:(fc + 1) * 8],
                               start=(fc == 0), stop=(fc == FC - 1), skip_group_check=True)
            return ins
        R("tensor", gmm, waits=[self.dve2, self.dve], sig=self.pe2)
        R("vector", lambda v: v.tensor_tensor(self.gsc[:], self.psR[:, 0:8], self.gb_bc[:], ALU.add),
          waits=[self.pe2], sig=self.dve, ss=True)
        R("vector", lambda v: v.reduce_max(self.gsc2[:, 0:1], self.gsc[:], axis=AX, negate=True),
          sig=self.dve, ss=True)
        R("scalar", lambda sc: sc.activation(self.gsc[:], self.gsc[:], AF.Exp,
                                             bias=self.gsc2[:, 0:1]),
          waits=[self.dve], sig=self.act, ss=True)
        R("vector", lambda v: v.reduce_sum(self.gsc2[:, 1:2], self.gsc[:], axis=AX),
          waits=[self.act], sig=self.dve, ss=True)
        R("vector", lambda v: v.reciprocal(self.gsc2[:, 2:3], self.gsc2[:, 1:2]),
          sig=self.dve, ss=True)
        R("vector", lambda v, s=s, tt=tt: v.tensor_scalar(
            self.gate_all[:, (s * self.TT + tt) * 8:(s * self.TT + tt + 1) * 8],
            self.gsc[:], self.gsc2[:, 2:3], None, ALU.mult),
          ss=True)
        R("vector", lambda v: v.tensor_scalar(self.gb16[:], self.gsc[:], self.gsc2[:, 2:3],
                                              None, ALU.mult), sig=self.dve, ss=True)
        R("tensor", lambda t: t.transpose(self.psR[:].bitcast(BF16)[0:8, 0:128],
                                          self.gb16[:], self.idn[:]),
          waits=[self.dve], sig=self.pe2)
        R("vector", lambda v, b=b: v.tensor_copy(self.grow[b][:], self.psR[:].bitcast(BF16)[0:8, 0:128]),
          waits=[self.pe2], sig=self.dve)
        R("scalar", lambda sc, s=s, tt=tt, b=b: sc.dma_start(
            self.gtd[:, s * S + tt * 128: s * S + (tt + 1) * 128], self.grow[b][:]),
          waits=[self.dve], sig=self.scl, n=16)

    def moe_phase1(self, e):
        S, FF, FFC = self.S, self.FF, self.FFC
        nTok = self.TOK // 512
        ph = Ph(self.nc, f"moe1_{e}")
        R = ph.rec
        if e == 0:
            for fc in range(FC):
                self.stage_w(R, self.WA[:, fc * FF:(fc + 1) * FF],
                             self.w1[e, fc * 128:(fc + 1) * 128, :], FF)
            R("sync", lambda sy: sy.dma_start(
                self.rowtmp[0:FFC // 2, 0:128],
                self.b1[e].rearrange("(a b) -> a b", b=128)[0:FFC // 2]),
              sig=self.syn, n=16)
            R("sync", lambda sy: sy.dma_start(
                self.rowtmp[0:FFC // 2, 128:256],
                self.b1[e].rearrange("(a b) -> a b", b=128)[FFC // 2:FFC]),
              sig=self.syn, n=16)
            self.w1tok = [self.stage_fence(R)]
            R("tensor", lambda t: t.transpose(self.psR[:, 0:128],
                                              self.rowtmp[:, 0:128], self.idnf[:]),
              waits=[self.syn], sig=self.pe2)
            R("tensor", lambda t: t.transpose(self.psR[:, 128:256],
                                              self.rowtmp[:, 128:256], self.idnf[:]),
              sig=self.pe2)
            R("vector", lambda v: v.tensor_copy(self.b1c[0][:, 0:FFC // 2], self.psR[:, 0:FFC // 2]),
              waits=[self.pe2], sig=self.dve)
            R("vector", lambda v: v.tensor_copy(self.b1c[0][:, FFC // 2:FFC],
                                                self.psR[:, 128:128 + FFC // 2]),
              sig=self.dve)
            self.b1tok = self.dve.n
        for ffc in range(FFC):
            self.stage_w(R, self.WB[:, ffc * D:(ffc + 1) * D],
                         self.w2[e, ffc * 128:(ffc + 1) * 128, :], D)
        self.w2tok = [self.stage_fence(R)]

        if getattr(self, "dbg", False) and e == self.NE - 1:
            R("sync", lambda sy: sy.dma_start(self.dbgd[3, :, :],
                                              self.WA[:, 0:self.S]),
              waits=[(self.dve, self.w1tok[-1])], sig=self.syn, n=16)
            R("sync", lambda sy: sy.dma_start(self.dbgd[4, :, :],
                                              self.WA[:, 3072:3072 + self.S]),
              waits=[(self.dve, self.w1tok[-1])], sig=self.syn, n=16)
            R("sync", lambda sy: sy.dma_start(
                self.dbgd[5, :, 0:32].bitcast(F32)[:, 0:32] if False else self.dbgd[5, :, 0:64],
                self.b1c[0][:, 0:32].bitcast(BF16)),
              waits=[self.dve], sig=self.syn, n=16)
        h2tok = []
        hid_scl = []
        for tokT in range(nTok):
            sq = (tokT * 512) // S
            off = (tokT * 512) % S
            b = tokT % 2
            w = []
            if tokT >= 2:
                w.append((self.pe, self.moe1_pe[tokT - 2] if hasattr(self, 'moe1_pe') and len(self.moe1_pe) > tokT - 2 else 0))
            w = [(self.pe, h2tok[tokT - 2][1])] if tokT >= 2 else [(self.scl, self.scl.n)]
            R("sync", lambda sy, sq=sq, off=off, b=b: sy.dma_start(
                self.h2cur[b][:].rearrange("p (c j) -> p c j", c=FC),
                self.h2d.rearrange("n (c p) s -> n p c s", p=128)[sq, :, :, off:off + 512]),
              waits=w, sig=self.syn, n=16)
            htok = self.syn.n
            for ffc in range(FFC):
                r = (tokT * FFC + ffc) % 3
                wmm = [(self.syn, htok), (self.dve, self.w1tok[-1])]
                pstile = (ffc % 2, (ffc // 2) % 2)
                if not hasattr(self, "ps1_last"):
                    self.ps1_last = {}
                if pstile in self.ps1_last:
                    wmm.append((self.act2, self.ps1_last[pstile]))
                prev = getattr(self, "hsb_last", {})
                if r in prev:
                    wmm.append((self.scl, prev[r]))
                def mm1(t, ffc=ffc, b=b):
                    ps = self.psP[ffc % 2] if (ffc // 2) % 2 == 0 else self.psS[ffc % 2]
                    for fc in range(FC):
                        ins = t.matmul(ps[:],
                                       self.WA[:, fc * FF + ffc * 128: fc * FF + (ffc + 1) * 128],
                                       self.h2cur[b][:, fc * 512:(fc + 1) * 512],
                                       start=(fc == 0), stop=(fc == FC - 1))
                    return ins
                R("tensor", mm1, waits=wmm, sig=self.pe)
                petok = self.pe.n
                def gelu(sc, ffc=ffc, r=r):
                    ps = self.psP[ffc % 2] if (ffc // 2) % 2 == 0 else self.psS[ffc % 2]
                    return sc.activation(self.hsb[r][:], ps[:], AF.Gelu_apprx_tanh,
                                         bias=self.b1c[e % 2][:, ffc:ffc + 1])
                R("scalar", gelu, waits=[(self.pe, petok), (self.dve, self.b1tok)], sig=self.act2, ss=True)
                self.ps1_last[(ffc % 2, (ffc // 2) % 2)] = self.act2.n
                R("scalar", lambda sc, ffc=ffc, tokT=tokT, r=r: sc.dma_start(
                    self.hidd[ffc * 128:(ffc + 1) * 128, tokT * 512:(tokT + 1) * 512],
                    self.hsb[r][:]),
                  waits=[(self.act2, self.act2.n, True)], sig=self.scl, n=16)
                if not hasattr(self, "hsb_last"):
                    self.hsb_last = {}
                self.hsb_last[r] = self.scl.n
            h2tok.append((htok, self.pe.n))
            hid_scl.append(self.scl.n)
        self.hid_ready = self.scl.n
        ph.emit()

    def moe_phase2(self, e):
        S, FF, FFC, D_ = self.S, self.FF, self.FFC, D
        nTT = self.TOK // 128
        ph = Ph(self.nc, f"moe2_{e}")
        R = ph.rec
        if e + 1 < self.NE:
            for fc in range(FC):
                self.stage_w(R, self.WA[:, fc * FF:(fc + 1) * FF],
                             self.w1[e + 1, fc * 128:(fc + 1) * 128, :], FF)
            R("sync", lambda sy, e=e: sy.dma_start(
                self.rowtmp[0:FFC // 2, 0:128],
                self.b1[e + 1].rearrange("(a b) -> a b", b=128)[0:FFC // 2]),
              sig=self.syn, n=16)
            R("sync", lambda sy, e=e: sy.dma_start(
                self.rowtmp[0:FFC // 2, 128:256],
                self.b1[e + 1].rearrange("(a b) -> a b", b=128)[FFC // 2:FFC]),
              sig=self.syn, n=16)
            self.w1tok = [self.stage_fence(R)]
            R("tensor", lambda t: t.transpose(self.psR[:, 0:128],
                                              self.rowtmp[:, 0:128], self.idnf[:]),
              waits=[self.syn, self.act2, self.dve], sig=self.pe2)
            R("tensor", lambda t: t.transpose(self.psR[:, 128:256],
                                              self.rowtmp[:, 128:256], self.idnf[:]),
              sig=self.pe2)
            R("vector", lambda v, e=e: v.tensor_copy(self.b1c[(e + 1) % 2][:, 0:FFC // 2],
                                                     self.psR[:, 0:FFC // 2]),
              waits=[self.pe2], sig=self.dve)
            R("vector", lambda v, e=e: v.tensor_copy(self.b1c[(e + 1) % 2][:, FFC // 2:FFC],
                                                     self.psR[:, 128:128 + FFC // 2]),
              sig=self.dve)
            self.b1tok = self.dve.n

        mtok = []
        for tt in range(nTT):
            b = tt % 2
            w = [(self.scl, self.hid_ready)]
            if tt >= 2:
                w.append((self.pe, mtok[tt - 2][0]))
            R("sync", lambda sy, tt=tt, b=b: sy.dma_start(
                self.hid2[:, b * FFC * 128:(b + 1) * FFC * 128].rearrange(
                    "p (c j) -> p c j", c=FFC),
                self.hidd.rearrange("(c p) s -> p c s", p=128)[:, :, tt * 128:(tt + 1) * 128]),
              waits=w, sig=self.syn, n=16)
            htok = self.syn.n
            wmm = [(self.syn, htok), (self.dve, self.w2tok[-1]),
                   (self.dve, self.b1tok)]
            if tt >= 2:
                wmm.append((self.dve, mtok[tt - 2][1]))
            def mm2(t, tt=tt, b=b):
                pair = [self.psO[:, 0:512], self.psO[:, 512:1024]] if tt % 2 == 0 \
                    else [self.psV[:], self.psR[:]]
                for half in range(2):
                    for ffc in range(FFC):
                        ins = t.matmul(pair[half],
                                       self.hid2[:, b * FFC * 128 + ffc * 128:
                                                 b * FFC * 128 + (ffc + 1) * 128],
                                       self.WB[:, ffc * D_ + half * 512: ffc * D_ + (half + 1) * 512],
                                       start=(ffc == 0), stop=(ffc == FFC - 1))
                return ins
            R("tensor", mm2, waits=wmm, sig=self.pe)
            petok = self.pe.n
            gcol_idx = tt * 8 + e
            def evac(v, tt=tt, e=e):
                pair = [self.psO[:, 0:512], self.psO[:, 512:1024]] if tt % 2 == 0 \
                    else [self.psV[:], self.psR[:]]
                gslice = self.gate_all[:, tt * 8 + e: tt * 8 + e + 1]
                v.tensor_scalar(self.t32[tt % 2][:, 0:512], pair[0], gslice, None, ALU.mult)
                return v.tensor_scalar(self.t32[tt % 2][:, 512:1024], pair[1], gslice, None, ALU.mult)
            wev = [(self.pe, petok)]
            if tt >= 2:
                wev.append((self.gsm, mtok[tt - 2][2]))
            R("vector", evac, waits=wev, sig=self.dve)
            dtok = self.dve.n
            R("gpsimd", lambda g, tt=tt, e=e: g.dma_start(
                self.moed[tt * 128:(tt + 1) * 128, :], self.t32[tt % 2][:],
                accum_op=(ALU.bypass if e == 0 else ALU.add)),
              waits=[(self.dve, dtok)] + ([(self.gsm, self.moeacc[tt])] if e > 0 else []),
              sig=self.gsm, n=16)
            if not hasattr(self, "moeacc"):
                self.moeacc = {}
            self.moeacc[tt] = self.gsm.n
            mtok.append((petok, dtok, self.gsm.n))
        ph.emit()

    def stage_e(self):
        S, TT = self.S, self.TT
        nTT = self.TOK // 128
        ph = Ph(self.nc, "E")
        R = ph.rec
        self.etok = {}
        self.b2tok = {}
        for tt in range(nTT):
            s, tl = tt // TT, tt % TT
            b = tt % 2
            w = [(self.gsm, self.moeacc[tt]), (self.scl, self.scl.n)]
            if tt >= 2:
                w.append((self.dve, self.etok[tt - 2]))
            R("sync", lambda sy, tt=tt, b=b: sy.dma_start(
                self.t32[b][:], self.moed[tt * 128:(tt + 1) * 128, :]),
              waits=w, sig=self.syn, n=16)
            gw_w = [(self.pe, self.b2tok[tt - 2])] if tt >= 2 else []
            R("sync", lambda sy, tt=tt, b=b: sy.dma_start(
                self.grow[b][:], self.gtd[:, tt * 128:(tt + 1) * 128]),
              waits=gw_w, sig=self.syn, n=16)
            ld = self.syn.n
            NEr = self.NE
            wb2 = [(self.syn, ld), self.dve]
            def b2mm(t, b=b, NEr=NEr):
                t.matmul(self.psO[:, 0:512], self.grow[b][0:NEr, :], self.b2bf[0:NEr, 0:512],
                         start=True, stop=True)
                return t.matmul(self.psO[:, 512:1024], self.grow[b][0:NEr, :],
                                self.b2bf[0:NEr, 512:1024], start=True, stop=True)
            R("tensor", b2mm, waits=wb2, sig=self.pe)
            self.b2tok[tt] = self.pe.n
            R("vector", lambda v, b=b: v.tensor_tensor(self.n32[:], self.t32[b][:], self.psO[:], ALU.add),
              waits=[self.pe], sig=self.dve)
            comb_tok = self.dve.n
            R("sync", lambda sy, s=s, tl=tl, b=b: sy.dma_start(
                self.t32[b][:], self.y[s, tl * 128:(tl + 1) * 128, :]),
              waits=[(self.dve, comb_tok), (self.scl, self.y_attn_tok)],
              sig=self.syn, n=16)
            ld2 = self.syn.n
            R("scalar", lambda sc: sc.activation(self.sqscr[:], self.n32[:], AF.Square),
              waits=[self.dve], sig=self.act)
            R("vector", lambda v: v.reduce_sum(self.stat[:, 0:1], self.n32[:], axis=AX),
              ss=True)
            R("vector", lambda v: v.reduce_sum(self.stat[:, 1:2], self.sqscr[:], axis=AX),
              waits=[self.act], sig=self.dve, ss=True)
            c = 1.0 / D
            R("vector", lambda v, c=c: v.tensor_scalar(self.stat[:, 2:3], self.stat[:, 0:1], c, None, ALU.mult),
              ss=True)
            R("vector", lambda v, c=c: v.tensor_scalar(self.stat[:, 3:4], self.stat[:, 1:2], c, None, ALU.mult),
              ss=True)
            R("vector", lambda v: v.tensor_tensor(self.stat[:, 4:5], self.stat[:, 2:3], self.stat[:, 2:3], ALU.mult),
              ss=True)
            R("vector", lambda v: v.tensor_tensor(self.stat[:, 5:6], self.stat[:, 3:4], self.stat[:, 4:5], ALU.subtract),
              sig=self.dve, ss=True)
            R("scalar", lambda sc: sc.activation(self.stat[:, 6:7], self.stat[:, 5:6], AF.Ln, bias=LNEPS),
              waits=[self.dve], ss=True)
            R("scalar", lambda sc: sc.activation(self.stat[:, 6:7], self.stat[:, 6:7], AF.Exp, scale=-0.5),
              sig=self.act, ss=True)
            R("vector", lambda v: v.tensor_tensor(self.stat[:, 7:8], self.stat[:, 2:3], self.stat[:, 6:7], ALU.mult),
              waits=[self.act], ss=True)
            R("vector", lambda v: v.tensor_scalar(self.stat[:, 7:8], self.stat[:, 7:8], -1.0, None, ALU.mult),
              sig=self.dve, ss=True)
            R("vector", lambda v: v.tensor_scalar(self.n32[:], self.n32[:],
                                                  self.stat[:, 6:7], self.stat[:, 7:8],
                                                  ALU.mult, ALU.add),
              sig=self.dve)
            R("vector", lambda v: v.tensor_tensor(self.sqscr[:], self.n32[:], self.g_bc[:], ALU.mult))
            R("vector", lambda v: v.tensor_tensor(self.sqscr[:], self.sqscr[:], self.b_bc[:], ALU.add))
            R("vector", lambda v, b=b: v.tensor_tensor(self.t32[b][:], self.t32[b][:], self.sqscr[:], ALU.add),
              waits=[(self.syn, ld2)], sig=self.dve)
            self.etok[tt] = self.dve.n
            R("scalar", lambda sc, s=s, tl=tl, b=b: sc.dma_start(
                self.y[s, tl * 128:(tl + 1) * 128, :], self.t32[b][:]),
              waits=[self.dve], sig=self.scl, n=16)
        ph.emit()

    def finish(self):
        ph = Ph(self.nc, "fin")
        def fin(sy):
            for ev in self.all_dma:
                if ev.n > 0:
                    sy.wait_ge(ev.h, ev.n)
            return sy.nop()
        ph.rec("sync", fin)
        ph.emit()


def make_in_map(inputs, core):
    x = inputs["L0_attn_residual"]
    return {
        "x": np.ascontiguousarray(x[2 * core: 2 * core + 2]),
        "wk1": inputs["L0_attn_W_key"],
        "wq1": inputs["L0_attn_W_query"],
        "wv1": inputs["L0_attn_W_values"],
        "wo1": inputs["L0_attn_W_output"],
        "bk1": inputs["L0_attn_Bias_key"],
        "bq1": inputs["L0_attn_Bias_query"],
        "bv1": inputs["L0_attn_Bias_values"],
        "bo1": inputs["L0_attn_Bias_output"],
        "lng": inputs["L0_ffn_ln_g"],
        "lnb": inputs["L0_ffn_ln_b"],
        "wq2": inputs["L0_ffn_Wq"],
        "wk2": inputs["L0_ffn_Wk"],
        "wv2": inputs["L0_ffn_Wv"],
        "wo2": inputs["L0_ffn_Wo"],
        "qs2": inputs["L0_ffn_q_scale"],
        "ks2": inputs["L0_ffn_k_scale"],
        "gw": inputs["L0_ffn_gate_w"],
        "gb": inputs["L0_ffn_gate_b"],
        "w1": inputs["L0_ffn_w1"],
        "b1": inputs["L0_ffn_b1"],
        "w2": inputs["L0_ffn_w2"],
        "b2": inputs["L0_ffn_b2"],
    }


def kernel(**inputs):
    b = Build(S=2048, NSEQ=2, NE=8, FF=4096)
    nc = b.build()
    in_maps = [make_in_map(inputs, c) for c in range(NCORES)]
    res = run_bass_kernel_spmd(nc, in_maps, core_ids=list(range(NCORES)))
    out = np.concatenate([r["y"] for r in res.results], axis=0)
    return out.astype(np.float32)


NCORES = 8


# version 10
# speedup vs baseline: 1.1035x; 1.1035x over previous
import numpy as np
import concourse.bass as bass
import concourse.mybir as mybir
from concourse.bass_utils import run_bass_kernel_spmd

F32 = mybir.dt.float32
BF16 = mybir.dt.bfloat16
AF = mybir.ActivationFunctionType
ALU = mybir.AluOpType
AX = mybir.AxisListType.X

D = 1024
H = 16
DH = 64
FC = 8
LNEPS = 1e-5
L2EPS = 1e-12
EPS1 = 1e-10


class Ev:

    def __init__(self, nc, name, engine, dma=False):
        self.h = nc.alloc_semaphore(name)
        self.n = 0
        self.engine = engine
        self.dma = dma


class Ph:
    def __init__(self, nc, name=""):
        self.nc = nc
        self.items = []
        self.name = name

    ENG_EV = {}

    def rec(self, engine, fn, waits=(), sig=None, n=1, ss=False, nop=False):
        if nop:
            fn0 = fn
            if engine == "scalar":
                def fn(E, fn0=fn0):
                    fn0(E)
                    return E.mul(Ph.NOPAP, Ph.NOPAP, 1.0)
            else:
                def fn(E, fn0=fn0):
                    fn0(E)
                    return E.engine_nop()
        wl = []
        for ev in waits:
            if ev is None:
                continue
            if isinstance(ev, tuple):
                if ev[1] > 0:
                    wl.append(ev if len(ev) == 3 else (ev[0], ev[1], False))
            elif ev.n > 0:
                wl.append((ev, ev.n, False))
        if ss and sig is None:
            sig = Ph.ENG_EV[engine]
        if sig is not None:
            assert sig.engine == engine, f"event {sig.h.name} owned by {sig.engine}, signaled from {engine}"
            sig.n += n
        self.items.append((engine, fn, wl, sig, n, ss))

    def emit(self):
        engines = []
        for it in self.items:
            if it[0] not in engines:
                engines.append(it[0])
        sigtgt = {}
        run = {}
        for it in self.items:
            sig = it[3]
            if sig is not None:
                run[sig] = run.get(sig, sig.n - sum(j[4] for j in self.items if j[3] is sig))
        for idx, it in enumerate(self.items):
            sig = it[3]
            if sig is not None:
                run[sig] += it[4]
                sigtgt[idx] = run[sig]
        with self.nc.Block() as blk:
            for ename in engines:
                section = getattr(blk, ename)

                def body(E, ename=ename):
                    waited = {}
                    for idx, (eng, fn, wl, sig, n, ss) in enumerate(self.items):
                        if eng != ename:
                            continue
                        for ev, v, force in wl:
                            if ev.engine == ename and not ev.dma and not force:
                                continue
                            if waited.get(ev, -1) >= v:
                                continue
                            E.wait_ge(ev.h, v)
                            waited[ev] = v
                        ins = fn(E)
                        if sig is not None:
                            ins.then_inc(sig.h, n)
                            if ss:
                                E.wait_ge(sig.h, sigtgt[idx])
                                waited[sig] = max(waited.get(sig, -1), sigtgt[idx])
                section(body)


class Build:
    def __init__(self, S=2048, NSEQ=2, NE=8, FF=4096, dbg=False):
        ik = "ExternalOutput" if dbg else "Internal"
        self.S = S
        self.NSEQ = NSEQ
        self.NE = NE
        self.FF = FF
        self.FFC = FF // 128
        self.TT = S // 128
        self.QC = S // 512
        self.TOK = S * NSEQ
        nc = bass.Bass("TRN2", target_bir_lowering=False)
        self.nc = nc
        S_ = S

        dt = nc.dram_tensor
        self.x = dt("x", [NSEQ, S, D], F32, kind="ExternalInput").ap()
        self.wk1 = dt("wk1", [H, D, DH], F32, kind="ExternalInput").ap()
        self.wq1 = dt("wq1", [H, D, DH], F32, kind="ExternalInput").ap()
        self.wv1 = dt("wv1", [H, D, DH], F32, kind="ExternalInput").ap()
        self.wo1 = dt("wo1", [H, DH, D], F32, kind="ExternalInput").ap()
        self.bk1 = dt("bk1", [H, DH], F32, kind="ExternalInput").ap()
        self.bq1 = dt("bq1", [H, DH], F32, kind="ExternalInput").ap()
        self.bv1 = dt("bv1", [H, DH], F32, kind="ExternalInput").ap()
        self.bo1 = dt("bo1", [D], F32, kind="ExternalInput").ap()
        self.lng = dt("lng", [D], F32, kind="ExternalInput").ap()
        self.lnb = dt("lnb", [D], F32, kind="ExternalInput").ap()
        self.wq2 = dt("wq2", [D, D], F32, kind="ExternalInput").ap()
        self.wk2 = dt("wk2", [D, D], F32, kind="ExternalInput").ap()
        self.wv2 = dt("wv2", [D, D], F32, kind="ExternalInput").ap()
        self.wo2 = dt("wo2", [D, D], F32, kind="ExternalInput").ap()
        self.qs2 = dt("qs2", [H, 1, DH], F32, kind="ExternalInput").ap()
        self.ks2 = dt("ks2", [H, 1, DH], F32, kind="ExternalInput").ap()
        self.gw = dt("gw", [D, 8], F32, kind="ExternalInput").ap()
        self.gb = dt("gb", [8], F32, kind="ExternalInput").ap()
        self.w1 = dt("w1", [NE, D, FF], F32, kind="ExternalInput").ap()
        self.b1 = dt("b1", [NE, FF], F32, kind="ExternalInput").ap()
        self.w2 = dt("w2", [NE, FF, D], F32, kind="ExternalInput").ap()
        self.b2 = dt("b2", [NE, D], F32, kind="ExternalInput").ap()
        self.y = dt("y", [NSEQ, S, D], F32, kind="ExternalOutput").ap()
        self.h2d = dt("h2d", [NSEQ, D, S], BF16, kind=ik).ap()
        self.hidd = dt("hidd", [FF, self.TOK], BF16, kind=ik).ap()
        self.moed = dt("moed", [self.TOK, D], F32, kind=ik).ap()
        self.wfd = dt("wfd", [FC, 128, S], BF16, kind=ik).ap()
        self.gtd = dt("gtd", [8, self.TOK], BF16, kind=ik).ap()
        self.dbg = dbg
        if dbg:
            self.dbgd = dt("dbgd", [8, 128, S], BF16, kind="ExternalOutput").ap()
            self.dbgf = dt("dbgf", [4, 128, 512], F32, kind="ExternalOutput").ap()

        def at(name, shape, dtype):
            return nc.alloc_sbuf_tensor(name, shape, dtype)

        wa_cols = max(FC * self.FF, FC * S_ + self.TT * H * 65)
        self.WA = at("WA", [128, wa_cols], BF16)
        self.xnh = self.WA[:, 0:FC * S_]
        self.Vt = self.WA[:, FC * S_: FC * S_ + self.TT * H * 65]
        wb_cols = max(self.FFC * D, 3 * FC * D + 4 * S_)
        self.WB = at("WB", [128, wb_cols], BF16)
        self.w0 = self.WB[:, 0:FC * D]
        self.w1s = self.WB[:, FC * D:2 * FC * D]
        self.w2s = self.WB[:, 2 * FC * D:3 * FC * D]
        kq0 = 3 * FC * D
        self.kb = [self.WB[0:65, kq0 + i * S_: kq0 + (i + 1) * S_] for i in range(2)]
        self.qb = [self.WB[0:65, kq0 + (2 + i) * S_: kq0 + (3 + i) * S_] for i in range(2)]
        self.Creg = at("Creg", [128, 8192], BF16)
        self.h2cur = [self.Creg[:, i * 4096:(i + 1) * 4096] for i in range(2)]
        self.hid2 = self.Creg
        self.wfb = [self.Creg[:, i * S_:(i + 1) * S_] for i in range(2)]
        self.opj = [self.Creg[:, 2 * S_ + i * 1024: 2 * S_ + (i + 1) * 1024] for i in range(2)]
        self.h2ft = self.Creg[:, 2 * S_ + 2048: 2 * S_ + 3072]
        self.t32 = [at(f"t32_{i}", [128, D], F32) for i in range(2)]
        self.sqscr = at("sqscr", [128, D], F32)
        self.n32 = at("n32", [128, D], F32)
        self.g_bc = at("g_bc", [128, D], BF16)
        self.b_bc = at("b_bc", [128, D], BF16)
        self.bo_bc = at("bo_bc", [128, D], BF16)
        self.bv_bc = at("bv_bc", [128, D], BF16)
        self.gb_bc = at("gb_bc", [128, 8], F32)
        self.TR = at("TR", [65, S_], BF16)
        self.sufB = at("sufB", [65, self.TT * 65], BF16)
        self.lrow = at("lrow", [65, 512], F32)
        self.mk = at("mk", [65, 4], F32)
        self.r0a = at("r0a", [1, 512], F32)
        self.r0b = at("r0b", [1, 512], BF16)
        self.rbc_sb = at("rbc_sb", [64, 512], BF16)
        self.rb32 = at("rb32", [64, 512], F32)
        self.recrow = at("recrow", [65, 512], F32)
        self.sq_sc = at("sq_sc", [64, 512], BF16)
        self.wnorm = at("wnorm", [64, 512], BF16)
        self.dsc = at("dsc", [128, 256], BF16)
        self.U01 = at("U01", [128, 128], BF16)
        self.L01 = at("L01", [128, 128], BF16)
        self.idn = at("idn", [128, 128], BF16)
        self.idnf = at("idnf", [128, 128], F32)
        self.ones_c = at("ones_c", [128, 1], BF16)
        self.onef_c = at("onef_c", [128, 1], F32)
        self.ones_r0 = at("ones_r0", [1, 128], BF16)
        self.onef_r64 = at("onef_r64", [65, 64], F32)
        self.onef_r = at("onef_r", [65, 128], F32)
        self.bkc = at("bkc", [64, H], F32)
        self.bqc = at("bqc", [64, H], F32)
        self.qsc = at("qsc", [64, H], F32)
        self.ksc = at("ksc", [64, H], F32)
        self.gcol = at("gcol", [128, FC], F32)
        self.bcol = at("bcol", [128, FC], F32)
        self.b1c = [at(f"b1c_{i}", [128, 32], F32) for i in range(2)]
        self.b2bf = at("b2bf", [8, D], BF16)
        self.gwc = at("gwc", [128, FC * 8], BF16)
        self.stat = at("stat", [128, 8], F32)
        self.gsc = at("gsc", [128, 8], F32)
        self.gsc2 = at("gsc2", [128, 8], F32)
        self.gb16 = at("gb16", [128, 8], BF16)
        self.grow = [at(f"grow_{i}", [8, 128], BF16) for i in range(2)]
        self.gate_all = at("gate_all", [128, NSEQ * self.TT * 8], F32)
        self.hsb = [at(f"hsb_{i}", [128, 512], BF16) for i in range(3)]
        self.xb16 = at("xb16", [128, D], BF16)
        self.rowtmp = at("rowtmp", [128, 256], F32)
        self.wst = [at(f"wst_{i}", [128, 512], F32) for i in range(3)]
        self.wstF = at("wstF", [128, 512], F32)
        self.npad = at("npad", [1, 1], F32)
        Ph.NOPAP = self.npad[0:1, 0:1]
        self.cst = at("cst", [128, 4], F32)
        for i, vv in enumerate([LNEPS, L2EPS, 1e-30, EPS1 / 8]):
            nc.const_aps.register(F32, vv, self.cst[:, i:i + 1])

        ap_ = nc.alloc_psum_tensor
        self.psP = [ap_(f"psP{i}", [128, 512], F32) for i in range(2)]
        self.psS = [ap_(f"psS{i}", [128, 512], F32) for i in range(2)]
        self.psV = ap_("psV", [128, 512], F32)
        self.psR = ap_("psR", [128, 512], F32)
        self.psO = ap_("psO", [128, 1024], F32)
        self.psOb = self.psO[:].bitcast(BF16)

        self.pe = Ev(nc, "pe", "tensor")
        self.pe2 = Ev(nc, "pe2", "tensor")
        self.act = Ev(nc, "act", "scalar")
        self.act2 = Ev(nc, "act2", "scalar")
        self.dve = Ev(nc, "dve", "vector")
        self.dve2 = Ev(nc, "dve2", "vector")
        self.pool = Ev(nc, "pool", "gpsimd")
        self.syn = Ev(nc, "syn", "sync", dma=True)
        self.scl = Ev(nc, "scl", "scalar", dma=True)
        self.gps = Ev(nc, "gps", "gpsimd", dma=True)
        self.gsm = Ev(nc, "gsm", "gpsimd", dma=True)
        self.gsh = Ev(nc, "gsh", "gpsimd", dma=True)
        self.all_dma = [self.syn, self.scl, self.gps, self.gsm, self.gsh]
        Ph.ENG_EV = {"vector": self.dve, "scalar": self.act,
                     "tensor": self.pe, "gpsimd": self.pool}

    def stage_w(self, R, dst, src, cols, part=128):
        if not hasattr(self, "wst_last"):
            self.wst_last = {0: 0, 1: 0, 2: 0}
            self._convq = []
        multi = len(src.shape) > 2
        inner = int(np.prod(src.shape[2:])) if multi else 1
        if multi:
            assert 512 % inner == 0 and cols == src.shape[1] * inner
        pieces = list(range(0, cols, 512))
        for a in pieces:
            c = min(512, cols - a)
            b = getattr(self, "wst_rr", 0)
            self.wst_rr = (b + 1) % 3
            if any(j[1] == b for j in self._convq):
                self.stage_fence(R)
            def dmafn(sy, b=b, a=a, c=c, src=src, part=part, multi=multi, inner=inner):
                if multi:
                    return sy.dma_start(
                        self.wst[b][0:part, 0:c].rearrange(
                            "p (x y) -> p x y", y=inner),
                        src[0:part, a // inner:(a + c) // inner])
                return sy.dma_start(self.wst[b][0:part, 0:c], src[0:part, a:a + c])
            R("sync", dmafn, waits=[(self.dve, self.wst_last[b])], sig=self.syn, n=16)
            tok = self.syn.n
            if part == 128 and c == 512:
                self._flush_convs(R, tok)
            self._convq.append((tok, b, dst, a, c, part))
        return None

    def _flush_convs(self, R, tok_f):
        rem = []
        for (tok_p, b, dst, a, c, part) in self._convq:
            if tok_p < tok_f:
                R("vector", lambda v, b=b, a=a, c=c, dst=dst, part=part:
                  v.tensor_copy(dst[0:part, a:a + c], self.wst[b][0:part, 0:c]),
                  waits=[(self.syn, tok_f)], sig=self.dve, ss=True)
                self.wst_last[b] = self.dve.n
            else:
                rem.append((tok_p, b, dst, a, c, part))
        self._convq = rem

    def stage_fence(self, R):
        if not getattr(self, "_convq", None):
            return self.dve.n
        R("sync", lambda sy: sy.dma_start(self.wstF[:], self.x[0, 0:128, 0:512]),
          sig=self.syn, n=16)
        self._flush_convs(R, self.syn.n)
        R("vector", lambda v: v.engine_nop(), sig=self.dve)
        return self.dve.n

    def vsl(self, t, h, n=65):
        return self.Vt[:, t * (H * 65) + 65 * h: t * (H * 65) + 65 * h + n]

    def build(self, upto="end", a2mode="full"):
        self.a2mode = a2mode
        self.setup()
        for s in range(self.NSEQ):
            self.stage_a(s)
            if upto == "A":
                continue
            self.attn(s, 1)
            if upto == "attn1":
                continue
            self.attn(s, 2)
        if upto in ("A", "attn1", "attn2"):
            self.finish()
            return self.nc
        for e in range(self.NE):
            self.moe_phase1(e)
            self.moe_phase2(e)
        self.stage_e()
        self.finish()
        return self.nc

    def setup(self):
        S = self.S
        ph = Ph(self.nc, "setup")
        R = ph.rec
        def mk_msets(g):
            g.memset(self.idn[:], 0.0)
            g.memset(self.idnf[:], 0.0)
            g.memset(self.U01[:], 0.0)
            return g.memset(self.L01[:], 1.0)
        R("gpsimd", mk_msets, sig=self.pool, ss=True)
        def mk_asels(g):
            g.affine_select(out=self.idn[:], in_=self.idn[:], compare_op=ALU.not_equal,
                            fill=1.0, base=0, pattern=[[-1, 128]], channel_multiplier=1)
            g.affine_select(out=self.idnf[:], in_=self.idnf[:], compare_op=ALU.not_equal,
                            fill=1.0, base=0, pattern=[[-1, 128]], channel_multiplier=1)
            g.affine_select(out=self.U01[:], in_=self.U01[:], compare_op=ALU.is_gt,
                            fill=1.0, base=0, pattern=[[-1, 128]], channel_multiplier=1)
            return g.affine_select(out=self.L01[:], in_=self.L01[:], compare_op=ALU.is_gt,
                                   fill=0.0, base=0, pattern=[[-1, 128]], channel_multiplier=1)
        R("gpsimd", mk_asels, sig=self.pool, ss=True)
        def mk_masks(g):
            g.memset(self.ones_c[:], 1.0)
            g.memset(self.onef_c[:], 1.0)
            g.memset(self.ones_r0[:], 1.0)
            g.memset(self.onef_r64[64:65, :], 1.0)
            g.memset(self.onef_r[:], 1.0)
            for i, vv in enumerate([LNEPS, L2EPS, 1e-30, EPS1 / 8]):
                g.memset(self.cst[:, i:i + 1], vv)
            g.memset(self.kb[0][64:65, :], 1.0)
            g.memset(self.kb[1][64:65, :], 1.0)
            ins = g.memset(
                self.Vt.rearrange("p (t h c) -> p t h c", h=H, c=65)[:, :, :, 64:65], 1.0)
            return ins
        R("gpsimd", mk_masks, sig=self.pool)

        for fc in range(FC):
            self.stage_w(R, self.w0[:, fc * D:(fc + 1) * D],
                         self.wk1.rearrange("n d h -> d (n h)")[fc * 128:(fc + 1) * 128]
                         if False else
                         self.wk1.rearrange("n d h -> d n h")[fc * 128:(fc + 1) * 128], D)
            self.stage_w(R, self.w1s[:, fc * D:(fc + 1) * D],
                         self.wq1.rearrange("n d h -> d n h")[fc * 128:(fc + 1) * 128], D)
            self.stage_w(R, self.w2s[:, fc * D:(fc + 1) * D],
                         self.wv1.rearrange("n d h -> d n h")[fc * 128:(fc + 1) * 128], D)

        self.stage_w(R, self.b2bf, self.b2, D, part=self.NE)
        for fc in range(FC):
            self.stage_w(R, self.gwc[:, fc * 8:(fc + 1) * 8],
                         self.gw[fc * 128:(fc + 1) * 128, :], 8)

        def rows(sy):
            sy.dma_start(self.t32[0][0:H, 0:DH], self.bk1).then_inc(self.syn.h, 16)
            sy.dma_start(self.t32[0][0:H, 64:64 + DH], self.bq1).then_inc(self.syn.h, 16)
            sy.dma_start(self.t32[0][0:H, 128:128 + DH],
                         self.qs2.rearrange("h o d -> h (o d)")).then_inc(self.syn.h, 16)
            sy.dma_start(self.t32[0][0:H, 192:192 + DH],
                         self.ks2.rearrange("h o d -> h (o d)")).then_inc(self.syn.h, 16)
            sy.dma_start(self.t32[0][32:33, 0:D], self.lng[None, :]).then_inc(self.syn.h, 16)
            sy.dma_start(self.t32[1][0:1, 0:D], self.lnb[None, :]).then_inc(self.syn.h, 16)
            sy.dma_start(self.sqscr[0:1, 0:D], self.bo1[None, :]).then_inc(self.syn.h, 16)
            sy.dma_start(self.sqscr[32:33, 0:D],
                         self.bv1.rearrange("h d -> (h d)")[None, :]).then_inc(self.syn.h, 16)
            ins = sy.dma_start(self.sqscr[64:65, 0:8], self.gb[None, :])
            return ins
        self.syn.n += 8 * 16
        R("sync", rows, sig=self.syn, n=16)

        self.stage_w(R, self.xb16[:, 0:128],
                     self.lng.rearrange("(a b) -> a b", b=128), 128, part=FC)
        self.stage_w(R, self.xb16[:, 128:256],
                     self.lnb.rearrange("(a b) -> a b", b=128), 128, part=FC)
        self.lnchunk_tok = self.stage_fence(R)
        ph.emit()

        ph = Ph(self.nc, "setup2")
        R = ph.rec
        R("tensor", lambda t: t.transpose(self.psR[0:DH, 0:128], self.t32[0][:, 0:DH], self.idnf[:]),
          waits=[self.syn], sig=self.pe)
        R("tensor", lambda t: t.transpose(self.psR[0:DH, 128:256],
                                          self.t32[0][:, 64:64 + DH], self.idnf[:]), sig=self.pe)
        R("tensor", lambda t: t.transpose(self.psR[0:DH, 256:384],
                                          self.t32[0][:, 128:128 + DH], self.idnf[:]), sig=self.pe)
        R("tensor", lambda t: t.transpose(self.psR[0:DH, 384:512],
                                          self.t32[0][:, 192:192 + DH], self.idnf[:]), sig=self.pe)
        R("tensor", lambda t: t.transpose(self.psV[:].bitcast(BF16)[:, 0:128],
                                          self.xb16[:, 0:128], self.idn[:]),
          waits=[self.dve], sig=self.pe)
        R("tensor", lambda t: t.transpose(self.psV[:].bitcast(BF16)[:, 128:256],
                                          self.xb16[:, 128:256], self.idn[:]), sig=self.pe)
        def bc(t, dst, src_row, base):
            return t.matmul(dst, self.onef_r[base:base + 1, :], src_row,
                            start=True, stop=True, skip_group_check=True)
        R("tensor", lambda t: bc(t, self.psO[:, 0:512], self.t32[0][32:33, 0:512], 32), sig=self.pe)
        R("tensor", lambda t: bc(t, self.psO[:, 512:1024], self.t32[0][32:33, 512:1024], 32), sig=self.pe)
        R("tensor", lambda t: bc(t, self.psP[0][:, 0:512], self.t32[1][0:1, 0:512], 0), sig=self.pe)
        R("tensor", lambda t: bc(t, self.psP[1][:, 0:512], self.t32[1][0:1, 512:1024], 0), sig=self.pe)
        R("tensor", lambda t: bc(t, self.psS[0][:, 0:512], self.sqscr[0:1, 0:512], 0), sig=self.pe)
        R("tensor", lambda t: bc(t, self.psS[1][:, 0:512], self.sqscr[0:1, 512:1024], 0), sig=self.pe)
        R("tensor", lambda t: bc(t, self.psV[:, 200:208], self.sqscr[64:65, 0:8], 64), sig=self.pe)

        def evacs(v):
            v.tensor_copy(self.bkc[:, :], self.psR[0:DH, 0:H])
            v.tensor_copy(self.bqc[:, :], self.psR[0:DH, 128:128 + H])
            v.tensor_copy(self.qsc[:, :], self.psR[0:DH, 256:256 + H])
            v.tensor_copy(self.ksc[:, :], self.psR[0:DH, 384:384 + H])
            v.tensor_copy(self.gcol[:, :], self.psV[:].bitcast(BF16)[:, 0:FC])
            v.tensor_copy(self.bcol[:, :], self.psV[:].bitcast(BF16)[:, 128:128 + FC])
            v.tensor_copy(self.g_bc[:, :], self.psO[:, :])
            v.tensor_copy(self.b_bc[:, 0:512], self.psP[0][:, :])
            v.tensor_copy(self.b_bc[:, 512:1024], self.psP[1][:, :])
            v.tensor_copy(self.bo_bc[:, 0:512], self.psS[0][:, :])
            v.tensor_copy(self.bo_bc[:, 512:1024], self.psS[1][:, :])
            return v.tensor_copy(self.gb_bc[:, :], self.psV[:, 200:208])
        R("vector", evacs, waits=[self.pe], sig=self.dve)
        R("tensor", lambda t: bc(t, self.psR[:, 0:512], self.sqscr[32:33, 0:512], 32),
          waits=[self.dve], sig=self.pe)
        R("vector", lambda v: v.tensor_copy(self.bv_bc[:, 0:512], self.psR[:, 0:512]),
          waits=[self.pe], sig=self.dve)
        R("tensor", lambda t: bc(t, self.psR[:, 0:512], self.sqscr[32:33, 512:1024], 32),
          waits=[self.dve], sig=self.pe)
        R("vector", lambda v: v.tensor_copy(self.bv_bc[:, 512:1024], self.psR[:, 0:512]),
          waits=[self.pe], sig=self.dve)
        ph.emit()

    def stage_a(self, s):
        S, TT = self.S, self.TT
        ph = Ph(self.nc, f"A{s}")
        R = ph.rec
        xtok = []
        evtok = []
        for tt in range(TT):
            b = tt % 2
            w = [(self.dve2, evtok[tt - 2][0])] if tt >= 2 else [(self.scl, self.scl.n)]
            R("sync", lambda sy, s=s, tt=tt, b=b:
              sy.dma_start(self.t32[b][:], self.x[s, tt * 128:(tt + 1) * 128, :]),
              waits=w, sig=self.syn, n=16)
            xtok.append(self.syn.n)
            R("scalar", lambda sc, b=b: sc.activation(self.sqscr[:], self.t32[b][:], AF.Square),
              waits=[(self.syn, xtok[tt]), (self.dve, self.dve.n)], sig=self.act)
            R("vector", lambda v, b=b: v.reduce_sum(self.stat[:, 0:1], self.t32[b][:], axis=AX),
              waits=[(self.syn, xtok[tt])], sig=self.dve, ss=True)
            R("vector", lambda v: v.reduce_sum(self.stat[:, 1:2], self.sqscr[:], axis=AX),
              waits=[self.act], sig=self.dve, ss=True)
            c = 1.0 / D
            R("vector", lambda v, c=c: v.tensor_scalar(self.stat[:, 2:3], self.stat[:, 0:1], c, None, ALU.mult),
              ss=True)
            R("vector", lambda v, c=c: v.tensor_scalar(self.stat[:, 3:4], self.stat[:, 1:2], c, None, ALU.mult),
              ss=True)
            R("vector", lambda v: v.tensor_tensor(self.stat[:, 4:5], self.stat[:, 2:3], self.stat[:, 2:3], ALU.mult),
              ss=True)
            R("vector", lambda v: v.tensor_tensor(self.stat[:, 5:6], self.stat[:, 3:4], self.stat[:, 4:5], ALU.subtract),
              sig=self.dve, ss=True)
            R("scalar", lambda sc: sc.activation(self.stat[:, 6:7], self.stat[:, 5:6], AF.Ln, bias=LNEPS),
              waits=[self.dve], ss=True)
            R("scalar", lambda sc: sc.activation(self.stat[:, 6:7], self.stat[:, 6:7], AF.Exp, scale=-0.5),
              sig=self.act, ss=True, nop=True)
            R("vector", lambda v: v.tensor_tensor(self.stat[:, 7:8], self.stat[:, 2:3], self.stat[:, 6:7], ALU.mult),
              waits=[self.act], ss=True)
            R("vector", lambda v: v.tensor_scalar(self.stat[:, 7:8], self.stat[:, 7:8], -1.0, None, ALU.mult),
              sig=self.dve, ss=True)
            if getattr(self, "a_noln", False):
                R("vector", lambda v, b=b: v.tensor_copy(self.xb16[:], self.t32[b][:]),
                  waits=[self.pe, self.act], sig=self.dve2)
            else:
                R("vector", lambda v, b=b: v.tensor_scalar(
                    self.xb16[:], self.t32[b][:], self.stat[:, 6:7], self.stat[:, 7:8],
                    ALU.mult, ALU.add),
                  waits=[self.pe, self.act], sig=self.dve2, ss=True, nop=True)
            evtok.append((self.dve2.n, self.dve.n))
            def trs(t, tt=tt):
                for fc in range(FC):
                    ins = t.transpose(self.psOb[:, fc * 128:(fc + 1) * 128],
                                      self.xb16[:, fc * 128:(fc + 1) * 128], self.idn[:])
                return ins
            R("tensor", trs, waits=[self.dve2], sig=self.pe)
            R("vector", lambda v, tt=tt: v.tensor_copy(
                self.xnh.rearrange("p (fc s) -> p fc s", fc=FC)[:, :, tt * 128:(tt + 1) * 128],
                self.psOb[:, 0:FC * 128].rearrange("p (fc j) -> p fc j", fc=FC)),
              waits=[self.pe], sig=self.dve2)
        if getattr(self, "dbg", False) and s == 0:
            R("gpsimd", lambda g: g.dma_start(self.dbgd[7], self.xnh[:, 0:self.S]),
              waits=[self.dve2], sig=self.gsh, n=16)
        ph.emit()

    def attn(self, s, kind):
        a1 = kind == 1
        S, TT, QC = self.S, self.TT, self.QC
        nc = self.nc

        if not a1:
            ph = Ph(nc, f"hconv{s}")
            def hconv(v):
                for fc in range(FC):
                    ins = v.tensor_scalar(
                        self.xnh[:, fc * S:(fc + 1) * S], self.xnh[:, fc * S:(fc + 1) * S],
                        self.gcol[:, fc:fc + 1], self.bcol[:, fc:fc + 1], ALU.mult, ALU.add)
                return ins
            ph.rec("vector", hconv, sig=self.dve)
            for fc in range(FC):
                self.stage_w(ph.rec, self.w0[:, fc * D:(fc + 1) * D],
                             self.wk2[fc * 128:(fc + 1) * 128, :], D)
                self.stage_w(ph.rec, self.w1s[:, fc * D:(fc + 1) * D],
                             self.wq2[fc * 128:(fc + 1) * 128, :], D)
                self.stage_w(ph.rec, self.w2s[:, fc * D:(fc + 1) * D],
                             self.wv2[fc * 128:(fc + 1) * 128, :], D)
            self.stage_fence(ph.rec)
            ph.emit()
            if getattr(self, "dbg_stall", False):
                with nc.Block() as blk:
                    @blk.gpsimd
                    def _(g):
                        for _i in range(40):
                            g.memset(self.t32[0][:], 0.0)
        elif s == 1:
            ph = Ph(nc, "w1reload")
            for fc in range(FC):
                self.stage_w(ph.rec, self.w0[:, fc * D:(fc + 1) * D],
                             self.wk1.rearrange("n d h -> d n h")[fc * 128:(fc + 1) * 128], D)
                self.stage_w(ph.rec, self.w1s[:, fc * D:(fc + 1) * D],
                             self.wq1.rearrange("n d h -> d n h")[fc * 128:(fc + 1) * 128], D)
                self.stage_w(ph.rec, self.w2s[:, fc * D:(fc + 1) * D],
                             self.wv1.rearrange("n d h -> d n h")[fc * 128:(fc + 1) * 128], D)
            self.stage_fence(ph.rec)
            ph.emit()
        wready = self.dve.n

        ph = Ph(nc, f"attn{kind}_{s}")
        R = ph.rec

        vev = []
        for tt in range(TT):
            pp = [self.psP[tt % 2], self.psS[tt % 2]]
            w = [(self.dve, wready)]
            if tt >= 2:
                w.append((self.dve, vev[tt - 2]))
            def vproj(t, tt=tt, pp=pp):
                for half in range(2):
                    for fc in range(FC):
                        ins = t.matmul(
                            pp[half][:],
                            self.xnh[:, fc * S + tt * 128: fc * S + (tt + 1) * 128],
                            self.w2s[:, fc * D + half * 512: fc * D + (half + 1) * 512],
                            start=(fc == 0), stop=(fc == FC - 1))
                return ins
            R("tensor", vproj, waits=w, sig=self.pe)
            def vevac(v, tt=tt, pp=pp, a1=a1):
                for h in range(H):
                    src = pp[h // 8][:, (h % 8) * 64:(h % 8) * 64 + 64]
                    dst = self.vsl(tt, h, 64)
                    if a1:
                        ins = v.tensor_tensor(dst, src, self.bv_bc[:, h * 64:(h + 1) * 64], ALU.add)
                    else:
                        ins = v.tensor_copy(dst, src)
                return ins
            R("vector", vevac, waits=[self.pe], sig=self.dve)
            vev.append(self.dve.n)
        vdone_dve = self.dve.n

        first = True
        for fc in range(FC):
            if a1:
                src = self.wo1.rearrange("h d e -> (h d) e")[fc * 128:(fc + 1) * 128, :]
            else:
                src = self.wo2[fc * 128:(fc + 1) * 128, :]
            if first:
                R("vector", lambda v: v.engine_nop(), waits=[(self.dve, vdone_dve, True)],
                  sig=self.dve)
                for bb in range(3):
                    self.wst_last[bb] = self.dve.n
                first = False
            self.stage_w(R, self.w2s[:, fc * D:(fc + 1) * D], src, D)
        self.stage_fence(R)
        wo_ready = self.dve.n

        self.vdone_dve = vdone_dve
        self.wf_guard = self.scl.n
        wf_scl = []
        nheads = 0 if (not a1 and getattr(self, "a2mode", "full") == "nohead") else H
        if not a1 and getattr(self, "dbg_nheads", None):
            nheads = self.dbg_nheads
        self.hstate = {}
        for h in range(nheads):
            if h == 0:
                self.rec_prep(R, s, kind, 0)
            if h + 1 < nheads:
                self.rec_prep(R, s, kind, h + 1)
            self.rec_main(R, s, kind, h, wf_scl)

        ph.emit()

        if a1 or getattr(self, "a2mode", "full") not in ("noout", "nohead"):
            self.attn_out(s, kind, wo_ready, wf_scl)
        self.y_attn_tok = self.scl.n

    def rec_prep(self, R, s, kind, h):
        a1 = kind == 1
        S, TT, QC = self.S, self.TT, self.QC
        i = h % 2
        kb, qb = self.kb[i], self.qb[i]
        tr = self.TR if i == 0 else self.t32[0][:].bitcast(BF16)
        if not hasattr(self, "headuse"):
            self.headuse = {}
        guard = []
        if i in self.headuse:
            guard = [(self.pe, self.headuse[i])]

        if getattr(self, "dbg", False) and kind == 2 and h == 0 and getattr(self, "dbg_wdump", False):
            R("vector", lambda v: v.tensor_copy(
                self.Creg[:, 4096:4096 + 8 * S], self.xnh[:, 0:8 * S]),
              waits=[(self.dve, self.vdone_dve)], sig=self.dve, ss=True)

        if getattr(self, "dbg_projonly", False) and kind == 2 and h == 0:
            for side, (wslot, psp) in enumerate([(self.w0, self.psP[0]), (self.w1s, self.psP[1])]):
                def projX(t, wslot=wslot, psp=psp, h=h):
                    for fc in range(FC):
                        ins = t.matmul(
                            psp[0:64, :],
                            wslot[:, fc * D + h * 64: fc * D + h * 64 + 64],
                            self.xnh[:, fc * S: fc * S + 512],
                            start=(fc == 0), stop=(fc == FC - 1))
                    return ins
                wX = ([self.dve2, (self.dve, self.dve.n)]
                      if getattr(self, "dbg_wowait", False)
                      else [self.dve2, (self.dve, self.vdone_dve)])
                R("tensor", projX, waits=wX, sig=self.pe)
            R("vector", lambda v: v.tensor_copy(self.n32[0:64, 0:512], self.psP[0][0:64, :]),
              waits=[self.pe], sig=self.dve, ss=True)
            R("vector", lambda v: v.tensor_copy(self.n32[0:64, 512:1024], self.psP[1][0:64, :]),
              sig=self.dve, ss=True)
            R("sync", lambda sy: sy.dma_start(self.dbgf[0, 0:64, :], self.n32[0:64, 0:512]),
              waits=[self.dve], sig=self.syn, n=16)
            R("sync", lambda sy: sy.dma_start(self.dbgf[1, 0:64, :], self.n32[0:64, 512:1024]),
              waits=[self.dve], sig=self.syn, n=16)
            if getattr(self, "dbg_slivers", True):
                for fc in range(FC):
                    R("sync", lambda sy, fc=fc: sy.dma_start(
                        self.dbgd[0, :, fc * 64:(fc + 1) * 64],
                        self.w0[:, fc * D + h * 64: fc * D + h * 64 + 64]),
                      waits=[self.dve], sig=self.syn, n=16)
                    R("sync", lambda sy, fc=fc: sy.dma_start(
                        self.dbgd[1, :, fc * 64:(fc + 1) * 64],
                        self.w1s[:, fc * D + h * 64: fc * D + h * 64 + 64]),
                      waits=[self.dve], sig=self.syn, n=16)
            return

        for side, (wslot, psp, bias_c, sc_c) in enumerate(
                [(self.w0, self.psP[0], self.bkc, self.ksc),
                 (self.w1s, self.psP[1], self.bqc, self.qsc)]):
            dstb = kb if side == 0 else qb
            for qc in range(QC):
                def proj(t, wslot=wslot, psp=psp, qc=qc, h=h):
                    for fc in range(FC):
                        ins = t.matmul(
                            psp[0:64, :],
                            wslot[:, fc * D + h * 64: fc * D + h * 64 + 64],
                            self.xnh[:, fc * S + qc * 512: fc * S + (qc + 1) * 512],
                            start=(fc == 0), stop=(fc == FC - 1))
                    return ins
                R("tensor", proj,
                  waits=guard + [self.dve2, (self.dve, self.vdone_dve)], sig=self.pe)
                if a1:
                    R("vector", lambda v, psp=psp, dstb=dstb, qc=qc, bias_c=bias_c, h=h:
                      v.tensor_scalar(dstb[0:64, qc * 512:(qc + 1) * 512], psp[0:64, :],
                                      bias_c[:, h:h + 1], None, ALU.add),
                      waits=[self.pe], sig=self.dve2, ss=True, nop=True)
                else:
                    R("scalar", lambda sc, psp=psp: sc.activation(
                        self.sq_sc[0:64, :], psp[0:64, :], AF.Square),
                      waits=[self.pe, self.pe2], sig=self.act)
                    R("tensor", lambda t: t.matmul(
                        self.psR[0:1, :], self.ones_c[0:64, 0:1], self.sq_sc[0:64, :],
                        start=True, stop=True, skip_group_check=True),
                      waits=[self.act, self.dve2], sig=self.pe2)
                    R("scalar", lambda sc: sc.activation(self.r0a[:], self.psR[0:1, :],
                                                         AF.Ln, bias=L2EPS),
                      waits=[self.pe2], sig=self.act, ss=True)
                    R("scalar", lambda sc: sc.activation(self.r0b[:], self.r0a[:],
                                                         AF.Exp, scale=-0.5),
                      sig=self.act, ss=True, nop=True)
                    R("tensor", lambda t: t.matmul(
                        self.psR[0:64, :], self.ones_r0[0:1, 0:64], self.r0b[:],
                        start=True, stop=True, skip_group_check=True),
                      waits=[self.act], sig=self.pe2)
                    R("vector", lambda v: v.tensor_copy(self.rbc_sb[:], self.psR[0:64, :]),
                      waits=[self.pe2], sig=self.dve, ss=True)
                    R("vector", lambda v, psp=psp, dstb=dstb, qc=qc, sc_c=sc_c, h=h:
                      v.scalar_tensor_tensor(
                          dstb[0:64, qc * 512:(qc + 1) * 512], psp[0:64, :],
                          sc_c[:, h:h + 1], self.rbc_sb[:], ALU.mult, ALU.mult),
                      sig=self.dve2, ss=True, nop=True)

        ksrc = kb
        for qc in range(QC):
            R("scalar", lambda sc, qc=qc: sc.activation(
                self.sq_sc[0:64, :], ksrc[0:64, qc * 512:(qc + 1) * 512], AF.Square),
              waits=[self.dve2, self.pe2], sig=self.act)
            R("tensor", lambda t: t.matmul(
                self.psR[64:65, :], self.ones_c[0:64, 0:1], self.sq_sc[0:64, :],
                start=True, stop=True, skip_group_check=True),
              waits=[self.act, self.dve], sig=self.pe2)
            if qc == 0:
                R("vector", lambda v: v.reduce_max(self.mk[64:65, 0:1], self.psR[64:65, :], axis=AX),
                  waits=[self.pe2], sig=self.dve, ss=True)
            else:
                R("vector", lambda v: v.reduce_max(self.mk[64:65, 2:3], self.psR[64:65, :], axis=AX),
                  waits=[self.pe2], sig=self.dve, ss=True)
                R("vector", lambda v: v.tensor_tensor(self.mk[64:65, 0:1], self.mk[64:65, 0:1],
                                                      self.mk[64:65, 2:3], ALU.max),
                  sig=self.dve, ss=True, nop=True)
        R("scalar", lambda sc: sc.activation(self.mk[64:65, 1:2], self.mk[64:65, 0:1],
                                             AF.Ln, bias=1e-30), waits=[self.dve], ss=True)
        R("scalar", lambda sc: sc.mul(self.mk[64:65, 1:2], self.mk[64:65, 1:2], 0.5),
          sig=self.act, ss=True)
        for qc in range(QC):
            R("scalar", lambda sc, qc=qc: sc.activation(
                self.sq_sc[0:64, :], qb[0:64, qc * 512:(qc + 1) * 512], AF.Square),
              waits=[self.pe2], sig=self.act)
            R("tensor", lambda t: t.matmul(
                self.psR[64:65, :], self.ones_c[0:64, 0:1], self.sq_sc[0:64, :],
                start=True, stop=True, skip_group_check=True),
              waits=[self.act], sig=self.pe2)
            R("scalar", lambda sc: sc.activation(self.lrow[64:65, :], self.psR[64:65, :],
                                                 AF.Ln, bias=1e-30),
              waits=[self.pe2], sig=self.act, ss=True)
            R("scalar", lambda sc: sc.activation(
                self.lrow[64:65, :], self.lrow[64:65, :], AF.Exp,
                scale=0.5, bias=self.mk[64:65, 1:2]), sig=self.act, ss=True)
            R("scalar", lambda sc, qc=qc: sc.mul(
                qb[64:65, qc * 512:(qc + 1) * 512], self.lrow[64:65, :], -1.0),
              sig=self.act2, ss=True, nop=True)
            if a1:
                R("scalar", lambda sc, qc=qc, tr=tr: sc.activation(
                    tr[64:65, qc * 512:(qc + 1) * 512],
                    qb[64:65, qc * 512:(qc + 1) * 512],
                    AF.Exp, scale=0.125, bias=EPS1 / 8), sig=self.act2, ss=True, nop=True)
        brow_tok = (self.act2.n, self.pe2.n)

        suf_dve = 0
        if a1:
            sufb = self.sufB if i == 0 else self.t32[1][:].bitcast(BF16)
            def suf_slot(t):
                if t < 7:
                    return self.psR[64:65, t * 65:(t + 1) * 65]
                if t < 14:
                    return self.psO[64:65, (t - 7) * 65:(t - 6) * 65]
                return self.psO[64:65, 512:577]
            def sufmms(t):
                for ts in range(TT - 1):
                    for tp in range(ts + 1, TT):
                        ins = t.matmul(suf_slot(ts), self.ones_c[:, 0:1],
                                       self.vsl(tp, h), start=(tp == ts + 1),
                                       stop=(tp == TT - 1), skip_group_check=True)
                return ins
            R("tensor", sufmms, waits=[self.act, self.dve], sig=self.pe2)
            def sufevac(v, sufb=sufb):
                n1 = min(TT - 1, 7)
                v.tensor_copy(sufb[64:65, 0:n1 * 65], self.psR[64:65, 0:n1 * 65])
                if TT - 1 > 7:
                    n2 = min(TT - 1, 14) - 7
                    v.tensor_copy(sufb[64:65, 455:455 + n2 * 65],
                                  self.psO[64:65, 0:n2 * 65])
                if TT - 1 > 14:
                    v.tensor_copy(sufb[64:65, 910:975], self.psO[64:65, 512:577])
                return v.memset(sufb[64:65, (TT - 1) * 65:TT * 65], 0.0)
            R("vector", sufevac, waits=[self.pe2], sig=self.dve, ss=True, nop=True)
            suf_dve = self.dve.n
        self.hstate[h] = (brow_tok, suf_dve)

    def rec_main(self, R, s, kind, h, wf_scl):
        a1 = kind == 1
        S, TT, QC = self.S, self.TT, self.QC
        i = h % 2
        if h not in self.hstate:
            return
        kb, qb = self.kb[i], self.qb[i]
        tr = self.TR if i == 0 else self.t32[0][:].bitcast(BF16)
        sufb = self.sufB if i == 0 else self.t32[1][:].bitcast(BF16)
        brow_tok, suf_dve = self.hstate[h]

        scale = 0.125 if a1 else 10.0
        np_ = 0
        if not hasattr(self, "psS_last"):
            self.psS_last = {}
        if not hasattr(self, "pt_last"):
            self.pt_last = {}
        if not hasattr(self, "dsc_par"):
            self.dsc_par = 0
        for qc in range(QC):
            NT = 4 * qc + 4
            meta = []
            for kt in range(NT):
                j = kt - 4 * qc
                col0 = max(0, j * 128)
                meta.append((kt, j, col0, 512 - col0, kt % 2, np_ % 3))
                np_ += 1
            exp_tok = [0] * NT
            fix_tok = [0] * NT

            def rec_S(i, qc=qc, meta=meta, exp_tok=exp_tok):
                kt, j, col0, ncol, b, r = meta[i]
                w = [(self.act2, brow_tok[0])]
                if b in self.psS_last:
                    w.append((self.act2, self.psS_last[b]))
                def smm(t, kt=kt, b=b, qc=qc, col0=col0, ncol=ncol):
                    return t.matmul(
                        self.psS[b][:, col0:col0 + ncol],
                        kb[0:65, kt * 128:(kt + 1) * 128],
                        qb[0:65, qc * 512 + col0: qc * 512 + col0 + ncol],
                        start=True, stop=True)
                R("tensor", smm, waits=w, sig=self.pe)
                we = [self.pe]
                if r in self.pt_last:
                    we.append((self.pe, self.pt_last[r]))
                R("scalar", lambda sc, b=b, r=r, col0=col0, ncol=ncol, scale=scale:
                  sc.activation(self.hsb[r][:, 0:ncol], self.psS[b][:, col0:col0 + ncol],
                                AF.Exp, scale=scale),
                  waits=we, sig=self.act2, nop=True)
                self.psS_last[b] = self.act2.n
                exp_tok[i] = self.act2.n

            def rec_fix(i, qc=qc, meta=meta, exp_tok=exp_tok, fix_tok=fix_tok):
                kt, j, col0, ncol, b, r = meta[i]
                if a1:
                    dp = self.dsc_par
                    self.dsc_par ^= 1
                    R("tensor", lambda t, qc=qc, j=j, tr=tr: t.matmul(
                        self.psR[0:128, 0:128], kb[64:65, 0:128],
                        tr[64:65, qc * 512 + j * 128: qc * 512 + j * 128 + 128],
                        start=True, stop=True, skip_group_check=True),
                      waits=[self.act, self.dve], sig=self.pe2)
                    R("vector", lambda v, r=r: v.tensor_tensor(
                        self.hsb[r][:, 0:128], self.hsb[r][:, 0:128], self.U01[:], ALU.mult),
                      waits=[(self.act2, exp_tok[i])], sig=self.dve)
                    R("vector", lambda v, dp=dp: v.tensor_tensor(
                        self.dsc[:, dp * 128:(dp + 1) * 128],
                        self.psR[0:128, 0:128], self.L01[:], ALU.mult),
                      waits=[self.pe2], sig=self.dve)
                    R("vector", lambda v, r=r, dp=dp: v.tensor_tensor(
                        self.hsb[r][:, 0:128], self.hsb[r][:, 0:128],
                        self.dsc[:, dp * 128:(dp + 1) * 128], ALU.add),
                      sig=self.dve, ss=True, nop=True)
                else:
                    R("vector", lambda v, r=r: v.tensor_tensor(
                        self.hsb[r][:, 0:128], self.hsb[r][:, 0:128], self.U01[:], ALU.mult),
                      waits=[(self.act2, exp_tok[i])], sig=self.dve, ss=True, nop=True)
                fix_tok[i] = self.dve.n

            def rec_pv_clean(i, qc=qc, meta=meta, exp_tok=exp_tok):
                kt, j, col0, ncol, b, r = meta[i]
                off = 128 if j >= 0 else 0
                if ncol <= off:
                    return
                w = [(self.act2, exp_tok[i])]
                if i == 0:
                    w.append(self.dve2)
                R("tensor", lambda t, kt=kt, r=r, col0=col0, off=off, ncol=ncol, i=i:
                  t.matmul(self.psV[0:65, col0 + off:col0 + ncol],
                           self.vsl(kt, h), self.hsb[r][:, off:ncol],
                           start=(i == 0), stop=False, skip_group_check=True),
                  waits=w, sig=self.pe)
                if off == 0:
                    self.pt_last[r] = self.pe.n

            def rec_pv_diag(i, qc=qc, meta=meta, fix_tok=fix_tok):
                kt, j, col0, ncol, b, r = meta[i]
                R("tensor", lambda t, kt=kt, r=r, col0=col0:
                  t.matmul(self.psV[0:65, col0:col0 + 128],
                           self.vsl(kt, h), self.hsb[r][:, 0:128],
                           start=False, stop=False, skip_group_check=True),
                  waits=[(self.dve, fix_tok[i])], sig=self.pe)
                self.pt_last[r] = self.pe.n

            rec_S(0)
            pend = None
            for i in range(NT):
                if i + 1 < NT:
                    rec_S(i + 1)
                if meta[i][1] >= 0:
                    rec_fix(i)
                rec_pv_clean(i)
                if pend is not None:
                    rec_pv_diag(pend)
                    pend = None
                if meta[i][1] >= 0:
                    pend = i
            if pend is not None:
                rec_pv_diag(pend)
            if a1:
                for j2 in range(4):
                    R("tensor", lambda t, qc=qc, j2=j2, tr=tr, sufb=sufb: t.matmul(
                        self.psV[0:65, j2 * 128:(j2 + 1) * 128],
                        sufb[64:65, (4 * qc + j2) * 65:(4 * qc + j2 + 1) * 65],
                        tr[64:65, qc * 512 + j2 * 128: qc * 512 + (j2 + 1) * 128],
                        start=False, stop=(j2 == 3), skip_group_check=True),
                      waits=[(self.dve, suf_dve)], sig=self.pe)
            pv_done = self.pe.n
            R("vector", lambda v: v.reciprocal(self.recrow[64:65, :], self.psV[64:65, :]),
              waits=[(self.pe, pv_done)], sig=self.dve, ss=True, nop=True)
            R("tensor", lambda t: t.matmul(
                self.psR[0:64, :], self.onef_r64[64:65, 0:64], self.recrow[64:65, :],
                start=True, stop=True, skip_group_check=True),
              waits=[self.dve], sig=self.pe2)
            R("vector", lambda v: v.tensor_copy(self.rb32[:], self.psR[0:64, :]),
              waits=[self.pe2], sig=self.dve, ss=True)
            R("vector", lambda v: v.tensor_tensor(
                self.wnorm[:], self.psV[0:64, :], self.rb32[:], ALU.mult),
              sig=self.dve, ss=True, nop=True)
            cbuf = (h // 2) % 2
            wfw = [(self.scl, wf_scl[-2])] if len(wf_scl) >= 2 else                   [(self.scl, self.wf_guard)] if hasattr(self, "wf_guard") else []
            if h % 2 == 0:
                R("vector", lambda v, cbuf=cbuf, qc=qc: v.tensor_copy(
                    self.wfb[cbuf][0:64, qc * 512:(qc + 1) * 512], self.wnorm[:]),
                  waits=wfw, sig=self.dve2)
            else:
                R("gpsimd", lambda g, cbuf=cbuf, qc=qc: g.dma_start(
                    self.wfb[cbuf][64:128, qc * 512:(qc + 1) * 512], self.wnorm[:]),
                  waits=wfw + [self.dve], sig=self.gsh, n=16)
                R("vector", lambda v: v.engine_nop(), waits=[self.gsh], sig=self.dve2)
        if getattr(self, "dbg", False) and kind == 2 and h == 0 and getattr(self, "dbg_wdump", False):
            if getattr(self, "dbg_wdump", 0) == 2:
                for sl in range(8):
                    R("sync", lambda sy, sl=sl: sy.dma_start(
                        self.dbgd[sl, :, 0:512], self.Creg[:, 4096 + sl * 512:4096 + (sl + 1) * 512]),
                      waits=[self.dve2, self.dve], sig=self.syn, n=16)
            else:
                for sl in range(6):
                    R("sync", lambda sy, sl=sl: sy.dma_start(
                        self.dbgd[2 + sl, :, 0:512], self.w0[:, sl * 512:(sl + 1) * 512]),
                      waits=[self.dve2], sig=self.syn, n=16)
            R("vector", lambda v: v.tensor_copy(self.n32[0:64, 0:512], self.psP[0][0:64, :]),
              waits=[self.dve2, self.pe, self.pe2], sig=self.dve, ss=True)
            R("vector", lambda v: v.tensor_copy(self.n32[0:64, 512:1024], self.psP[1][0:64, :]),
              sig=self.dve, ss=True)
            R("sync", lambda sy: sy.dma_start(self.dbgf[0, 0:64, :], self.n32[0:64, 0:512]),
              waits=[self.dve], sig=self.syn, n=16)
            R("sync", lambda sy: sy.dma_start(self.dbgf[1, 0:64, :], self.n32[0:64, 512:1024]),
              waits=[self.dve], sig=self.syn, n=16)
        if getattr(self, "dbg", False) and h == 0 and kind == getattr(self, "dbgkind", 1):
            R("sync", lambda sy: sy.dma_start(self.dbgd[0, 0:65, :], kb[0:65, :]),
              waits=[self.dve2, self.act2], sig=self.syn, n=16)
            R("sync", lambda sy: sy.dma_start(self.dbgd[1, 0:65, :], qb[0:65, :]),
              waits=[self.dve2, self.act2], sig=self.syn, n=16)
            R("sync", lambda sy: sy.dma_start(self.dbgd[2, 64:65, :], self.TR[64:65, :]),
              waits=[self.act2], sig=self.syn, n=16)
            R("sync", lambda sy: sy.dma_start(self.dbgd[3, 64:65, 0:self.TT * 65], self.sufB[64:65, :]),
              waits=[self.dve], sig=self.syn, n=16)
            R("sync", lambda sy: sy.dma_start(self.dbgd[4, 64:65, 0:512], self.wnorm[0:1, :]),
              waits=[self.dve], sig=self.syn, n=16)
            R("sync", lambda sy: sy.dma_start(self.dbgd[6, 0:64, 0:512], self.wnorm[:]),
              waits=[self.dve], sig=self.syn, n=16)
            R("sync", lambda sy: sy.dma_start(
                self.dbgd[7], self.xnh[:, 0:self.S]),
              waits=[self.dve2], sig=self.syn, n=16)
        self.headuse[i] = self.pe.n
        if h % 2 == 1:
            cbuf = (h // 2) % 2
            c = h // 2
            R("scalar", lambda sc, c=c, cbuf=cbuf: sc.dma_start(
                self.wfd[c], self.wfb[cbuf]),
              waits=[self.dve2, self.gsh], sig=self.scl, n=16)
            wf_scl.append(self.scl.n)

    def attn_out(self, s, kind, wo_ready, wf_scl):
        a1 = kind == 1
        S, TT = self.S, self.TT
        ph = Ph(self.nc, f"aout{kind}_{s}")
        R = ph.rec
        optok = []
        petok = []
        a1tok = []
        for tt in range(TT):
            b = tt % 2
            w = [(self.scl, wf_scl[-1])]
            if tt >= 2:
                w.append((self.pe, petok[tt - 2]))
            R("sync", lambda sy, tt=tt, b=b: sy.dma_start(
                self.opj[b][:].rearrange("p (c j) -> p c j", c=FC),
                self.wfd.rearrange("c p s -> p c s")[:, :, tt * 128:(tt + 1) * 128]),
              waits=w, sig=self.syn, n=16)
            optok.append(self.syn.n)
            if a1:
                xw = [(self.dve, a1tok[tt - 2]), (self.scl, self.scl.n)] if tt >= 2 else []
                R("sync", lambda sy, tt=tt, b=b: sy.dma_start(
                    self.t32[b][:], self.x[s, tt * 128:(tt + 1) * 128, :]),
                  waits=xw, sig=self.syn, n=16)
            wmm = [(self.syn, optok[tt] + (16 if a1 else 0)), (self.dve, wo_ready)]
            if tt >= 1:
                wmm.append(self.dve)
                wmm.append(self.dve2)
            def omm(t, b=b):
                for half in range(2):
                    for c in range(FC):
                        ins = t.matmul(self.psO[:, half * 512:(half + 1) * 512],
                                       self.opj[b][:, c * 128:(c + 1) * 128],
                                       self.w2s[:, c * D + half * 512: c * D + (half + 1) * 512],
                                       start=(c == 0), stop=(c == FC - 1))
                return ins
            R("tensor", omm, waits=wmm, sig=self.pe)
            petok.append(self.pe.n)
            if not a1 and getattr(self, "a2mode", "full") == "plainout":
                R("vector", lambda v: v.tensor_tensor(self.n32[:], self.psO[:], self.bo_bc[:], ALU.add),
                  waits=[self.pe], sig=self.dve)
                continue
            if a1:
                R("vector", lambda v: v.tensor_tensor(self.n32[:], self.psO[:], self.bo_bc[:], ALU.add),
                  waits=[self.pe], sig=self.dve)
                R("vector", lambda v, b=b: v.tensor_tensor(
                    self.t32[b][:], self.n32[:], self.t32[b][:], ALU.add),
                  waits=[(self.scl, self.scl.n), (self.syn, optok[tt] + 16)],
                  sig=self.dve, ss=True, nop=True)
                a1tok.append(self.dve.n)
                R("scalar", lambda sc, s=s, tt=tt, b=b: sc.dma_start(
                    self.y[s, tt * 128:(tt + 1) * 128, :], self.t32[b][:]),
                  waits=[self.dve], sig=self.scl, n=16)
            else:
                self.rec_h2_gate(R, s, tt, b)
        ph.emit()

    def rec_h2_gate(self, R, s, tt, b):
        S = self.S
        R("scalar", lambda sc: sc.activation(self.sqscr[:], self.psO[:], AF.Square),
          waits=[self.pe], sig=self.act)
        R("vector", lambda v: v.reduce_sum(self.stat[:, 0:1], self.psO[:], axis=AX),
          waits=[self.pe, self.act], sig=self.dve, ss=True)
        R("vector", lambda v: v.reduce_sum(self.stat[:, 1:2], self.sqscr[:], axis=AX),
          sig=self.dve, ss=True)
        c = 1.0 / D
        R("vector", lambda v, c=c: v.tensor_scalar(self.stat[:, 2:3], self.stat[:, 0:1], c, None, ALU.mult),
          ss=True)
        R("vector", lambda v, c=c: v.tensor_scalar(self.stat[:, 3:4], self.stat[:, 1:2], c, None, ALU.mult),
          ss=True)
        R("vector", lambda v: v.tensor_tensor(self.stat[:, 4:5], self.stat[:, 2:3], self.stat[:, 2:3], ALU.mult),
          ss=True)
        R("vector", lambda v: v.tensor_tensor(self.stat[:, 5:6], self.stat[:, 3:4], self.stat[:, 4:5], ALU.subtract),
          sig=self.dve, ss=True)
        R("scalar", lambda sc: sc.activation(self.stat[:, 6:7], self.stat[:, 5:6], AF.Ln, bias=LNEPS),
          waits=[self.dve], ss=True)
        R("scalar", lambda sc: sc.activation(self.stat[:, 6:7], self.stat[:, 6:7], AF.Exp, scale=-0.5),
          sig=self.act, ss=True, nop=True)
        R("vector", lambda v: v.tensor_tensor(self.stat[:, 7:8], self.stat[:, 2:3], self.stat[:, 6:7], ALU.mult),
          waits=[self.act], ss=True)
        R("vector", lambda v: v.tensor_scalar(self.stat[:, 7:8], self.stat[:, 7:8], -1.0, None, ALU.mult),
          sig=self.dve, ss=True)
        R("vector", lambda v: v.tensor_scalar(self.n32[:], self.psO[:],
                                              self.stat[:, 6:7], self.stat[:, 7:8],
                                              ALU.mult, ALU.add),
          sig=self.dve)
        R("vector", lambda v: v.tensor_tensor(self.sqscr[:], self.n32[:], self.g_bc[:], ALU.mult))
        R("vector", lambda v: v.tensor_tensor(self.xb16[:], self.sqscr[:], self.b_bc[:], ALU.add),
          waits=[self.pe], sig=self.dve)
        if getattr(self, "a2mode", "full") == "h2a":
            return
        def trs(t):
            for fc in range(FC):
                ins = t.transpose(self.psOb[:, fc * 128:(fc + 1) * 128],
                                  self.xb16[:, fc * 128:(fc + 1) * 128], self.idn[:])
            return ins
        R("tensor", trs, waits=[self.dve, self.dve2], sig=self.pe)
        R("vector", lambda v: v.tensor_copy(self.h2ft[:], self.psOb[:, 0:FC * 128]),
          waits=[self.pe], sig=self.dve2, ss=True, nop=True)
        if getattr(self, "a2mode", "full") == "h2b":
            return
        R("scalar", lambda sc, s=s, tt=tt: sc.dma_start(
            self.h2d.rearrange("n (c p) s -> n p c s", p=128)[s, :, :, tt * 128:(tt + 1) * 128],
            self.h2ft[:].rearrange("p (c j) -> p c j", c=FC)),
          waits=[self.dve2], sig=self.scl, n=16)
        if getattr(self, "a2mode", "full") == "h2nogate":
            R("vector", lambda v, s=s, tt=tt: v.memset(
                self.gate_all[:, (s * self.TT + tt) * 8:(s * self.TT + tt + 1) * 8], 0.125),
              sig=self.dve)
            R("vector", lambda v, b=b: v.memset(self.grow[b][:], 0.125), sig=self.dve)
            R("scalar", lambda sc, s=s, tt=tt, b=b: sc.dma_start(
                self.gtd[:, s * self.S + tt * 128: s * self.S + (tt + 1) * 128], self.grow[b][:]),
              waits=[self.dve], sig=self.scl, n=16)
            return
        def gmm(t):
            for fc in range(FC):
                ins = t.matmul(self.psR[:, 0:8], self.h2ft[:, fc * 128:(fc + 1) * 128],
                               self.gwc[:, fc * 8:(fc + 1) * 8],
                               start=(fc == 0), stop=(fc == FC - 1), skip_group_check=True)
            return ins
        R("tensor", gmm, waits=[self.dve2, self.dve], sig=self.pe2)
        R("vector", lambda v: v.tensor_tensor(self.gsc[:], self.psR[:, 0:8], self.gb_bc[:], ALU.add),
          waits=[self.pe2], sig=self.dve, ss=True)
        R("vector", lambda v: v.reduce_max(self.gsc2[:, 0:1], self.gsc[:], axis=AX, negate=True),
          sig=self.dve, ss=True)
        R("scalar", lambda sc: sc.activation(self.gsc[:], self.gsc[:], AF.Exp,
                                             bias=self.gsc2[:, 0:1]),
          waits=[self.dve], sig=self.act, ss=True)
        R("vector", lambda v: v.reduce_sum(self.gsc2[:, 1:2], self.gsc[:], axis=AX),
          waits=[self.act], sig=self.dve, ss=True)
        R("vector", lambda v: v.reciprocal(self.gsc2[:, 2:3], self.gsc2[:, 1:2]),
          sig=self.dve, ss=True)
        R("vector", lambda v, s=s, tt=tt: v.tensor_scalar(
            self.gate_all[:, (s * self.TT + tt) * 8:(s * self.TT + tt + 1) * 8],
            self.gsc[:], self.gsc2[:, 2:3], None, ALU.mult),
          ss=True)
        R("vector", lambda v: v.tensor_scalar(self.gb16[:], self.gsc[:], self.gsc2[:, 2:3],
                                              None, ALU.mult), sig=self.dve, ss=True, nop=True)
        R("tensor", lambda t: t.transpose(self.psR[:].bitcast(BF16)[0:8, 0:128],
                                          self.gb16[:], self.idn[:]),
          waits=[self.dve], sig=self.pe2)
        R("vector", lambda v, b=b: v.tensor_copy(self.grow[b][:], self.psR[:].bitcast(BF16)[0:8, 0:128]),
          waits=[self.pe2], sig=self.dve)
        R("scalar", lambda sc, s=s, tt=tt, b=b: sc.dma_start(
            self.gtd[:, s * S + tt * 128: s * S + (tt + 1) * 128], self.grow[b][:]),
          waits=[self.dve], sig=self.scl, n=16)

    def moe_phase1(self, e):
        S, FF, FFC = self.S, self.FF, self.FFC
        nTok = self.TOK // 512
        ph = Ph(self.nc, f"moe1_{e}")
        R = ph.rec
        if e == 0:
            for fc in range(FC):
                self.stage_w(R, self.WA[:, fc * FF:(fc + 1) * FF],
                             self.w1[e, fc * 128:(fc + 1) * 128, :], FF)
            R("sync", lambda sy: sy.dma_start(
                self.rowtmp[0:FFC // 2, 0:128],
                self.b1[e].rearrange("(a b) -> a b", b=128)[0:FFC // 2]),
              sig=self.syn, n=16)
            R("sync", lambda sy: sy.dma_start(
                self.rowtmp[0:FFC // 2, 128:256],
                self.b1[e].rearrange("(a b) -> a b", b=128)[FFC // 2:FFC]),
              sig=self.syn, n=16)
            self.w1tok = [self.stage_fence(R)]
            R("tensor", lambda t: t.transpose(self.psR[:, 0:128],
                                              self.rowtmp[:, 0:128], self.idnf[:]),
              waits=[self.syn], sig=self.pe2)
            R("tensor", lambda t: t.transpose(self.psR[:, 128:256],
                                              self.rowtmp[:, 128:256], self.idnf[:]),
              sig=self.pe2)
            R("vector", lambda v: v.tensor_copy(self.b1c[0][:, 0:FFC // 2], self.psR[:, 0:FFC // 2]),
              waits=[self.pe2], sig=self.dve)
            R("vector", lambda v: v.tensor_copy(self.b1c[0][:, FFC // 2:FFC],
                                                self.psR[:, 128:128 + FFC // 2]),
              sig=self.dve, ss=True, nop=True)
            self.b1tok = self.dve.n
        for ffc in range(FFC):
            self.stage_w(R, self.WB[:, ffc * D:(ffc + 1) * D],
                         self.w2[e, ffc * 128:(ffc + 1) * 128, :], D)
        self.w2tok = [self.stage_fence(R)]

        if getattr(self, "dbg", False) and e == self.NE - 1:
            R("sync", lambda sy: sy.dma_start(self.dbgd[3, :, :],
                                              self.WA[:, 0:self.S]),
              waits=[(self.dve, self.w1tok[-1])], sig=self.syn, n=16)
            R("sync", lambda sy: sy.dma_start(self.dbgd[4, :, :],
                                              self.WA[:, 3072:3072 + self.S]),
              waits=[(self.dve, self.w1tok[-1])], sig=self.syn, n=16)
            R("sync", lambda sy: sy.dma_start(
                self.dbgd[5, :, 0:32].bitcast(F32)[:, 0:32] if False else self.dbgd[5, :, 0:64],
                self.b1c[0][:, 0:32].bitcast(BF16)),
              waits=[self.dve], sig=self.syn, n=16)
        h2tok = []
        hid_scl = []
        for tokT in range(nTok):
            sq = (tokT * 512) // S
            off = (tokT * 512) % S
            b = tokT % 2
            w = []
            if tokT >= 2:
                w.append((self.pe, self.moe1_pe[tokT - 2] if hasattr(self, 'moe1_pe') and len(self.moe1_pe) > tokT - 2 else 0))
            w = [(self.pe, h2tok[tokT - 2][1])] if tokT >= 2 else [(self.scl, self.scl.n)]
            R("sync", lambda sy, sq=sq, off=off, b=b: sy.dma_start(
                self.h2cur[b][:].rearrange("p (c j) -> p c j", c=FC),
                self.h2d.rearrange("n (c p) s -> n p c s", p=128)[sq, :, :, off:off + 512]),
              waits=w, sig=self.syn, n=16)
            htok = self.syn.n
            for ffc in range(FFC):
                r = (tokT * FFC + ffc) % 3
                wmm = [(self.syn, htok), (self.dve, self.w1tok[-1])]
                pstile = (ffc % 2, (ffc // 2) % 2)
                if not hasattr(self, "ps1_last"):
                    self.ps1_last = {}
                if pstile in self.ps1_last:
                    wmm.append((self.act2, self.ps1_last[pstile]))
                prev = getattr(self, "hsb_last", {})
                if r in prev:
                    wmm.append((self.scl, prev[r]))
                def mm1(t, ffc=ffc, b=b):
                    ps = self.psP[ffc % 2] if (ffc // 2) % 2 == 0 else self.psS[ffc % 2]
                    for fc in range(FC):
                        ins = t.matmul(ps[:],
                                       self.WA[:, fc * FF + ffc * 128: fc * FF + (ffc + 1) * 128],
                                       self.h2cur[b][:, fc * 512:(fc + 1) * 512],
                                       start=(fc == 0), stop=(fc == FC - 1))
                    return ins
                R("tensor", mm1, waits=wmm, sig=self.pe)
                petok = self.pe.n
                def gelu(sc, ffc=ffc, r=r):
                    ps = self.psP[ffc % 2] if (ffc // 2) % 2 == 0 else self.psS[ffc % 2]
                    return sc.activation(self.hsb[r][:], ps[:], AF.Gelu_apprx_tanh,
                                         bias=self.b1c[e % 2][:, ffc:ffc + 1])
                R("scalar", gelu, waits=[(self.pe, petok), (self.dve, self.b1tok)], sig=self.act2, ss=True)
                self.ps1_last[(ffc % 2, (ffc // 2) % 2)] = self.act2.n
                R("scalar", lambda sc, ffc=ffc, tokT=tokT, r=r: sc.dma_start(
                    self.hidd[ffc * 128:(ffc + 1) * 128, tokT * 512:(tokT + 1) * 512],
                    self.hsb[r][:]),
                  waits=[(self.act2, self.act2.n, True)], sig=self.scl, n=16)
                if not hasattr(self, "hsb_last"):
                    self.hsb_last = {}
                self.hsb_last[r] = self.scl.n
            h2tok.append((htok, self.pe.n))
            hid_scl.append(self.scl.n)
        self.hid_ready = self.scl.n
        ph.emit()

    def moe_phase2(self, e):
        S, FF, FFC, D_ = self.S, self.FF, self.FFC, D
        nTT = self.TOK // 128
        ph = Ph(self.nc, f"moe2_{e}")
        R = ph.rec
        if e + 1 < self.NE:
            for fc in range(FC):
                self.stage_w(R, self.WA[:, fc * FF:(fc + 1) * FF],
                             self.w1[e + 1, fc * 128:(fc + 1) * 128, :], FF)
            R("sync", lambda sy, e=e: sy.dma_start(
                self.rowtmp[0:FFC // 2, 0:128],
                self.b1[e + 1].rearrange("(a b) -> a b", b=128)[0:FFC // 2]),
              sig=self.syn, n=16)
            R("sync", lambda sy, e=e: sy.dma_start(
                self.rowtmp[0:FFC // 2, 128:256],
                self.b1[e + 1].rearrange("(a b) -> a b", b=128)[FFC // 2:FFC]),
              sig=self.syn, n=16)
            self.w1tok = [self.stage_fence(R)]
            R("tensor", lambda t: t.transpose(self.psR[:, 0:128],
                                              self.rowtmp[:, 0:128], self.idnf[:]),
              waits=[self.syn, self.act2, self.dve], sig=self.pe2)
            R("tensor", lambda t: t.transpose(self.psR[:, 128:256],
                                              self.rowtmp[:, 128:256], self.idnf[:]),
              sig=self.pe2)
            R("vector", lambda v, e=e: v.tensor_copy(self.b1c[(e + 1) % 2][:, 0:FFC // 2],
                                                     self.psR[:, 0:FFC // 2]),
              waits=[self.pe2], sig=self.dve)
            R("vector", lambda v, e=e: v.tensor_copy(self.b1c[(e + 1) % 2][:, FFC // 2:FFC],
                                                     self.psR[:, 128:128 + FFC // 2]),
              sig=self.dve, ss=True, nop=True)
            self.b1tok = self.dve.n

        mtok = []
        for tt in range(nTT):
            b = tt % 2
            w = [(self.scl, self.hid_ready)]
            if tt >= 2:
                w.append((self.pe, mtok[tt - 2][0]))
            R("sync", lambda sy, tt=tt, b=b: sy.dma_start(
                self.hid2[:, b * FFC * 128:(b + 1) * FFC * 128].rearrange(
                    "p (c j) -> p c j", c=FFC),
                self.hidd.rearrange("(c p) s -> p c s", p=128)[:, :, tt * 128:(tt + 1) * 128]),
              waits=w, sig=self.syn, n=16)
            htok = self.syn.n
            wmm = [(self.syn, htok), (self.dve, self.w2tok[-1]),
                   (self.dve, self.b1tok)]
            if tt >= 2:
                wmm.append((self.dve, mtok[tt - 2][1]))
            def mm2(t, tt=tt, b=b):
                pair = [self.psO[:, 0:512], self.psO[:, 512:1024]] if tt % 2 == 0 \
                    else [self.psV[:], self.psR[:]]
                for half in range(2):
                    for ffc in range(FFC):
                        ins = t.matmul(pair[half],
                                       self.hid2[:, b * FFC * 128 + ffc * 128:
                                                 b * FFC * 128 + (ffc + 1) * 128],
                                       self.WB[:, ffc * D_ + half * 512: ffc * D_ + (half + 1) * 512],
                                       start=(ffc == 0), stop=(ffc == FFC - 1))
                return ins
            R("tensor", mm2, waits=wmm, sig=self.pe)
            petok = self.pe.n
            gcol_idx = tt * 8 + e
            def evac(v, tt=tt, e=e):
                pair = [self.psO[:, 0:512], self.psO[:, 512:1024]] if tt % 2 == 0 \
                    else [self.psV[:], self.psR[:]]
                gslice = self.gate_all[:, tt * 8 + e: tt * 8 + e + 1]
                v.tensor_scalar(self.t32[tt % 2][:, 0:512], pair[0], gslice, None, ALU.mult)
                return v.tensor_scalar(self.t32[tt % 2][:, 512:1024], pair[1], gslice, None, ALU.mult)
            wev = [(self.pe, petok)]
            if tt >= 2:
                wev.append((self.gsm, mtok[tt - 2][2]))
            R("vector", evac, waits=wev, sig=self.dve, ss=True, nop=True)
            dtok = self.dve.n
            R("gpsimd", lambda g, tt=tt, e=e: g.dma_start(
                self.moed[tt * 128:(tt + 1) * 128, :], self.t32[tt % 2][:],
                accum_op=(ALU.bypass if e == 0 else ALU.add)),
              waits=[(self.dve, dtok)] + ([(self.gsm, self.moeacc[tt])] if e > 0 else []),
              sig=self.gsm, n=16)
            if not hasattr(self, "moeacc"):
                self.moeacc = {}
            self.moeacc[tt] = self.gsm.n
            mtok.append((petok, dtok, self.gsm.n))
        ph.emit()

    def stage_e(self):
        S, TT = self.S, self.TT
        nTT = self.TOK // 128
        ph = Ph(self.nc, "E")
        R = ph.rec
        self.etok = {}
        self.b2tok = {}
        for tt in range(nTT):
            s, tl = tt // TT, tt % TT
            b = tt % 2
            w = [(self.gsm, self.moeacc[tt]), (self.scl, self.scl.n)]
            if tt >= 2:
                w.append((self.dve, self.etok[tt - 2]))
            R("sync", lambda sy, tt=tt, b=b: sy.dma_start(
                self.t32[b][:], self.moed[tt * 128:(tt + 1) * 128, :]),
              waits=w, sig=self.syn, n=16)
            gw_w = [(self.pe, self.b2tok[tt - 2])] if tt >= 2 else []
            R("sync", lambda sy, tt=tt, b=b: sy.dma_start(
                self.grow[b][:], self.gtd[:, tt * 128:(tt + 1) * 128]),
              waits=gw_w, sig=self.syn, n=16)
            ld = self.syn.n
            NEr = self.NE
            wb2 = [(self.syn, ld), self.dve]
            def b2mm(t, b=b, NEr=NEr):
                t.matmul(self.psO[:, 0:512], self.grow[b][0:NEr, :], self.b2bf[0:NEr, 0:512],
                         start=True, stop=True)
                return t.matmul(self.psO[:, 512:1024], self.grow[b][0:NEr, :],
                                self.b2bf[0:NEr, 512:1024], start=True, stop=True)
            R("tensor", b2mm, waits=wb2, sig=self.pe)
            self.b2tok[tt] = self.pe.n
            R("vector", lambda v, b=b: v.tensor_tensor(self.n32[:], self.t32[b][:], self.psO[:], ALU.add),
              waits=[self.pe], sig=self.dve)
            comb_tok = self.dve.n
            R("sync", lambda sy, s=s, tl=tl, b=b: sy.dma_start(
                self.t32[b][:], self.y[s, tl * 128:(tl + 1) * 128, :]),
              waits=[(self.dve, comb_tok), (self.scl, self.y_attn_tok)],
              sig=self.syn, n=16)
            ld2 = self.syn.n
            R("scalar", lambda sc: sc.activation(self.sqscr[:], self.n32[:], AF.Square),
              waits=[self.dve], sig=self.act)
            R("vector", lambda v: v.reduce_sum(self.stat[:, 0:1], self.n32[:], axis=AX),
              ss=True)
            R("vector", lambda v: v.reduce_sum(self.stat[:, 1:2], self.sqscr[:], axis=AX),
              waits=[self.act], sig=self.dve, ss=True)
            c = 1.0 / D
            R("vector", lambda v, c=c: v.tensor_scalar(self.stat[:, 2:3], self.stat[:, 0:1], c, None, ALU.mult),
              ss=True)
            R("vector", lambda v, c=c: v.tensor_scalar(self.stat[:, 3:4], self.stat[:, 1:2], c, None, ALU.mult),
              ss=True)
            R("vector", lambda v: v.tensor_tensor(self.stat[:, 4:5], self.stat[:, 2:3], self.stat[:, 2:3], ALU.mult),
              ss=True)
            R("vector", lambda v: v.tensor_tensor(self.stat[:, 5:6], self.stat[:, 3:4], self.stat[:, 4:5], ALU.subtract),
              sig=self.dve, ss=True)
            R("scalar", lambda sc: sc.activation(self.stat[:, 6:7], self.stat[:, 5:6], AF.Ln, bias=LNEPS),
              waits=[self.dve], ss=True)
            R("scalar", lambda sc: sc.activation(self.stat[:, 6:7], self.stat[:, 6:7], AF.Exp, scale=-0.5),
              sig=self.act, ss=True, nop=True)
            R("vector", lambda v: v.tensor_tensor(self.stat[:, 7:8], self.stat[:, 2:3], self.stat[:, 6:7], ALU.mult),
              waits=[self.act], ss=True)
            R("vector", lambda v: v.tensor_scalar(self.stat[:, 7:8], self.stat[:, 7:8], -1.0, None, ALU.mult),
              sig=self.dve, ss=True)
            R("vector", lambda v: v.tensor_scalar(self.n32[:], self.n32[:],
                                                  self.stat[:, 6:7], self.stat[:, 7:8],
                                                  ALU.mult, ALU.add),
              sig=self.dve)
            R("vector", lambda v: v.tensor_tensor(self.sqscr[:], self.n32[:], self.g_bc[:], ALU.mult))
            R("vector", lambda v: v.tensor_tensor(self.sqscr[:], self.sqscr[:], self.b_bc[:], ALU.add))
            R("vector", lambda v, b=b: v.tensor_tensor(self.t32[b][:], self.t32[b][:], self.sqscr[:], ALU.add),
              waits=[(self.syn, ld2)], sig=self.dve, ss=True, nop=True)
            self.etok[tt] = self.dve.n
            R("scalar", lambda sc, s=s, tl=tl, b=b: sc.dma_start(
                self.y[s, tl * 128:(tl + 1) * 128, :], self.t32[b][:]),
              waits=[self.dve], sig=self.scl, n=16)
        ph.emit()

    def finish(self):
        ph = Ph(self.nc, "fin")
        def fin(sy):
            for ev in self.all_dma:
                if ev.n > 0:
                    sy.wait_ge(ev.h, ev.n)
            return sy.nop()
        ph.rec("sync", fin)
        ph.emit()


def make_in_map(inputs, core):
    x = inputs["L0_attn_residual"]
    return {
        "x": np.ascontiguousarray(x[2 * core: 2 * core + 2]),
        "wk1": inputs["L0_attn_W_key"],
        "wq1": inputs["L0_attn_W_query"],
        "wv1": inputs["L0_attn_W_values"],
        "wo1": inputs["L0_attn_W_output"],
        "bk1": inputs["L0_attn_Bias_key"],
        "bq1": inputs["L0_attn_Bias_query"],
        "bv1": inputs["L0_attn_Bias_values"],
        "bo1": inputs["L0_attn_Bias_output"],
        "lng": inputs["L0_ffn_ln_g"],
        "lnb": inputs["L0_ffn_ln_b"],
        "wq2": inputs["L0_ffn_Wq"],
        "wk2": inputs["L0_ffn_Wk"],
        "wv2": inputs["L0_ffn_Wv"],
        "wo2": inputs["L0_ffn_Wo"],
        "qs2": inputs["L0_ffn_q_scale"],
        "ks2": inputs["L0_ffn_k_scale"],
        "gw": inputs["L0_ffn_gate_w"],
        "gb": inputs["L0_ffn_gate_b"],
        "w1": inputs["L0_ffn_w1"],
        "b1": inputs["L0_ffn_b1"],
        "w2": inputs["L0_ffn_w2"],
        "b2": inputs["L0_ffn_b2"],
    }


def kernel(**inputs):
    b = Build(S=2048, NSEQ=2, NE=8, FF=4096)
    nc = b.build()
    in_maps = [make_in_map(inputs, c) for c in range(NCORES)]
    res = run_bass_kernel_spmd(nc, in_maps, core_ids=list(range(NCORES)))
    out = np.concatenate([r["y"] for r in res.results], axis=0)
    return out.astype(np.float32)


NCORES = 8


# version 11
# speedup vs baseline: 1.1227x; 1.0175x over previous
import numpy as np
import concourse.bass as bass
import concourse.mybir as mybir
from concourse.bass_utils import run_bass_kernel_spmd

F32 = mybir.dt.float32
BF16 = mybir.dt.bfloat16
AF = mybir.ActivationFunctionType
ALU = mybir.AluOpType
AX = mybir.AxisListType.X

D = 1024
H = 16
DH = 64
FC = 8
LNEPS = 1e-5
L2EPS = 1e-12
EPS1 = 1e-10


class Ev:

    def __init__(self, nc, name, engine, dma=False):
        self.h = nc.alloc_semaphore(name)
        self.n = 0
        self.engine = engine
        self.dma = dma


class Ph:
    def __init__(self, nc, name=""):
        self.nc = nc
        self.items = []
        self.name = name

    ENG_EV = {}

    def rec(self, engine, fn, waits=(), sig=None, n=1, ss=False, nop=False):
        if nop:
            fn0 = fn
            if engine == "scalar":
                def fn(E, fn0=fn0):
                    fn0(E)
                    return E.mul(Ph.NOPAP, Ph.NOPAP, 1.0)
            else:
                def fn(E, fn0=fn0):
                    fn0(E)
                    return E.engine_nop()
        wl = []
        for ev in waits:
            if ev is None:
                continue
            if isinstance(ev, tuple):
                if ev[1] > 0:
                    wl.append(ev if len(ev) == 3 else (ev[0], ev[1], False))
            elif ev.n > 0:
                wl.append((ev, ev.n, False))
        if ss and sig is None:
            sig = Ph.ENG_EV[engine]
        if sig is not None:
            assert sig.engine == engine, f"event {sig.h.name} owned by {sig.engine}, signaled from {engine}"
            sig.n += n
        self.items.append((engine, fn, wl, sig, n, ss))

    def emit(self):
        engines = []
        for it in self.items:
            if it[0] not in engines:
                engines.append(it[0])
        sigtgt = {}
        run = {}
        for it in self.items:
            sig = it[3]
            if sig is not None:
                run[sig] = run.get(sig, sig.n - sum(j[4] for j in self.items if j[3] is sig))
        for idx, it in enumerate(self.items):
            sig = it[3]
            if sig is not None:
                run[sig] += it[4]
                sigtgt[idx] = run[sig]
        with self.nc.Block() as blk:
            for ename in engines:
                section = getattr(blk, ename)

                def body(E, ename=ename):
                    waited = {}
                    for idx, (eng, fn, wl, sig, n, ss) in enumerate(self.items):
                        if eng != ename:
                            continue
                        for ev, v, force in wl:
                            if ev.engine == ename and not ev.dma and not force:
                                continue
                            if waited.get(ev, -1) >= v:
                                continue
                            E.wait_ge(ev.h, v)
                            waited[ev] = v
                        ins = fn(E)
                        if sig is not None:
                            ins.then_inc(sig.h, n)
                            if ss:
                                E.wait_ge(sig.h, sigtgt[idx])
                                waited[sig] = max(waited.get(sig, -1), sigtgt[idx])
                section(body)


class Build:
    def __init__(self, S=2048, NSEQ=2, NE=8, FF=4096, dbg=False):
        ik = "ExternalOutput" if dbg else "Internal"
        self.S = S
        self.NSEQ = NSEQ
        self.NE = NE
        self.FF = FF
        self.FFC = FF // 128
        self.TT = S // 128
        self.QC = S // 512
        self.TOK = S * NSEQ
        nc = bass.Bass("TRN2", target_bir_lowering=False)
        self.nc = nc
        S_ = S

        dt = nc.dram_tensor
        self.x = dt("x", [NSEQ, S, D], F32, kind="ExternalInput").ap()
        self.wk1 = dt("wk1", [H, D, DH], F32, kind="ExternalInput").ap()
        self.wq1 = dt("wq1", [H, D, DH], F32, kind="ExternalInput").ap()
        self.wv1 = dt("wv1", [H, D, DH], F32, kind="ExternalInput").ap()
        self.wo1 = dt("wo1", [H, DH, D], F32, kind="ExternalInput").ap()
        self.bk1 = dt("bk1", [H, DH], F32, kind="ExternalInput").ap()
        self.bq1 = dt("bq1", [H, DH], F32, kind="ExternalInput").ap()
        self.bv1 = dt("bv1", [H, DH], F32, kind="ExternalInput").ap()
        self.bo1 = dt("bo1", [D], F32, kind="ExternalInput").ap()
        self.lng = dt("lng", [D], F32, kind="ExternalInput").ap()
        self.lnb = dt("lnb", [D], F32, kind="ExternalInput").ap()
        self.wq2 = dt("wq2", [D, D], F32, kind="ExternalInput").ap()
        self.wk2 = dt("wk2", [D, D], F32, kind="ExternalInput").ap()
        self.wv2 = dt("wv2", [D, D], F32, kind="ExternalInput").ap()
        self.wo2 = dt("wo2", [D, D], F32, kind="ExternalInput").ap()
        self.qs2 = dt("qs2", [H, 1, DH], F32, kind="ExternalInput").ap()
        self.ks2 = dt("ks2", [H, 1, DH], F32, kind="ExternalInput").ap()
        self.gw = dt("gw", [D, 8], F32, kind="ExternalInput").ap()
        self.gb = dt("gb", [8], F32, kind="ExternalInput").ap()
        self.w1 = dt("w1", [NE, D, FF], F32, kind="ExternalInput").ap()
        self.b1 = dt("b1", [NE, FF], F32, kind="ExternalInput").ap()
        self.w2 = dt("w2", [NE, FF, D], F32, kind="ExternalInput").ap()
        self.b2 = dt("b2", [NE, D], F32, kind="ExternalInput").ap()
        self.y = dt("y", [NSEQ, S, D], F32, kind="ExternalOutput").ap()
        self.h2d = dt("h2d", [NSEQ, D, S], BF16, kind=ik).ap()
        self.hidd = dt("hidd", [FF, self.TOK], BF16, kind=ik).ap()
        self.moed = dt("moed", [self.TOK, D], F32, kind=ik).ap()
        self.wfd = dt("wfd", [FC, 128, S], BF16, kind=ik).ap()
        self.gtd = dt("gtd", [8, self.TOK], BF16, kind=ik).ap()
        self.dbg = dbg
        if dbg:
            self.dbgd = dt("dbgd", [8, 128, S], BF16, kind="ExternalOutput").ap()
            self.dbgf = dt("dbgf", [4, 128, 512], F32, kind="ExternalOutput").ap()

        def at(name, shape, dtype):
            return nc.alloc_sbuf_tensor(name, shape, dtype)

        wa_cols = max(FC * self.FF, FC * S_ + self.TT * H * 65)
        self.WA = at("WA", [128, wa_cols], BF16)
        self.xnh = self.WA[:, 0:FC * S_]
        self.Vt = self.WA[:, FC * S_: FC * S_ + self.TT * H * 65]
        wb_cols = max(self.FFC * D, 3 * FC * D + 4 * S_)
        self.WB = at("WB", [128, wb_cols], BF16)
        self.w0 = self.WB[:, 0:FC * D]
        self.w1s = self.WB[:, FC * D:2 * FC * D]
        self.w2s = self.WB[:, 2 * FC * D:3 * FC * D]
        kq0 = 3 * FC * D
        self.kb = [self.WB[0:65, kq0 + i * S_: kq0 + (i + 1) * S_] for i in range(2)]
        self.qb = [self.WB[0:65, kq0 + (2 + i) * S_: kq0 + (3 + i) * S_] for i in range(2)]
        self.Creg = at("Creg", [128, 8192], BF16)
        self.h2cur = [self.Creg[:, i * 4096:(i + 1) * 4096] for i in range(2)]
        self.hid2 = self.Creg
        self.wfb = [self.Creg[:, i * S_:(i + 1) * S_] for i in range(2)]
        self.opj = [self.Creg[:, 2 * S_ + i * 1024: 2 * S_ + (i + 1) * 1024] for i in range(2)]
        self.h2ft = self.Creg[:, 2 * S_ + 2048: 2 * S_ + 3072]
        self.t32 = [at(f"t32_{i}", [128, D], F32) for i in range(2)]
        self.sqscr = at("sqscr", [128, D], F32)
        self.n32 = at("n32", [128, D], F32)
        self.g_bc = at("g_bc", [128, D], BF16)
        self.b_bc = at("b_bc", [128, D], BF16)
        self.bo_bc = at("bo_bc", [128, D], BF16)
        self.bv_bc = at("bv_bc", [128, D], BF16)
        self.gb_bc = at("gb_bc", [128, 8], F32)
        self.TR = at("TR", [65, S_], BF16)
        self.sufB = at("sufB", [65, self.TT * 65], BF16)
        self.lrow = at("lrow", [65, 512], F32)
        self.mk = at("mk", [65, 8], F32)
        self.r0a = at("r0a", [1, 512], F32)
        self.r0b = at("r0b", [1, 512], BF16)
        self.rbc_sb = at("rbc_sb", [64, 512], BF16)
        self.rb32 = at("rb32", [64, 512], F32)
        self.recrow = at("recrow", [65, 512], F32)
        self.sq_sc = at("sq_sc", [64, 512], BF16)
        self.wnorm = at("wnorm", [64, 512], BF16)
        self.dsc = at("dsc", [128, 256], BF16)
        self.U01 = at("U01", [128, 128], BF16)
        self.L01 = at("L01", [128, 128], BF16)
        self.idn = at("idn", [128, 128], BF16)
        self.idnf = at("idnf", [128, 128], F32)
        self.ones_c = at("ones_c", [128, 1], BF16)
        self.onef_c = at("onef_c", [128, 1], F32)
        self.ones_r0 = at("ones_r0", [1, 128], BF16)
        self.onef_r64 = at("onef_r64", [65, 64], F32)
        self.onef_r = at("onef_r", [65, 128], F32)
        self.bkc = at("bkc", [64, H], F32)
        self.bqc = at("bqc", [64, H], F32)
        self.qsc = at("qsc", [64, H], F32)
        self.ksc = at("ksc", [64, H], F32)
        self.gcol = at("gcol", [128, FC], F32)
        self.bcol = at("bcol", [128, FC], F32)
        self.b1c = [at(f"b1c_{i}", [128, 32], F32) for i in range(2)]
        self.b2bf = at("b2bf", [8, D], BF16)
        self.gwc = at("gwc", [128, FC * 8], BF16)
        self.stat = at("stat", [128, 8], F32)
        self.gsc = at("gsc", [128, 8], F32)
        self.gsc2 = at("gsc2", [128, 8], F32)
        self.gb16 = at("gb16", [128, 8], BF16)
        self.grow = [at(f"grow_{i}", [8, 128], BF16) for i in range(2)]
        self.gate_all = at("gate_all", [128, NSEQ * self.TT * 8], F32)
        self.hsb = [at(f"hsb_{i}", [128, 512], BF16) for i in range(3)]
        self.xb16 = at("xb16", [128, D], BF16)
        self.rowtmp = at("rowtmp", [128, 256], F32)
        self.wst = [at(f"wst_{i}", [128, 512], F32) for i in range(3)]
        self.wstF = at("wstF", [128, 512], F32)
        self.npad = at("npad", [1, 1], F32)
        Ph.NOPAP = self.npad[0:1, 0:1]
        self.cst = at("cst", [128, 4], F32)
        for i, vv in enumerate([LNEPS, L2EPS, 1e-30, EPS1 / 8]):
            nc.const_aps.register(F32, vv, self.cst[:, i:i + 1])

        ap_ = nc.alloc_psum_tensor
        self.psP = [ap_(f"psP{i}", [128, 512], F32) for i in range(2)]
        self.psS = [ap_(f"psS{i}", [128, 512], F32) for i in range(2)]
        self.psV = ap_("psV", [128, 512], F32)
        self.psR = ap_("psR", [128, 512], F32)
        self.psO = ap_("psO", [128, 1024], F32)
        self.psOb = self.psO[:].bitcast(BF16)

        self.pe = Ev(nc, "pe", "tensor")
        self.pe2 = Ev(nc, "pe2", "tensor")
        self.act = Ev(nc, "act", "scalar")
        self.act2 = Ev(nc, "act2", "scalar")
        self.dve = Ev(nc, "dve", "vector")
        self.dve2 = Ev(nc, "dve2", "vector")
        self.pool = Ev(nc, "pool", "gpsimd")
        self.syn = Ev(nc, "syn", "sync", dma=True)
        self.scl = Ev(nc, "scl", "scalar", dma=True)
        self.gps = Ev(nc, "gps", "gpsimd", dma=True)
        self.gsm = Ev(nc, "gsm", "gpsimd", dma=True)
        self.gsh = Ev(nc, "gsh", "gpsimd", dma=True)
        self.all_dma = [self.syn, self.scl, self.gps, self.gsm, self.gsh]
        Ph.ENG_EV = {"vector": self.dve, "scalar": self.act,
                     "tensor": self.pe, "gpsimd": self.pool}

    def stage_w(self, R, dst, src, cols, part=128):
        if not hasattr(self, "wst_last"):
            self.wst_last = {0: 0, 1: 0, 2: 0}
            self._convq = []
        multi = len(src.shape) > 2
        inner = int(np.prod(src.shape[2:])) if multi else 1
        if multi:
            assert 512 % inner == 0 and cols == src.shape[1] * inner
        pieces = list(range(0, cols, 512))
        for a in pieces:
            c = min(512, cols - a)
            b = getattr(self, "wst_rr", 0)
            self.wst_rr = (b + 1) % 3
            if any(j[1] == b for j in self._convq):
                self.stage_fence(R)
            def dmafn(sy, b=b, a=a, c=c, src=src, part=part, multi=multi, inner=inner):
                if multi:
                    return sy.dma_start(
                        self.wst[b][0:part, 0:c].rearrange(
                            "p (x y) -> p x y", y=inner),
                        src[0:part, a // inner:(a + c) // inner])
                return sy.dma_start(self.wst[b][0:part, 0:c], src[0:part, a:a + c])
            R("sync", dmafn, waits=[(self.dve, self.wst_last[b])], sig=self.syn, n=16)
            tok = self.syn.n
            if part == 128 and c == 512:
                self._flush_convs(R, tok)
            self._convq.append((tok, b, dst, a, c, part))
        return None

    def _flush_convs(self, R, tok_f):
        rem = []
        for (tok_p, b, dst, a, c, part) in self._convq:
            if tok_p < tok_f:
                R("vector", lambda v, b=b, a=a, c=c, dst=dst, part=part:
                  v.tensor_copy(dst[0:part, a:a + c], self.wst[b][0:part, 0:c]),
                  waits=[(self.syn, tok_f)], sig=self.dve, ss=True)
                self.wst_last[b] = self.dve.n
            else:
                rem.append((tok_p, b, dst, a, c, part))
        self._convq = rem

    def stage_fence(self, R):
        if not getattr(self, "_convq", None):
            return self.dve.n
        R("sync", lambda sy: sy.dma_start(self.wstF[:], self.x[0, 0:128, 0:512]),
          sig=self.syn, n=16)
        self._flush_convs(R, self.syn.n)
        R("vector", lambda v: v.engine_nop(), sig=self.dve)
        return self.dve.n

    def vsl(self, t, h, n=65):
        return self.Vt[:, t * (H * 65) + 65 * h: t * (H * 65) + 65 * h + n]

    def build(self, upto="end", a2mode="full"):
        self.a2mode = a2mode
        self.setup()
        for s in range(self.NSEQ):
            self.stage_a(s)
            if upto == "A":
                continue
            self.attn(s, 1)
            if upto == "attn1":
                continue
            self.attn(s, 2)
        if upto in ("A", "attn1", "attn2"):
            self.finish()
            return self.nc
        for e in range(self.NE):
            self.moe_phase1(e)
            self.moe_phase2(e)
        self.stage_e()
        self.finish()
        return self.nc

    def setup(self):
        S = self.S
        ph = Ph(self.nc, "setup")
        R = ph.rec
        def mk_msets(g):
            g.memset(self.idn[:], 0.0)
            g.memset(self.idnf[:], 0.0)
            g.memset(self.U01[:], 0.0)
            return g.memset(self.L01[:], -1.0)
        R("gpsimd", mk_msets, sig=self.pool, ss=True)
        def mk_asels(g):
            g.affine_select(out=self.idn[:], in_=self.idn[:], compare_op=ALU.not_equal,
                            fill=1.0, base=0, pattern=[[-1, 128]], channel_multiplier=1)
            g.affine_select(out=self.idnf[:], in_=self.idnf[:], compare_op=ALU.not_equal,
                            fill=1.0, base=0, pattern=[[-1, 128]], channel_multiplier=1)
            g.affine_select(out=self.U01[:], in_=self.U01[:], compare_op=ALU.is_gt,
                            fill=1.0, base=0, pattern=[[-1, 128]], channel_multiplier=1)
            return g.affine_select(out=self.L01[:], in_=self.L01[:], compare_op=ALU.is_gt,
                                   fill=0.0, base=0, pattern=[[-1, 128]], channel_multiplier=1)
        R("gpsimd", mk_asels, sig=self.pool, ss=True)
        def mk_masks(g):
            g.memset(self.ones_c[:], 1.0)
            g.memset(self.onef_c[:], 1.0)
            g.memset(self.ones_r0[:], 1.0)
            g.memset(self.onef_r64[64:65, :], 1.0)
            g.memset(self.onef_r[:], 1.0)
            for i, vv in enumerate([LNEPS, L2EPS, 1e-30, EPS1 / 8]):
                g.memset(self.cst[:, i:i + 1], vv)
            g.memset(self.kb[0][64:65, :], -1.0)
            g.memset(self.kb[1][64:65, :], -1.0)
            ins = g.memset(
                self.Vt.rearrange("p (t h c) -> p t h c", h=H, c=65)[:, :, :, 64:65], 1.0)
            return ins
        R("gpsimd", mk_masks, sig=self.pool)

        for fc in range(FC):
            self.stage_w(R, self.w0[:, fc * D:(fc + 1) * D],
                         self.wk1.rearrange("n d h -> d (n h)")[fc * 128:(fc + 1) * 128]
                         if False else
                         self.wk1.rearrange("n d h -> d n h")[fc * 128:(fc + 1) * 128], D)
            self.stage_w(R, self.w1s[:, fc * D:(fc + 1) * D],
                         self.wq1.rearrange("n d h -> d n h")[fc * 128:(fc + 1) * 128], D)
            self.stage_w(R, self.w2s[:, fc * D:(fc + 1) * D],
                         self.wv1.rearrange("n d h -> d n h")[fc * 128:(fc + 1) * 128], D)

        self.stage_w(R, self.b2bf, self.b2, D, part=self.NE)
        for fc in range(FC):
            self.stage_w(R, self.gwc[:, fc * 8:(fc + 1) * 8],
                         self.gw[fc * 128:(fc + 1) * 128, :], 8)

        def rows(sy):
            sy.dma_start(self.t32[0][0:H, 0:DH], self.bk1).then_inc(self.syn.h, 16)
            sy.dma_start(self.t32[0][0:H, 64:64 + DH], self.bq1).then_inc(self.syn.h, 16)
            sy.dma_start(self.t32[0][0:H, 128:128 + DH],
                         self.qs2.rearrange("h o d -> h (o d)")).then_inc(self.syn.h, 16)
            sy.dma_start(self.t32[0][0:H, 192:192 + DH],
                         self.ks2.rearrange("h o d -> h (o d)")).then_inc(self.syn.h, 16)
            sy.dma_start(self.t32[0][32:33, 0:D], self.lng[None, :]).then_inc(self.syn.h, 16)
            sy.dma_start(self.t32[1][0:1, 0:D], self.lnb[None, :]).then_inc(self.syn.h, 16)
            sy.dma_start(self.sqscr[0:1, 0:D], self.bo1[None, :]).then_inc(self.syn.h, 16)
            sy.dma_start(self.sqscr[32:33, 0:D],
                         self.bv1.rearrange("h d -> (h d)")[None, :]).then_inc(self.syn.h, 16)
            ins = sy.dma_start(self.sqscr[64:65, 0:8], self.gb[None, :])
            return ins
        self.syn.n += 8 * 16
        R("sync", rows, sig=self.syn, n=16)

        self.stage_w(R, self.xb16[:, 0:128],
                     self.lng.rearrange("(a b) -> a b", b=128), 128, part=FC)
        self.stage_w(R, self.xb16[:, 128:256],
                     self.lnb.rearrange("(a b) -> a b", b=128), 128, part=FC)
        self.lnchunk_tok = self.stage_fence(R)
        ph.emit()

        ph = Ph(self.nc, "setup2")
        R = ph.rec
        R("tensor", lambda t: t.transpose(self.psR[0:DH, 0:128], self.t32[0][:, 0:DH], self.idnf[:]),
          waits=[self.syn], sig=self.pe)
        R("tensor", lambda t: t.transpose(self.psR[0:DH, 128:256],
                                          self.t32[0][:, 64:64 + DH], self.idnf[:]), sig=self.pe)
        R("tensor", lambda t: t.transpose(self.psR[0:DH, 256:384],
                                          self.t32[0][:, 128:128 + DH], self.idnf[:]), sig=self.pe)
        R("tensor", lambda t: t.transpose(self.psR[0:DH, 384:512],
                                          self.t32[0][:, 192:192 + DH], self.idnf[:]), sig=self.pe)
        R("tensor", lambda t: t.transpose(self.psV[:].bitcast(BF16)[:, 0:128],
                                          self.xb16[:, 0:128], self.idn[:]),
          waits=[self.dve], sig=self.pe)
        R("tensor", lambda t: t.transpose(self.psV[:].bitcast(BF16)[:, 128:256],
                                          self.xb16[:, 128:256], self.idn[:]), sig=self.pe)
        def bc(t, dst, src_row, base):
            return t.matmul(dst, self.onef_r[base:base + 1, :], src_row,
                            start=True, stop=True, skip_group_check=True)
        R("tensor", lambda t: bc(t, self.psO[:, 0:512], self.t32[0][32:33, 0:512], 32), sig=self.pe)
        R("tensor", lambda t: bc(t, self.psO[:, 512:1024], self.t32[0][32:33, 512:1024], 32), sig=self.pe)
        R("tensor", lambda t: bc(t, self.psP[0][:, 0:512], self.t32[1][0:1, 0:512], 0), sig=self.pe)
        R("tensor", lambda t: bc(t, self.psP[1][:, 0:512], self.t32[1][0:1, 512:1024], 0), sig=self.pe)
        R("tensor", lambda t: bc(t, self.psS[0][:, 0:512], self.sqscr[0:1, 0:512], 0), sig=self.pe)
        R("tensor", lambda t: bc(t, self.psS[1][:, 0:512], self.sqscr[0:1, 512:1024], 0), sig=self.pe)
        R("tensor", lambda t: bc(t, self.psV[:, 200:208], self.sqscr[64:65, 0:8], 64), sig=self.pe)

        def evacs(v):
            v.tensor_copy(self.bkc[:, :], self.psR[0:DH, 0:H])
            v.tensor_copy(self.bqc[:, :], self.psR[0:DH, 128:128 + H])
            v.tensor_copy(self.qsc[:, :], self.psR[0:DH, 256:256 + H])
            v.tensor_copy(self.ksc[:, :], self.psR[0:DH, 384:384 + H])
            v.tensor_copy(self.gcol[:, :], self.psV[:].bitcast(BF16)[:, 0:FC])
            v.tensor_copy(self.bcol[:, :], self.psV[:].bitcast(BF16)[:, 128:128 + FC])
            v.tensor_copy(self.g_bc[:, :], self.psO[:, :])
            v.tensor_copy(self.b_bc[:, 0:512], self.psP[0][:, :])
            v.tensor_copy(self.b_bc[:, 512:1024], self.psP[1][:, :])
            v.tensor_copy(self.bo_bc[:, 0:512], self.psS[0][:, :])
            v.tensor_copy(self.bo_bc[:, 512:1024], self.psS[1][:, :])
            return v.tensor_copy(self.gb_bc[:, :], self.psV[:, 200:208])
        R("vector", evacs, waits=[self.pe], sig=self.dve)
        R("tensor", lambda t: bc(t, self.psR[:, 0:512], self.sqscr[32:33, 0:512], 32),
          waits=[self.dve], sig=self.pe)
        R("vector", lambda v: v.tensor_copy(self.bv_bc[:, 0:512], self.psR[:, 0:512]),
          waits=[self.pe], sig=self.dve)
        R("tensor", lambda t: bc(t, self.psR[:, 0:512], self.sqscr[32:33, 512:1024], 32),
          waits=[self.dve], sig=self.pe)
        R("vector", lambda v: v.tensor_copy(self.bv_bc[:, 512:1024], self.psR[:, 0:512]),
          waits=[self.pe], sig=self.dve)
        ph.emit()

    def stage_a(self, s):
        S, TT = self.S, self.TT
        ph = Ph(self.nc, f"A{s}")
        R = ph.rec
        xtok = []
        evtok = []
        for tt in range(TT):
            b = tt % 2
            w = [(self.dve2, evtok[tt - 2][0])] if tt >= 2 else [(self.scl, self.scl.n)]
            R("sync", lambda sy, s=s, tt=tt, b=b:
              sy.dma_start(self.t32[b][:], self.x[s, tt * 128:(tt + 1) * 128, :]),
              waits=w, sig=self.syn, n=16)
            xtok.append(self.syn.n)
            R("scalar", lambda sc, b=b: sc.activation(self.sqscr[:], self.t32[b][:], AF.Square),
              waits=[(self.syn, xtok[tt]), (self.dve, self.dve.n)], sig=self.act)
            R("vector", lambda v, b=b: v.reduce_sum(self.stat[:, 0:1], self.t32[b][:], axis=AX),
              waits=[(self.syn, xtok[tt])], sig=self.dve, ss=True)
            R("vector", lambda v: v.reduce_sum(self.stat[:, 1:2], self.sqscr[:], axis=AX),
              waits=[self.act], sig=self.dve, ss=True)
            c = 1.0 / D
            R("vector", lambda v, c=c: v.tensor_scalar(self.stat[:, 2:3], self.stat[:, 0:1], c, None, ALU.mult),
              ss=True)
            R("vector", lambda v, c=c: v.tensor_scalar(self.stat[:, 3:4], self.stat[:, 1:2], c, None, ALU.mult),
              ss=True)
            R("vector", lambda v: v.tensor_tensor(self.stat[:, 4:5], self.stat[:, 2:3], self.stat[:, 2:3], ALU.mult),
              ss=True)
            R("vector", lambda v: v.tensor_tensor(self.stat[:, 5:6], self.stat[:, 3:4], self.stat[:, 4:5], ALU.subtract),
              sig=self.dve, ss=True)
            R("scalar", lambda sc: sc.activation(self.stat[:, 6:7], self.stat[:, 5:6], AF.Ln, bias=LNEPS),
              waits=[self.dve], ss=True)
            R("scalar", lambda sc: sc.activation(self.stat[:, 6:7], self.stat[:, 6:7], AF.Exp, scale=-0.5),
              sig=self.act, ss=True, nop=True)
            R("vector", lambda v: v.tensor_tensor(self.stat[:, 7:8], self.stat[:, 2:3], self.stat[:, 6:7], ALU.mult),
              waits=[self.act], ss=True)
            R("vector", lambda v: v.tensor_scalar(self.stat[:, 7:8], self.stat[:, 7:8], -1.0, None, ALU.mult),
              sig=self.dve, ss=True)
            if getattr(self, "a_noln", False):
                R("vector", lambda v, b=b: v.tensor_copy(self.xb16[:], self.t32[b][:]),
                  waits=[self.pe, self.act], sig=self.dve2)
            else:
                R("vector", lambda v, b=b: v.tensor_scalar(
                    self.xb16[:], self.t32[b][:], self.stat[:, 6:7], self.stat[:, 7:8],
                    ALU.mult, ALU.add),
                  waits=[self.pe, self.act], sig=self.dve2, ss=True, nop=True)
            evtok.append((self.dve2.n, self.dve.n))
            def trs(t, tt=tt):
                for fc in range(FC):
                    ins = t.transpose(self.psOb[:, fc * 128:(fc + 1) * 128],
                                      self.xb16[:, fc * 128:(fc + 1) * 128], self.idn[:])
                return ins
            R("tensor", trs, waits=[self.dve2], sig=self.pe)
            R("vector", lambda v, tt=tt: v.tensor_copy(
                self.xnh.rearrange("p (fc s) -> p fc s", fc=FC)[:, :, tt * 128:(tt + 1) * 128],
                self.psOb[:, 0:FC * 128].rearrange("p (fc j) -> p fc j", fc=FC)),
              waits=[self.pe], sig=self.dve2)
        if getattr(self, "dbg", False) and s == 0:
            R("gpsimd", lambda g: g.dma_start(self.dbgd[7], self.xnh[:, 0:self.S]),
              waits=[self.dve2], sig=self.gsh, n=16)
        ph.emit()

    def attn(self, s, kind):
        a1 = kind == 1
        S, TT, QC = self.S, self.TT, self.QC
        nc = self.nc

        if not a1:
            ph = Ph(nc, f"hconv{s}")
            def hconv(v):
                for fc in range(FC):
                    ins = v.tensor_scalar(
                        self.xnh[:, fc * S:(fc + 1) * S], self.xnh[:, fc * S:(fc + 1) * S],
                        self.gcol[:, fc:fc + 1], self.bcol[:, fc:fc + 1], ALU.mult, ALU.add)
                return ins
            ph.rec("vector", hconv, sig=self.dve)
            for fc in range(FC):
                self.stage_w(ph.rec, self.w0[:, fc * D:(fc + 1) * D],
                             self.wk2[fc * 128:(fc + 1) * 128, :], D)
                self.stage_w(ph.rec, self.w1s[:, fc * D:(fc + 1) * D],
                             self.wq2[fc * 128:(fc + 1) * 128, :], D)
                self.stage_w(ph.rec, self.w2s[:, fc * D:(fc + 1) * D],
                             self.wv2[fc * 128:(fc + 1) * 128, :], D)
            self.stage_fence(ph.rec)
            ph.emit()
            if getattr(self, "dbg_stall", False):
                with nc.Block() as blk:
                    @blk.gpsimd
                    def _(g):
                        for _i in range(40):
                            g.memset(self.t32[0][:], 0.0)
        elif s == 1:
            ph = Ph(nc, "w1reload")
            for fc in range(FC):
                self.stage_w(ph.rec, self.w0[:, fc * D:(fc + 1) * D],
                             self.wk1.rearrange("n d h -> d n h")[fc * 128:(fc + 1) * 128], D)
                self.stage_w(ph.rec, self.w1s[:, fc * D:(fc + 1) * D],
                             self.wq1.rearrange("n d h -> d n h")[fc * 128:(fc + 1) * 128], D)
                self.stage_w(ph.rec, self.w2s[:, fc * D:(fc + 1) * D],
                             self.wv1.rearrange("n d h -> d n h")[fc * 128:(fc + 1) * 128], D)
            self.stage_fence(ph.rec)
            ph.emit()
        wready = self.dve.n

        ph = Ph(nc, f"attn{kind}_{s}")
        R = ph.rec

        vev = []
        for tt in range(TT):
            pp = [self.psP[tt % 2], self.psS[tt % 2]]
            w = [(self.dve, wready)]
            if tt >= 2:
                w.append((self.dve, vev[tt - 2]))
            def vproj(t, tt=tt, pp=pp):
                for half in range(2):
                    for fc in range(FC):
                        ins = t.matmul(
                            pp[half][:],
                            self.xnh[:, fc * S + tt * 128: fc * S + (tt + 1) * 128],
                            self.w2s[:, fc * D + half * 512: fc * D + (half + 1) * 512],
                            start=(fc == 0), stop=(fc == FC - 1))
                return ins
            R("tensor", vproj, waits=w, sig=self.pe)
            def vevac(v, tt=tt, pp=pp, a1=a1):
                for h in range(H):
                    src = pp[h // 8][:, (h % 8) * 64:(h % 8) * 64 + 64]
                    dst = self.vsl(tt, h, 64)
                    if a1:
                        ins = v.tensor_tensor(dst, src, self.bv_bc[:, h * 64:(h + 1) * 64], ALU.add)
                    else:
                        ins = v.tensor_copy(dst, src)
                return ins
            R("vector", vevac, waits=[self.pe], sig=self.dve)
            vev.append(self.dve.n)
        vdone_dve = self.dve.n

        first = True
        for fc in range(FC):
            if a1:
                src = self.wo1.rearrange("h d e -> (h d) e")[fc * 128:(fc + 1) * 128, :]
            else:
                src = self.wo2[fc * 128:(fc + 1) * 128, :]
            if first:
                R("vector", lambda v: v.engine_nop(), waits=[(self.dve, vdone_dve, True)],
                  sig=self.dve)
                for bb in range(3):
                    self.wst_last[bb] = self.dve.n
                first = False
            self.stage_w(R, self.w2s[:, fc * D:(fc + 1) * D], src, D)
        self.stage_fence(R)
        wo_ready = self.dve.n

        self.vdone_dve = vdone_dve
        self.wf_guard = self.scl.n
        wf_scl = []
        nheads = 0 if (not a1 and getattr(self, "a2mode", "full") == "nohead") else H
        if not a1 and getattr(self, "dbg_nheads", None):
            nheads = self.dbg_nheads
        if a1 and getattr(self, "dbg_nheads1", None):
            nheads = self.dbg_nheads1
        self.hstate = {}
        for h in range(nheads):
            if h == 0:
                self.rec_prep(R, s, kind, 0)
            if h + 1 < nheads:
                self.rec_prep(R, s, kind, h + 1)
            self.rec_main(R, s, kind, h, wf_scl)

        ph.emit()

        if getattr(self, "dbg_nheads1", None) and a1:
            pass
        elif a1 or getattr(self, "a2mode", "full") not in ("noout", "nohead"):
            self.attn_out(s, kind, wo_ready, wf_scl)
        self.y_attn_tok = self.scl.n

    def rec_prep(self, R, s, kind, h):
        a1 = kind == 1
        S, TT, QC = self.S, self.TT, self.QC
        i = h % 2
        kb, qb = self.kb[i], self.qb[i]
        tr = self.TR if i == 0 else self.t32[0][:].bitcast(BF16)
        if not hasattr(self, "headuse"):
            self.headuse = {}
        guard = []
        if i in self.headuse:
            guard = [(self.pe, self.headuse[i])]

        if getattr(self, "dbg", False) and kind == 2 and h == 0 and getattr(self, "dbg_wdump", False):
            R("vector", lambda v: v.tensor_copy(
                self.Creg[:, 4096:4096 + 8 * S], self.xnh[:, 0:8 * S]),
              waits=[(self.dve, self.vdone_dve)], sig=self.dve, ss=True)

        if getattr(self, "dbg_projonly", False) and kind == 2 and h == 0:
            for side, (wslot, psp) in enumerate([(self.w0, self.psP[0]), (self.w1s, self.psP[1])]):
                def projX(t, wslot=wslot, psp=psp, h=h):
                    for fc in range(FC):
                        ins = t.matmul(
                            psp[0:64, :],
                            wslot[:, fc * D + h * 64: fc * D + h * 64 + 64],
                            self.xnh[:, fc * S: fc * S + 512],
                            start=(fc == 0), stop=(fc == FC - 1))
                    return ins
                wX = ([self.dve2, (self.dve, self.dve.n)]
                      if getattr(self, "dbg_wowait", False)
                      else [self.dve2, (self.dve, self.vdone_dve)])
                R("tensor", projX, waits=wX, sig=self.pe)
            R("vector", lambda v: v.tensor_copy(self.n32[0:64, 0:512], self.psP[0][0:64, :]),
              waits=[self.pe], sig=self.dve, ss=True)
            R("vector", lambda v: v.tensor_copy(self.n32[0:64, 512:1024], self.psP[1][0:64, :]),
              sig=self.dve, ss=True)
            R("sync", lambda sy: sy.dma_start(self.dbgf[0, 0:64, :], self.n32[0:64, 0:512]),
              waits=[self.dve], sig=self.syn, n=16)
            R("sync", lambda sy: sy.dma_start(self.dbgf[1, 0:64, :], self.n32[0:64, 512:1024]),
              waits=[self.dve], sig=self.syn, n=16)
            if getattr(self, "dbg_slivers", True):
                for fc in range(FC):
                    R("sync", lambda sy, fc=fc: sy.dma_start(
                        self.dbgd[0, :, fc * 64:(fc + 1) * 64],
                        self.w0[:, fc * D + h * 64: fc * D + h * 64 + 64]),
                      waits=[self.dve], sig=self.syn, n=16)
                    R("sync", lambda sy, fc=fc: sy.dma_start(
                        self.dbgd[1, :, fc * 64:(fc + 1) * 64],
                        self.w1s[:, fc * D + h * 64: fc * D + h * 64 + 64]),
                      waits=[self.dve], sig=self.syn, n=16)
            return

        for side, (wslot, psp, bias_c, sc_c) in enumerate(
                [(self.w0, self.psP[0], self.bkc, self.ksc),
                 (self.w1s, self.psP[1], self.bqc, self.qsc)]):
            dstb = kb if side == 0 else qb
            for qc in range(QC):
                def proj(t, wslot=wslot, psp=psp, qc=qc, h=h):
                    for fc in range(FC):
                        ins = t.matmul(
                            psp[0:64, :],
                            wslot[:, fc * D + h * 64: fc * D + h * 64 + 64],
                            self.xnh[:, fc * S + qc * 512: fc * S + (qc + 1) * 512],
                            start=(fc == 0), stop=(fc == FC - 1))
                    return ins
                R("tensor", proj,
                  waits=guard + [self.dve2, (self.dve, self.vdone_dve)], sig=self.pe)
                if a1:
                    R("vector", lambda v, psp=psp, dstb=dstb, qc=qc, bias_c=bias_c, h=h:
                      v.tensor_scalar(dstb[0:64, qc * 512:(qc + 1) * 512], psp[0:64, :],
                                      bias_c[:, h:h + 1], None, ALU.add),
                      waits=[self.pe], sig=self.dve2, ss=True, nop=True)
                else:
                    R("scalar", lambda sc, psp=psp: sc.activation(
                        self.sq_sc[0:64, :], psp[0:64, :], AF.Square),
                      waits=[self.pe, self.pe2], sig=self.act)
                    R("tensor", lambda t: t.matmul(
                        self.psR[0:1, :], self.ones_c[0:64, 0:1], self.sq_sc[0:64, :],
                        start=True, stop=True, skip_group_check=True),
                      waits=[self.act, self.dve2], sig=self.pe2)
                    R("scalar", lambda sc: sc.activation(self.r0a[:], self.psR[0:1, :],
                                                         AF.Ln, bias=L2EPS),
                      waits=[self.pe2], sig=self.act, ss=True)
                    R("scalar", lambda sc: sc.activation(self.r0b[:], self.r0a[:],
                                                         AF.Exp, scale=-0.5),
                      sig=self.act, ss=True, nop=True)
                    R("tensor", lambda t: t.matmul(
                        self.psR[0:64, :], self.ones_r0[0:1, 0:64], self.r0b[:],
                        start=True, stop=True, skip_group_check=True),
                      waits=[self.act], sig=self.pe2)
                    R("vector", lambda v: v.tensor_copy(self.rbc_sb[:], self.psR[0:64, :]),
                      waits=[self.pe2], sig=self.dve, ss=True)
                    R("vector", lambda v, psp=psp, dstb=dstb, qc=qc, sc_c=sc_c, h=h:
                      v.scalar_tensor_tensor(
                          dstb[0:64, qc * 512:(qc + 1) * 512], psp[0:64, :],
                          sc_c[:, h:h + 1], self.rbc_sb[:], ALU.mult, ALU.mult),
                      sig=self.dve2, ss=True, nop=True)

        suf_dve = 0
        if a1:
            sufb = self.sufB if i == 0 else self.t32[1][:].bitcast(BF16)
            def suf_slot(t):
                if t < 7:
                    return self.psR[64:65, t * 65:(t + 1) * 65]
                if t < 14:
                    return self.psO[64:65, (t - 7) * 65:(t - 6) * 65]
                return self.psO[64:65, 512:577]
            def sufmms(t):
                for ts in range(TT - 1):
                    for tp in range(ts + 1, TT):
                        ins = t.matmul(suf_slot(ts), self.ones_c[:, 0:1],
                                       self.vsl(tp, h), start=(tp == ts + 1),
                                       stop=(tp == TT - 1), skip_group_check=True)
                return ins
            R("tensor", sufmms, waits=[self.act, self.act2, self.dve], sig=self.pe2)
            def sufevac(v, sufb=sufb):
                n1 = min(TT - 1, 7)
                v.tensor_copy(sufb[64:65, 0:n1 * 65], self.psR[64:65, 0:n1 * 65])
                if TT - 1 > 7:
                    n2 = min(TT - 1, 14) - 7
                    v.tensor_copy(sufb[64:65, 455:455 + n2 * 65],
                                  self.psO[64:65, 0:n2 * 65])
                if TT - 1 > 14:
                    v.tensor_copy(sufb[64:65, 910:975], self.psO[64:65, 512:577])
                return v.memset(sufb[64:65, (TT - 1) * 65:TT * 65], 0.0)
            R("vector", sufevac, waits=[self.pe2], sig=self.dve, ss=True, nop=True)
            suf_dve = self.dve.n

        spots = [self.psR[64:65, 0:512], self.psO[64:65, 0:512],
                 self.psO[64:65, 512:1024], self.psR[64:65, 0:512]]
        ktok = []
        for qc in range(QC):
            R("scalar", lambda sc, qc=qc: sc.activation(
                self.sq_sc[0:64, :], kb[0:64, qc * 512:(qc + 1) * 512], AF.Square),
              waits=[self.dve2, self.pe2], sig=self.act)
            kw = [self.act, self.dve]
            if qc == 3:
                kw.append((self.dve, ktok[0]))
            R("tensor", lambda t, qc=qc: t.matmul(
                spots[qc], self.ones_c[0:64, 0:1], self.sq_sc[0:64, :],
                start=True, stop=True, skip_group_check=True),
              waits=kw, sig=self.pe2)
            R("vector", lambda v, qc=qc: v.reduce_max(
                self.mk[64:65, qc:qc + 1], spots[qc], axis=AX),
              waits=[self.pe2], sig=self.dve, ss=True)
            ktok.append(self.dve.n)
        if QC == 1:
            R("vector", lambda v: v.tensor_copy(self.mk[64:65, 6:7], self.mk[64:65, 0:1]),
              sig=self.dve, ss=True, nop=True)
        elif QC == 2:
            R("vector", lambda v: v.tensor_tensor(self.mk[64:65, 6:7], self.mk[64:65, 0:1],
                                                  self.mk[64:65, 1:2], ALU.max),
              sig=self.dve, ss=True, nop=True)
        else:
            R("vector", lambda v: v.tensor_tensor(self.mk[64:65, 4:5], self.mk[64:65, 0:1],
                                                  self.mk[64:65, 1:2], ALU.max),
              sig=self.dve, ss=True)
            R("vector", lambda v: v.tensor_tensor(self.mk[64:65, 5:6], self.mk[64:65, 2:3],
                                                  self.mk[64:65, 3:4], ALU.max),
              sig=self.dve, ss=True)
            R("vector", lambda v: v.tensor_tensor(self.mk[64:65, 6:7], self.mk[64:65, 4:5],
                                                  self.mk[64:65, 5:6], ALU.max),
              sig=self.dve, ss=True, nop=True)
        mk_dve = self.dve.n
        sqrt_tok = []
        for qc in range(QC):
            R("scalar", lambda sc, qc=qc: sc.activation(
                self.sq_sc[0:64, :], qb[0:64, qc * 512:(qc + 1) * 512], AF.Square),
              waits=[self.pe2], sig=self.act)
            qw = [self.act, (self.dve, ktok[qc])]
            if qc == 3:
                qw.append((self.act2, sqrt_tok[0]))
            R("tensor", lambda t, qc=qc: t.matmul(
                spots[qc], self.ones_c[0:64, 0:1], self.sq_sc[0:64, :],
                start=True, stop=True, skip_group_check=True),
              waits=qw, sig=self.pe2)
            R("scalar", lambda sc, qc=qc: sc.activation(
                qb[64:65, qc * 512:(qc + 1) * 512], spots[qc],
                AF.Sqrt, scale=self.mk[64:65, 6:7]),
              waits=[self.pe2, (self.dve, mk_dve)], sig=self.act2, ss=True, nop=True)
            sqrt_tok.append(self.act2.n)
            if a1:
                R("scalar", lambda sc, qc=qc, tr=tr: sc.activation(
                    tr[64:65, qc * 512:(qc + 1) * 512],
                    qb[64:65, qc * 512:(qc + 1) * 512],
                    AF.Exp, scale=-0.125, bias=EPS1 / 8), sig=self.act2, ss=True, nop=True)
        brow_tok = (self.act2.n, self.pe2.n)
        self.hstate[h] = (brow_tok, suf_dve)

    def rec_main(self, R, s, kind, h, wf_scl):
        a1 = kind == 1
        S, TT, QC = self.S, self.TT, self.QC
        i = h % 2
        if h not in self.hstate:
            return
        kb, qb = self.kb[i], self.qb[i]
        tr = self.TR if i == 0 else self.t32[0][:].bitcast(BF16)
        sufb = self.sufB if i == 0 else self.t32[1][:].bitcast(BF16)
        brow_tok, suf_dve = self.hstate[h]

        scale = 0.125 if a1 else 10.0
        np_ = 0
        if not hasattr(self, "psS_last"):
            self.psS_last = {}
        if not hasattr(self, "pt_last"):
            self.pt_last = {}
        if not hasattr(self, "dsc_par"):
            self.dsc_par = 0
        for qc in range(QC):
            NT = 4 * qc + 4
            meta = []
            for kt in range(NT):
                j = kt - 4 * qc
                col0 = max(0, j * 128)
                meta.append((kt, j, col0, 512 - col0, kt % 2, np_ % 3))
                np_ += 1
            exp_tok = [0] * NT
            fix_tok = [0] * NT

            def rec_S(i, qc=qc, meta=meta, exp_tok=exp_tok):
                kt, j, col0, ncol, b, r = meta[i]
                w = [(self.act2, brow_tok[0])]
                if b in self.psS_last:
                    w.append((self.act2, self.psS_last[b]))
                def smm(t, kt=kt, b=b, qc=qc, col0=col0, ncol=ncol):
                    return t.matmul(
                        self.psS[b][:, col0:col0 + ncol],
                        kb[0:65, kt * 128:(kt + 1) * 128],
                        qb[0:65, qc * 512 + col0: qc * 512 + col0 + ncol],
                        start=True, stop=True)
                R("tensor", smm, waits=w, sig=self.pe)
                we = [self.pe]
                if r in self.pt_last:
                    we.append((self.pe, self.pt_last[r]))
                R("scalar", lambda sc, b=b, r=r, col0=col0, ncol=ncol, scale=scale:
                  sc.activation(self.hsb[r][:, 0:ncol], self.psS[b][:, col0:col0 + ncol],
                                AF.Exp, scale=scale),
                  waits=we, sig=self.act2, nop=True)
                self.psS_last[b] = self.act2.n
                exp_tok[i] = self.act2.n

            def rec_fix(i, qc=qc, meta=meta, exp_tok=exp_tok, fix_tok=fix_tok):
                kt, j, col0, ncol, b, r = meta[i]
                if a1:
                    dp = self.dsc_par
                    self.dsc_par ^= 1
                    R("tensor", lambda t, qc=qc, j=j, tr=tr: t.matmul(
                        self.psR[0:128, 0:128], kb[64:65, 0:128],
                        tr[64:65, qc * 512 + j * 128: qc * 512 + j * 128 + 128],
                        start=True, stop=True, skip_group_check=True),
                      waits=[self.act, self.act2, self.dve], sig=self.pe2)
                    R("vector", lambda v, r=r: v.tensor_tensor(
                        self.hsb[r][:, 0:128], self.hsb[r][:, 0:128], self.U01[:], ALU.mult),
                      waits=[(self.act2, exp_tok[i])], sig=self.dve)
                    R("vector", lambda v, dp=dp: v.tensor_tensor(
                        self.dsc[:, dp * 128:(dp + 1) * 128],
                        self.psR[0:128, 0:128], self.L01[:], ALU.mult),
                      waits=[self.pe2], sig=self.dve)
                    R("vector", lambda v, r=r, dp=dp: v.tensor_tensor(
                        self.hsb[r][:, 0:128], self.hsb[r][:, 0:128],
                        self.dsc[:, dp * 128:(dp + 1) * 128], ALU.add),
                      sig=self.dve, ss=True, nop=True)
                else:
                    R("vector", lambda v, r=r: v.tensor_tensor(
                        self.hsb[r][:, 0:128], self.hsb[r][:, 0:128], self.U01[:], ALU.mult),
                      waits=[(self.act2, exp_tok[i])], sig=self.dve, ss=True, nop=True)
                fix_tok[i] = self.dve.n

            def rec_pv_clean(i, qc=qc, meta=meta, exp_tok=exp_tok):
                kt, j, col0, ncol, b, r = meta[i]
                off = 128 if j >= 0 else 0
                if ncol <= off:
                    return
                w = [(self.act2, exp_tok[i])]
                if i == 0:
                    w.append(self.dve2)
                R("tensor", lambda t, kt=kt, r=r, col0=col0, off=off, ncol=ncol, i=i:
                  t.matmul(self.psV[0:65, col0 + off:col0 + ncol],
                           self.vsl(kt, h), self.hsb[r][:, off:ncol],
                           start=(i == 0), stop=False, skip_group_check=True),
                  waits=w, sig=self.pe)
                if off == 0:
                    self.pt_last[r] = self.pe.n

            def rec_pv_diag(i, qc=qc, meta=meta, fix_tok=fix_tok):
                kt, j, col0, ncol, b, r = meta[i]
                R("tensor", lambda t, kt=kt, r=r, col0=col0:
                  t.matmul(self.psV[0:65, col0:col0 + 128],
                           self.vsl(kt, h), self.hsb[r][:, 0:128],
                           start=False, stop=False, skip_group_check=True),
                  waits=[(self.dve, fix_tok[i])], sig=self.pe)
                self.pt_last[r] = self.pe.n

            rec_S(0)
            pend = None
            for i in range(NT):
                if i + 1 < NT:
                    rec_S(i + 1)
                if meta[i][1] >= 0:
                    rec_fix(i)
                rec_pv_clean(i)
                if pend is not None:
                    rec_pv_diag(pend)
                    pend = None
                if meta[i][1] >= 0:
                    pend = i
            if pend is not None:
                rec_pv_diag(pend)
            if a1:
                for j2 in range(4):
                    R("tensor", lambda t, qc=qc, j2=j2, tr=tr, sufb=sufb: t.matmul(
                        self.psV[0:65, j2 * 128:(j2 + 1) * 128],
                        sufb[64:65, (4 * qc + j2) * 65:(4 * qc + j2 + 1) * 65],
                        tr[64:65, qc * 512 + j2 * 128: qc * 512 + (j2 + 1) * 128],
                        start=False, stop=(j2 == 3), skip_group_check=True),
                      waits=[(self.dve, suf_dve)], sig=self.pe)
            pv_done = self.pe.n
            R("vector", lambda v: v.reciprocal(self.recrow[64:65, :], self.psV[64:65, :]),
              waits=[(self.pe, pv_done)], sig=self.dve, ss=True, nop=True)
            R("tensor", lambda t: t.matmul(
                self.psR[0:64, :], self.onef_r64[64:65, 0:64], self.recrow[64:65, :],
                start=True, stop=True, skip_group_check=True),
              waits=[self.dve, self.act2], sig=self.pe2)
            R("vector", lambda v: v.tensor_copy(self.rb32[:], self.psR[0:64, :]),
              waits=[self.pe2], sig=self.dve, ss=True)
            R("vector", lambda v: v.tensor_tensor(
                self.wnorm[:], self.psV[0:64, :], self.rb32[:], ALU.mult),
              sig=self.dve, ss=True, nop=True)
            cbuf = (h // 2) % 2
            wfw = [(self.scl, wf_scl[-2])] if len(wf_scl) >= 2 else                   [(self.scl, self.wf_guard)] if hasattr(self, "wf_guard") else []
            if h % 2 == 0:
                R("vector", lambda v, cbuf=cbuf, qc=qc: v.tensor_copy(
                    self.wfb[cbuf][0:64, qc * 512:(qc + 1) * 512], self.wnorm[:]),
                  waits=wfw, sig=self.dve2)
            else:
                R("gpsimd", lambda g, cbuf=cbuf, qc=qc: g.dma_start(
                    self.wfb[cbuf][64:128, qc * 512:(qc + 1) * 512], self.wnorm[:]),
                  waits=wfw + [self.dve], sig=self.gsh, n=16)
                R("vector", lambda v: v.engine_nop(), waits=[self.gsh], sig=self.dve2)
        if getattr(self, "dbg", False) and kind == 2 and h == 0 and getattr(self, "dbg_wdump", False):
            if getattr(self, "dbg_wdump", 0) == 2:
                for sl in range(8):
                    R("sync", lambda sy, sl=sl: sy.dma_start(
                        self.dbgd[sl, :, 0:512], self.Creg[:, 4096 + sl * 512:4096 + (sl + 1) * 512]),
                      waits=[self.dve2, self.dve], sig=self.syn, n=16)
            else:
                for sl in range(6):
                    R("sync", lambda sy, sl=sl: sy.dma_start(
                        self.dbgd[2 + sl, :, 0:512], self.w0[:, sl * 512:(sl + 1) * 512]),
                      waits=[self.dve2], sig=self.syn, n=16)
            R("vector", lambda v: v.tensor_copy(self.n32[0:64, 0:512], self.psP[0][0:64, :]),
              waits=[self.dve2, self.pe, self.pe2], sig=self.dve, ss=True)
            R("vector", lambda v: v.tensor_copy(self.n32[0:64, 512:1024], self.psP[1][0:64, :]),
              sig=self.dve, ss=True)
            R("sync", lambda sy: sy.dma_start(self.dbgf[0, 0:64, :], self.n32[0:64, 0:512]),
              waits=[self.dve], sig=self.syn, n=16)
            R("sync", lambda sy: sy.dma_start(self.dbgf[1, 0:64, :], self.n32[0:64, 512:1024]),
              waits=[self.dve], sig=self.syn, n=16)
        if getattr(self, "dbg", False) and h == 0 and kind == getattr(self, "dbgkind", 1):
            R("sync", lambda sy: sy.dma_start(self.dbgd[0, 0:65, :], kb[0:65, :]),
              waits=[self.dve2, self.act2], sig=self.syn, n=16)
            R("sync", lambda sy: sy.dma_start(self.dbgd[1, 0:65, :], qb[0:65, :]),
              waits=[self.dve2, self.act2], sig=self.syn, n=16)
            R("sync", lambda sy: sy.dma_start(self.dbgd[2, 64:65, :], self.TR[64:65, :]),
              waits=[self.act2], sig=self.syn, n=16)
            R("sync", lambda sy: sy.dma_start(self.dbgd[3, 64:65, 0:self.TT * 65], self.sufB[64:65, :]),
              waits=[self.dve], sig=self.syn, n=16)
            R("sync", lambda sy: sy.dma_start(self.dbgd[4, 64:65, 0:512], self.wnorm[0:1, :]),
              waits=[self.dve], sig=self.syn, n=16)
            R("sync", lambda sy: sy.dma_start(self.dbgd[6, 0:64, 0:512], self.wnorm[:]),
              waits=[self.dve], sig=self.syn, n=16)
            R("sync", lambda sy: sy.dma_start(
                self.dbgd[7], self.xnh[:, 0:self.S]),
              waits=[self.dve2], sig=self.syn, n=16)
        self.headuse[i] = self.pe.n
        if h % 2 == 1:
            cbuf = (h // 2) % 2
            c = h // 2
            R("scalar", lambda sc, c=c, cbuf=cbuf: sc.dma_start(
                self.wfd[c], self.wfb[cbuf]),
              waits=[self.dve2, self.gsh], sig=self.scl, n=16)
            wf_scl.append(self.scl.n)

    def attn_out(self, s, kind, wo_ready, wf_scl):
        a1 = kind == 1
        S, TT = self.S, self.TT
        ph = Ph(self.nc, f"aout{kind}_{s}")
        R = ph.rec
        optok = []
        petok = []
        a1tok = []
        for tt in range(TT):
            b = tt % 2
            w = [(self.scl, wf_scl[-1])]
            if tt >= 2:
                w.append((self.pe, petok[tt - 2]))
            R("sync", lambda sy, tt=tt, b=b: sy.dma_start(
                self.opj[b][:].rearrange("p (c j) -> p c j", c=FC),
                self.wfd.rearrange("c p s -> p c s")[:, :, tt * 128:(tt + 1) * 128]),
              waits=w, sig=self.syn, n=16)
            optok.append(self.syn.n)
            if a1:
                xw = [(self.dve, a1tok[tt - 2]), (self.scl, self.scl.n)] if tt >= 2 else []
                R("sync", lambda sy, tt=tt, b=b: sy.dma_start(
                    self.t32[b][:], self.x[s, tt * 128:(tt + 1) * 128, :]),
                  waits=xw, sig=self.syn, n=16)
            wmm = [(self.syn, optok[tt] + (16 if a1 else 0)), (self.dve, wo_ready)]
            if tt >= 1:
                wmm.append(self.dve)
                wmm.append(self.dve2)
            def omm(t, b=b):
                for half in range(2):
                    for c in range(FC):
                        ins = t.matmul(self.psO[:, half * 512:(half + 1) * 512],
                                       self.opj[b][:, c * 128:(c + 1) * 128],
                                       self.w2s[:, c * D + half * 512: c * D + (half + 1) * 512],
                                       start=(c == 0), stop=(c == FC - 1))
                return ins
            R("tensor", omm, waits=wmm, sig=self.pe)
            petok.append(self.pe.n)
            if not a1 and getattr(self, "a2mode", "full") == "plainout":
                R("vector", lambda v: v.tensor_tensor(self.n32[:], self.psO[:], self.bo_bc[:], ALU.add),
                  waits=[self.pe], sig=self.dve)
                continue
            if a1:
                R("vector", lambda v: v.tensor_tensor(self.n32[:], self.psO[:], self.bo_bc[:], ALU.add),
                  waits=[self.pe], sig=self.dve)
                R("vector", lambda v, b=b: v.tensor_tensor(
                    self.t32[b][:], self.n32[:], self.t32[b][:], ALU.add),
                  waits=[(self.scl, self.scl.n), (self.syn, optok[tt] + 16)],
                  sig=self.dve, ss=True, nop=True)
                a1tok.append(self.dve.n)
                R("scalar", lambda sc, s=s, tt=tt, b=b: sc.dma_start(
                    self.y[s, tt * 128:(tt + 1) * 128, :], self.t32[b][:]),
                  waits=[self.dve], sig=self.scl, n=16)
            else:
                self.rec_h2_gate(R, s, tt, b)
        ph.emit()

    def rec_h2_gate(self, R, s, tt, b):
        S = self.S
        R("scalar", lambda sc: sc.activation(self.sqscr[:], self.psO[:], AF.Square),
          waits=[self.pe], sig=self.act)
        R("vector", lambda v: v.reduce_sum(self.stat[:, 0:1], self.psO[:], axis=AX),
          waits=[self.pe, self.act], sig=self.dve, ss=True)
        R("vector", lambda v: v.reduce_sum(self.stat[:, 1:2], self.sqscr[:], axis=AX),
          sig=self.dve, ss=True)
        c = 1.0 / D
        R("vector", lambda v, c=c: v.tensor_scalar(self.stat[:, 2:3], self.stat[:, 0:1], c, None, ALU.mult),
          ss=True)
        R("vector", lambda v, c=c: v.tensor_scalar(self.stat[:, 3:4], self.stat[:, 1:2], c, None, ALU.mult),
          ss=True)
        R("vector", lambda v: v.tensor_tensor(self.stat[:, 4:5], self.stat[:, 2:3], self.stat[:, 2:3], ALU.mult),
          ss=True)
        R("vector", lambda v: v.tensor_tensor(self.stat[:, 5:6], self.stat[:, 3:4], self.stat[:, 4:5], ALU.subtract),
          sig=self.dve, ss=True)
        R("scalar", lambda sc: sc.activation(self.stat[:, 6:7], self.stat[:, 5:6], AF.Ln, bias=LNEPS),
          waits=[self.dve], ss=True)
        R("scalar", lambda sc: sc.activation(self.stat[:, 6:7], self.stat[:, 6:7], AF.Exp, scale=-0.5),
          sig=self.act, ss=True, nop=True)
        R("vector", lambda v: v.tensor_tensor(self.stat[:, 7:8], self.stat[:, 2:3], self.stat[:, 6:7], ALU.mult),
          waits=[self.act], ss=True)
        R("vector", lambda v: v.tensor_scalar(self.stat[:, 7:8], self.stat[:, 7:8], -1.0, None, ALU.mult),
          sig=self.dve, ss=True)
        R("vector", lambda v: v.tensor_scalar(self.n32[:], self.psO[:],
                                              self.stat[:, 6:7], self.stat[:, 7:8],
                                              ALU.mult, ALU.add),
          sig=self.dve)
        R("vector", lambda v: v.tensor_tensor(self.sqscr[:], self.n32[:], self.g_bc[:], ALU.mult))
        R("vector", lambda v: v.tensor_tensor(self.xb16[:], self.sqscr[:], self.b_bc[:], ALU.add),
          waits=[self.pe], sig=self.dve)
        if getattr(self, "a2mode", "full") == "h2a":
            return
        def trs(t):
            for fc in range(FC):
                ins = t.transpose(self.psOb[:, fc * 128:(fc + 1) * 128],
                                  self.xb16[:, fc * 128:(fc + 1) * 128], self.idn[:])
            return ins
        R("tensor", trs, waits=[self.dve, self.dve2], sig=self.pe)
        R("vector", lambda v: v.tensor_copy(self.h2ft[:], self.psOb[:, 0:FC * 128]),
          waits=[self.pe], sig=self.dve2, ss=True, nop=True)
        if getattr(self, "a2mode", "full") == "h2b":
            return
        R("scalar", lambda sc, s=s, tt=tt: sc.dma_start(
            self.h2d.rearrange("n (c p) s -> n p c s", p=128)[s, :, :, tt * 128:(tt + 1) * 128],
            self.h2ft[:].rearrange("p (c j) -> p c j", c=FC)),
          waits=[self.dve2], sig=self.scl, n=16)
        if getattr(self, "a2mode", "full") == "h2nogate":
            R("vector", lambda v, s=s, tt=tt: v.memset(
                self.gate_all[:, (s * self.TT + tt) * 8:(s * self.TT + tt + 1) * 8], 0.125),
              sig=self.dve)
            R("vector", lambda v, b=b: v.memset(self.grow[b][:], 0.125), sig=self.dve)
            R("scalar", lambda sc, s=s, tt=tt, b=b: sc.dma_start(
                self.gtd[:, s * self.S + tt * 128: s * self.S + (tt + 1) * 128], self.grow[b][:]),
              waits=[self.dve], sig=self.scl, n=16)
            return
        def gmm(t):
            for fc in range(FC):
                ins = t.matmul(self.psR[:, 0:8], self.h2ft[:, fc * 128:(fc + 1) * 128],
                               self.gwc[:, fc * 8:(fc + 1) * 8],
                               start=(fc == 0), stop=(fc == FC - 1), skip_group_check=True)
            return ins
        R("tensor", gmm, waits=[self.dve2, self.dve], sig=self.pe2)
        R("vector", lambda v: v.tensor_tensor(self.gsc[:], self.psR[:, 0:8], self.gb_bc[:], ALU.add),
          waits=[self.pe2], sig=self.dve, ss=True)
        R("vector", lambda v: v.reduce_max(self.gsc2[:, 0:1], self.gsc[:], axis=AX, negate=True),
          sig=self.dve, ss=True)
        R("scalar", lambda sc: sc.activation(self.gsc[:], self.gsc[:], AF.Exp,
                                             bias=self.gsc2[:, 0:1]),
          waits=[self.dve], sig=self.act, ss=True)
        R("vector", lambda v: v.reduce_sum(self.gsc2[:, 1:2], self.gsc[:], axis=AX),
          waits=[self.act], sig=self.dve, ss=True)
        R("vector", lambda v: v.reciprocal(self.gsc2[:, 2:3], self.gsc2[:, 1:2]),
          sig=self.dve, ss=True)
        R("vector", lambda v, s=s, tt=tt: v.tensor_scalar(
            self.gate_all[:, (s * self.TT + tt) * 8:(s * self.TT + tt + 1) * 8],
            self.gsc[:], self.gsc2[:, 2:3], None, ALU.mult),
          ss=True)
        R("vector", lambda v: v.tensor_scalar(self.gb16[:], self.gsc[:], self.gsc2[:, 2:3],
                                              None, ALU.mult), sig=self.dve, ss=True, nop=True)
        R("tensor", lambda t: t.transpose(self.psR[:].bitcast(BF16)[0:8, 0:128],
                                          self.gb16[:], self.idn[:]),
          waits=[self.dve], sig=self.pe2)
        R("vector", lambda v, b=b: v.tensor_copy(self.grow[b][:], self.psR[:].bitcast(BF16)[0:8, 0:128]),
          waits=[self.pe2], sig=self.dve)
        R("scalar", lambda sc, s=s, tt=tt, b=b: sc.dma_start(
            self.gtd[:, s * S + tt * 128: s * S + (tt + 1) * 128], self.grow[b][:]),
          waits=[self.dve], sig=self.scl, n=16)

    def moe_phase1(self, e):
        S, FF, FFC = self.S, self.FF, self.FFC
        nTok = self.TOK // 512
        ph = Ph(self.nc, f"moe1_{e}")
        R = ph.rec
        if e == 0:
            for fc in range(FC):
                self.stage_w(R, self.WA[:, fc * FF:(fc + 1) * FF],
                             self.w1[e, fc * 128:(fc + 1) * 128, :], FF)
            R("sync", lambda sy: sy.dma_start(
                self.rowtmp[0:FFC // 2, 0:128],
                self.b1[e].rearrange("(a b) -> a b", b=128)[0:FFC // 2]),
              sig=self.syn, n=16)
            R("sync", lambda sy: sy.dma_start(
                self.rowtmp[0:FFC // 2, 128:256],
                self.b1[e].rearrange("(a b) -> a b", b=128)[FFC // 2:FFC]),
              sig=self.syn, n=16)
            self.w1tok = [self.stage_fence(R)]
            R("tensor", lambda t: t.transpose(self.psR[:, 0:128],
                                              self.rowtmp[:, 0:128], self.idnf[:]),
              waits=[self.syn], sig=self.pe2)
            R("tensor", lambda t: t.transpose(self.psR[:, 128:256],
                                              self.rowtmp[:, 128:256], self.idnf[:]),
              sig=self.pe2)
            R("vector", lambda v: v.tensor_copy(self.b1c[0][:, 0:FFC // 2], self.psR[:, 0:FFC // 2]),
              waits=[self.pe2], sig=self.dve)
            R("vector", lambda v: v.tensor_copy(self.b1c[0][:, FFC // 2:FFC],
                                                self.psR[:, 128:128 + FFC // 2]),
              sig=self.dve, ss=True, nop=True)
            self.b1tok = self.dve.n
        for ffc in range(FFC):
            self.stage_w(R, self.WB[:, ffc * D:(ffc + 1) * D],
                         self.w2[e, ffc * 128:(ffc + 1) * 128, :], D)
        self.w2tok = [self.stage_fence(R)]

        if getattr(self, "dbg", False) and e == self.NE - 1:
            R("sync", lambda sy: sy.dma_start(self.dbgd[3, :, :],
                                              self.WA[:, 0:self.S]),
              waits=[(self.dve, self.w1tok[-1])], sig=self.syn, n=16)
            R("sync", lambda sy: sy.dma_start(self.dbgd[4, :, :],
                                              self.WA[:, 3072:3072 + self.S]),
              waits=[(self.dve, self.w1tok[-1])], sig=self.syn, n=16)
            R("sync", lambda sy: sy.dma_start(
                self.dbgd[5, :, 0:32].bitcast(F32)[:, 0:32] if False else self.dbgd[5, :, 0:64],
                self.b1c[0][:, 0:32].bitcast(BF16)),
              waits=[self.dve], sig=self.syn, n=16)
        h2tok = []
        hid_scl = []
        for tokT in range(nTok):
            sq = (tokT * 512) // S
            off = (tokT * 512) % S
            b = tokT % 2
            w = []
            if tokT >= 2:
                w.append((self.pe, self.moe1_pe[tokT - 2] if hasattr(self, 'moe1_pe') and len(self.moe1_pe) > tokT - 2 else 0))
            w = [(self.pe, h2tok[tokT - 2][1])] if tokT >= 2 else [(self.scl, self.scl.n)]
            R("sync", lambda sy, sq=sq, off=off, b=b: sy.dma_start(
                self.h2cur[b][:].rearrange("p (c j) -> p c j", c=FC),
                self.h2d.rearrange("n (c p) s -> n p c s", p=128)[sq, :, :, off:off + 512]),
              waits=w, sig=self.syn, n=16)
            htok = self.syn.n
            for ffc in range(FFC):
                r = (tokT * FFC + ffc) % 3
                wmm = [(self.syn, htok), (self.dve, self.w1tok[-1])]
                pstile = (ffc % 2, (ffc // 2) % 2)
                if not hasattr(self, "ps1_last"):
                    self.ps1_last = {}
                if pstile in self.ps1_last:
                    wmm.append((self.act2, self.ps1_last[pstile]))
                prev = getattr(self, "hsb_last", {})
                if r in prev:
                    wmm.append((self.scl, prev[r]))
                def mm1(t, ffc=ffc, b=b):
                    ps = self.psP[ffc % 2] if (ffc // 2) % 2 == 0 else self.psS[ffc % 2]
                    for fc in range(FC):
                        ins = t.matmul(ps[:],
                                       self.WA[:, fc * FF + ffc * 128: fc * FF + (ffc + 1) * 128],
                                       self.h2cur[b][:, fc * 512:(fc + 1) * 512],
                                       start=(fc == 0), stop=(fc == FC - 1))
                    return ins
                R("tensor", mm1, waits=wmm, sig=self.pe)
                petok = self.pe.n
                def gelu(sc, ffc=ffc, r=r):
                    ps = self.psP[ffc % 2] if (ffc // 2) % 2 == 0 else self.psS[ffc % 2]
                    return sc.activation(self.hsb[r][:], ps[:], AF.Gelu_apprx_tanh,
                                         bias=self.b1c[e % 2][:, ffc:ffc + 1])
                R("scalar", gelu, waits=[(self.pe, petok), (self.dve, self.b1tok)], sig=self.act2, ss=True)
                self.ps1_last[(ffc % 2, (ffc // 2) % 2)] = self.act2.n
                R("scalar", lambda sc, ffc=ffc, tokT=tokT, r=r: sc.dma_start(
                    self.hidd[ffc * 128:(ffc + 1) * 128, tokT * 512:(tokT + 1) * 512],
                    self.hsb[r][:]),
                  waits=[(self.act2, self.act2.n, True)], sig=self.scl, n=16)
                if not hasattr(self, "hsb_last"):
                    self.hsb_last = {}
                self.hsb_last[r] = self.scl.n
            h2tok.append((htok, self.pe.n))
            hid_scl.append(self.scl.n)
        self.hid_ready = self.scl.n
        ph.emit()

    def moe_phase2(self, e):
        S, FF, FFC, D_ = self.S, self.FF, self.FFC, D
        nTT = self.TOK // 128
        ph = Ph(self.nc, f"moe2_{e}")
        R = ph.rec
        if e + 1 < self.NE:
            for fc in range(FC):
                self.stage_w(R, self.WA[:, fc * FF:(fc + 1) * FF],
                             self.w1[e + 1, fc * 128:(fc + 1) * 128, :], FF)
            R("sync", lambda sy, e=e: sy.dma_start(
                self.rowtmp[0:FFC // 2, 0:128],
                self.b1[e + 1].rearrange("(a b) -> a b", b=128)[0:FFC // 2]),
              sig=self.syn, n=16)
            R("sync", lambda sy, e=e: sy.dma_start(
                self.rowtmp[0:FFC // 2, 128:256],
                self.b1[e + 1].rearrange("(a b) -> a b", b=128)[FFC // 2:FFC]),
              sig=self.syn, n=16)
            self.w1tok = [self.stage_fence(R)]
            R("tensor", lambda t: t.transpose(self.psR[:, 0:128],
                                              self.rowtmp[:, 0:128], self.idnf[:]),
              waits=[self.syn, self.act2, self.dve], sig=self.pe2)
            R("tensor", lambda t: t.transpose(self.psR[:, 128:256],
                                              self.rowtmp[:, 128:256], self.idnf[:]),
              sig=self.pe2)
            R("vector", lambda v, e=e: v.tensor_copy(self.b1c[(e + 1) % 2][:, 0:FFC // 2],
                                                     self.psR[:, 0:FFC // 2]),
              waits=[self.pe2], sig=self.dve)
            R("vector", lambda v, e=e: v.tensor_copy(self.b1c[(e + 1) % 2][:, FFC // 2:FFC],
                                                     self.psR[:, 128:128 + FFC // 2]),
              sig=self.dve, ss=True, nop=True)
            self.b1tok = self.dve.n

        mtok = []
        for tt in range(nTT):
            b = tt % 2
            w = [(self.scl, self.hid_ready)]
            if tt >= 2:
                w.append((self.pe, mtok[tt - 2][0]))
            R("sync", lambda sy, tt=tt, b=b: sy.dma_start(
                self.hid2[:, b * FFC * 128:(b + 1) * FFC * 128].rearrange(
                    "p (c j) -> p c j", c=FFC),
                self.hidd.rearrange("(c p) s -> p c s", p=128)[:, :, tt * 128:(tt + 1) * 128]),
              waits=w, sig=self.syn, n=16)
            htok = self.syn.n
            wmm = [(self.syn, htok), (self.dve, self.w2tok[-1]),
                   (self.dve, self.b1tok)]
            if tt >= 2:
                wmm.append((self.dve, mtok[tt - 2][1]))
            def mm2(t, tt=tt, b=b):
                pair = [self.psO[:, 0:512], self.psO[:, 512:1024]] if tt % 2 == 0 \
                    else [self.psV[:], self.psR[:]]
                for half in range(2):
                    for ffc in range(FFC):
                        ins = t.matmul(pair[half],
                                       self.hid2[:, b * FFC * 128 + ffc * 128:
                                                 b * FFC * 128 + (ffc + 1) * 128],
                                       self.WB[:, ffc * D_ + half * 512: ffc * D_ + (half + 1) * 512],
                                       start=(ffc == 0), stop=(ffc == FFC - 1))
                return ins
            R("tensor", mm2, waits=wmm, sig=self.pe)
            petok = self.pe.n
            gcol_idx = tt * 8 + e
            def evac(v, tt=tt, e=e):
                pair = [self.psO[:, 0:512], self.psO[:, 512:1024]] if tt % 2 == 0 \
                    else [self.psV[:], self.psR[:]]
                gslice = self.gate_all[:, tt * 8 + e: tt * 8 + e + 1]
                v.tensor_scalar(self.t32[tt % 2][:, 0:512], pair[0], gslice, None, ALU.mult)
                return v.tensor_scalar(self.t32[tt % 2][:, 512:1024], pair[1], gslice, None, ALU.mult)
            wev = [(self.pe, petok)]
            if tt >= 2:
                wev.append((self.gsm, mtok[tt - 2][2]))
            R("vector", evac, waits=wev, sig=self.dve, ss=True, nop=True)
            dtok = self.dve.n
            R("gpsimd", lambda g, tt=tt, e=e: g.dma_start(
                self.moed[tt * 128:(tt + 1) * 128, :], self.t32[tt % 2][:],
                accum_op=(ALU.bypass if e == 0 else ALU.add)),
              waits=[(self.dve, dtok)] + ([(self.gsm, self.moeacc[tt])] if e > 0 else []),
              sig=self.gsm, n=16)
            if not hasattr(self, "moeacc"):
                self.moeacc = {}
            self.moeacc[tt] = self.gsm.n
            mtok.append((petok, dtok, self.gsm.n))
        ph.emit()

    def stage_e(self):
        S, TT = self.S, self.TT
        nTT = self.TOK // 128
        ph = Ph(self.nc, "E")
        R = ph.rec
        self.etok = {}
        self.b2tok = {}
        for tt in range(nTT):
            s, tl = tt // TT, tt % TT
            b = tt % 2
            w = [(self.gsm, self.moeacc[tt]), (self.scl, self.scl.n)]
            if tt >= 2:
                w.append((self.dve, self.etok[tt - 2]))
            R("sync", lambda sy, tt=tt, b=b: sy.dma_start(
                self.t32[b][:], self.moed[tt * 128:(tt + 1) * 128, :]),
              waits=w, sig=self.syn, n=16)
            gw_w = [(self.pe, self.b2tok[tt - 2])] if tt >= 2 else []
            R("sync", lambda sy, tt=tt, b=b: sy.dma_start(
                self.grow[b][:], self.gtd[:, tt * 128:(tt + 1) * 128]),
              waits=gw_w, sig=self.syn, n=16)
            ld = self.syn.n
            NEr = self.NE
            wb2 = [(self.syn, ld), self.dve]
            def b2mm(t, b=b, NEr=NEr):
                t.matmul(self.psO[:, 0:512], self.grow[b][0:NEr, :], self.b2bf[0:NEr, 0:512],
                         start=True, stop=True)
                return t.matmul(self.psO[:, 512:1024], self.grow[b][0:NEr, :],
                                self.b2bf[0:NEr, 512:1024], start=True, stop=True)
            R("tensor", b2mm, waits=wb2, sig=self.pe)
            self.b2tok[tt] = self.pe.n
            R("vector", lambda v, b=b: v.tensor_tensor(self.n32[:], self.t32[b][:], self.psO[:], ALU.add),
              waits=[self.pe], sig=self.dve)
            comb_tok = self.dve.n
            R("sync", lambda sy, s=s, tl=tl, b=b: sy.dma_start(
                self.t32[b][:], self.y[s, tl * 128:(tl + 1) * 128, :]),
              waits=[(self.dve, comb_tok), (self.scl, self.y_attn_tok)],
              sig=self.syn, n=16)
            ld2 = self.syn.n
            R("scalar", lambda sc: sc.activation(self.sqscr[:], self.n32[:], AF.Square),
              waits=[self.dve], sig=self.act)
            R("vector", lambda v: v.reduce_sum(self.stat[:, 0:1], self.n32[:], axis=AX),
              ss=True)
            R("vector", lambda v: v.reduce_sum(self.stat[:, 1:2], self.sqscr[:], axis=AX),
              waits=[self.act], sig=self.dve, ss=True)
            c = 1.0 / D
            R("vector", lambda v, c=c: v.tensor_scalar(self.stat[:, 2:3], self.stat[:, 0:1], c, None, ALU.mult),
              ss=True)
            R("vector", lambda v, c=c: v.tensor_scalar(self.stat[:, 3:4], self.stat[:, 1:2], c, None, ALU.mult),
              ss=True)
            R("vector", lambda v: v.tensor_tensor(self.stat[:, 4:5], self.stat[:, 2:3], self.stat[:, 2:3], ALU.mult),
              ss=True)
            R("vector", lambda v: v.tensor_tensor(self.stat[:, 5:6], self.stat[:, 3:4], self.stat[:, 4:5], ALU.subtract),
              sig=self.dve, ss=True)
            R("scalar", lambda sc: sc.activation(self.stat[:, 6:7], self.stat[:, 5:6], AF.Ln, bias=LNEPS),
              waits=[self.dve], ss=True)
            R("scalar", lambda sc: sc.activation(self.stat[:, 6:7], self.stat[:, 6:7], AF.Exp, scale=-0.5),
              sig=self.act, ss=True, nop=True)
            R("vector", lambda v: v.tensor_tensor(self.stat[:, 7:8], self.stat[:, 2:3], self.stat[:, 6:7], ALU.mult),
              waits=[self.act], ss=True)
            R("vector", lambda v: v.tensor_scalar(self.stat[:, 7:8], self.stat[:, 7:8], -1.0, None, ALU.mult),
              sig=self.dve, ss=True)
            R("vector", lambda v: v.tensor_scalar(self.n32[:], self.n32[:],
                                                  self.stat[:, 6:7], self.stat[:, 7:8],
                                                  ALU.mult, ALU.add),
              sig=self.dve)
            R("vector", lambda v: v.tensor_tensor(self.sqscr[:], self.n32[:], self.g_bc[:], ALU.mult))
            R("vector", lambda v: v.tensor_tensor(self.sqscr[:], self.sqscr[:], self.b_bc[:], ALU.add))
            R("vector", lambda v, b=b: v.tensor_tensor(self.t32[b][:], self.t32[b][:], self.sqscr[:], ALU.add),
              waits=[(self.syn, ld2)], sig=self.dve, ss=True, nop=True)
            self.etok[tt] = self.dve.n
            R("scalar", lambda sc, s=s, tl=tl, b=b: sc.dma_start(
                self.y[s, tl * 128:(tl + 1) * 128, :], self.t32[b][:]),
              waits=[self.dve], sig=self.scl, n=16)
        ph.emit()

    def finish(self):
        ph = Ph(self.nc, "fin")
        def fin(sy):
            for ev in self.all_dma:
                if ev.n > 0:
                    sy.wait_ge(ev.h, ev.n)
            return sy.nop()
        ph.rec("sync", fin)
        ph.emit()


def make_in_map(inputs, core):
    x = inputs["L0_attn_residual"]
    return {
        "x": np.ascontiguousarray(x[2 * core: 2 * core + 2]),
        "wk1": inputs["L0_attn_W_key"],
        "wq1": inputs["L0_attn_W_query"],
        "wv1": inputs["L0_attn_W_values"],
        "wo1": inputs["L0_attn_W_output"],
        "bk1": inputs["L0_attn_Bias_key"],
        "bq1": inputs["L0_attn_Bias_query"],
        "bv1": inputs["L0_attn_Bias_values"],
        "bo1": inputs["L0_attn_Bias_output"],
        "lng": inputs["L0_ffn_ln_g"],
        "lnb": inputs["L0_ffn_ln_b"],
        "wq2": inputs["L0_ffn_Wq"],
        "wk2": inputs["L0_ffn_Wk"],
        "wv2": inputs["L0_ffn_Wv"],
        "wo2": inputs["L0_ffn_Wo"],
        "qs2": inputs["L0_ffn_q_scale"],
        "ks2": inputs["L0_ffn_k_scale"],
        "gw": inputs["L0_ffn_gate_w"],
        "gb": inputs["L0_ffn_gate_b"],
        "w1": inputs["L0_ffn_w1"],
        "b1": inputs["L0_ffn_b1"],
        "w2": inputs["L0_ffn_w2"],
        "b2": inputs["L0_ffn_b2"],
    }


def kernel(**inputs):
    b = Build(S=2048, NSEQ=2, NE=8, FF=4096)
    nc = b.build()
    in_maps = [make_in_map(inputs, c) for c in range(NCORES)]
    res = run_bass_kernel_spmd(nc, in_maps, core_ids=list(range(NCORES)))
    out = np.concatenate([r["y"] for r in res.results], axis=0)
    return out.astype(np.float32)


NCORES = 8
